# version 30
# baseline (speedup 1.0000x reference)
import math
import numpy as np
import concourse.bass as bass
import concourse.mybir as mybir
from concourse.bass_utils import run_bass_kernel_spmd

F32 = mybir.dt.float32
BF16 = mybir.dt.bfloat16
AF = mybir.ActivationFunctionType
ALU = mybir.AluOpType
AX = mybir.AxisListType

S = 4096
D = 1024
NT = S // 128
NC = D // 128
H = 16
HD = 64
NPAIR = 8

PHASES = ("l0a", "l0f", "l1a", "l1f")
DUMP_INTERNALS = False



class _Op:
    __slots__ = ("eng", "fn", "dma", "deps", "idx", "need_inc", "sem", "val",
                 "inc", "waits", "tag", "slot")

    def __init__(self, eng, fn, dma, idx, tag):
        self.eng = eng
        self.fn = fn
        self.dma = dma
        self.idx = idx
        self.deps = set()
        self.need_inc = False
        self.sem = None
        self.val = 0
        self.inc = 16 if dma else 1
        self.waits = []
        self.tag = tag


def _key(x):
    if isinstance(x, str):
        return x
    if isinstance(x, bass.AP):
        return x.tensor.name
    return x.name


class Sched:
    ENGINES = ("tensor", "scalar", "vector", "gpsimd", "sync")

    def __init__(self, nc):
        self.nc = nc
        self.ops = []
        self.last_writer = {}
        self.readers = {}
        self.epoch = 0
        self.epoch_of = []

    def op(self, eng, fn, reads=(), writes=(), dma=False, tag=""):
        o = _Op(eng, fn, dma, len(self.ops), tag)
        raw_waw = set()
        war = set()
        for k in reads:
            k = _key(k)
            w = self.last_writer.get(k)
            if w is not None:
                raw_waw.add(w)
        for k in writes:
            k = _key(k)
            w = self.last_writer.get(k)
            if w is not None:
                raw_waw.add(w)
            for r in self.readers.get(k, ()):
                war.add(r)
        def keep(d):
            if d.dma or d.eng != eng:
                return True
            return eng not in ("tensor", "sync")
        o.deps = {d for d in (raw_waw | war) if keep(d)}
        for k in reads:
            self.readers.setdefault(_key(k), []).append(o)
        for k in writes:
            k = _key(k)
            self.last_writer[k] = o
            self.readers[k] = []
        self.ops.append(o)
        self.epoch_of.append(self.epoch)
        return o

    def barrier(self):
        tails = {}
        for o in self.ops:
            if self.epoch_of[o.idx] != self.epoch:
                continue
            tails[(o.eng, o.dma)] = o
        deps = set(tails.values())
        for eng in self.ENGINES:
            mydeps = {d for d in deps if d.dma or d.eng != eng}
            o = _Op(eng, lambda e: e.nop(), False, len(self.ops), "barrier")
            o.deps = mydeps
            self.ops.append(o)
            self.epoch_of.append(self.epoch)
        self.last_writer = {}
        self.readers = {}
        self.epoch += 1

    DMA_SLOTS = 24

    def finalize(self):
        nc = self.nc
        for o in self.ops:
            if o.dma:
                o.need_inc = True
            for d in o.deps:
                d.need_inc = True
        waiters = {}
        for o in self.ops:
            for d in o.deps:
                waiters.setdefault(d.idx, []).append(o)
        slot_last = {}
        dma_count = {}
        for o in self.ops:
            if not o.dma:
                continue
            nslot = self.DMA_SLOTS
            base = dma_count.get(o.eng, 0)
            chosen = None
            for k in range(nslot):
                s = (base + k) % nslot
                prev = slot_last.get((o.eng, s))
                if prev is None or all(w.idx < o.idx
                                       for w in waiters.get(prev.idx, ())):
                    chosen = s
                    if prev is not None:
                        for w in waiters.get(prev.idx, ()):
                            if w is not o:
                                o.deps.add(w)
                    break
            assert chosen is not None, f"no free dma slot for op {o.idx}"
            o.slot = (o.eng, chosen)
            slot_last[o.slot] = o
            dma_count[o.eng] = base + 1
            for d in o.deps:
                d.need_inc = True
        for o in self.ops:
            if o.tag != "barrier":
                continue
            for key, last in slot_last.items():
                if last.idx < o.idx and (last.dma or last.eng != o.eng):
                    o.deps.add(last)
                    last.need_inc = True
        sems = {}
        counters = {}

        def get_sem(key, name):
            if key not in sems:
                sems[key] = nc.alloc_semaphore(name)
                counters[key] = 0
            return key

        for o in self.ops:
            if o.need_inc:
                if o.dma:
                    k = get_sem(o.slot, f"d_{o.slot[0]}_{o.slot[1]}")
                else:
                    k = get_sem(o.eng, f"c_{o.eng}")
                counters[k] += o.inc
                o.sem = sems[k]
                o.val = counters[k]
        for k, v in counters.items():
            assert v < 2 ** 31, (k, v)
        marks = {e: {} for e in self.ENGINES}
        for o in self.ops:
            m = marks[o.eng]
            for d in sorted(o.deps, key=lambda d: d.idx):
                assert d.idx < o.idx
                assert d.sem is not None
                if m.get(d.sem.num, -1) >= d.val:
                    continue
                o.waits.append((d.sem, d.val))
                m[d.sem.num] = d.val
        by_eng = {e: [] for e in self.ENGINES}
        for o in self.ops:
            by_eng[o.eng].append(o)
        with nc.Block() as block:
            def make(eng_name):
                def run(engine):
                    for o in by_eng[eng_name]:
                        for sem, val in o.waits:
                            engine.wait_ge(sem, val)
                        ins = o.fn(engine)
                        if o.need_inc:
                            ins.then_inc(o.sem, o.inc)
                return run
            block.tensor(make("tensor"))
            block.scalar(make("scalar"))
            block.vector(make("vector"))
            block.gpsimd(make("gpsimd"))
            block.sync(make("sync"))



def _triu_ones(n):
    return np.triu(np.ones((n, n), np.float32))


def build_program(phases=None):
    phases = PHASES if phases is None else phases
    nc = bass.Bass("TRN2", target_bir_lowering=False)
    sch = Sched(nc)

    T = {}

    def din(name, shape, dtype=F32):
        T[name] = nc.dram_tensor(name, list(shape), dtype, kind="ExternalInput")
        return T[name]

    def dint(name, shape, dtype=F32):
        kind = "ExternalOutput" if DUMP_INTERNALS else "Internal"
        T[name] = nc.dram_tensor(name, list(shape), dtype, kind=kind)
        return T[name]

    din("x_in", [NT, 128, D])
    xout = nc.dram_tensor("xout", [NT, 128, D], F32, kind="ExternalOutput")
    T["xout"] = xout

    din("wqkvT", [D, 3 * D])
    din("wprojT", [D, D])
    din("U1", [D, 256])
    din("V1", [256, 4 * D])
    din("u2r", [32, 128, 256])
    din("V2", [256, D])
    din("b1", [1, 4 * D])
    din("b2", [1, D])
    din("ppT", [D, S])
    din("Wpv", [D, D])
    din("Wk1a", [D, D])
    din("Wkv", [D, D])
    din("Wg1", [2 * D, D])
    din("Wb1", [D, 512])
    din("wb2r", [4, 128, 2])
    din("wg2c", [128, NC])
    din("lngc", [128, NC])
    din("lnbc", [128, NC])
    din("Wo1a", [D, D])
    din("Wq1f", [D, D])
    din("Wk1f", [D, D])
    din("Wv1f", [D, D])
    din("Wo1f", [D, D])
    din("Wff", [D, D])
    din("bpv", [1, D])
    din("bk1a", [1, D])
    din("bkv", [1, D])
    din("bg1", [1, D])
    din("bg2", [1, 1])
    din("bb1", [1, 512])
    din("bb2", [1, 2])
    din("bo1a", [1, D])

    dint("pc_sp", [NC, 128, S], BF16)
    dint("ps_sp", [NC, 128, S], BF16)
    dint("q_sp", [NPAIR, 128, S], BF16)
    dint("k_sp", [NPAIR, 128, S], BF16)
    dint("v_sp", [NT, 128, H * 65], BF16)
    dint("o_sp", [NPAIR, 128, S], BF16)
    dint("x1", [NT, 128, D])
    dint("x2", [NT, 128, D])
    dint("x3", [NT, 128, D])

    tri = nc.inline_tensor(_triu_ones(128).astype(np.float32), name="c_triu")
    T["c_triu"] = tri

    alloc = nc.alloc_sbuf_tensor

    epsT = alloc("epsT", [128, 1], F32)
    ident = alloc("ident", [128, 128], BF16)
    mask_triu = alloc("mask_triu", [128, 128], BF16)
    ones64 = alloc("ones64", [1, 64], F32)

    ones1 = alloc("ones1", [1, 128], BF16)
    onesrow = alloc("onesrow", [1, 512], BF16)
    sch.op("vector", lambda v: v.memset(epsT[:, :], 1e-5), writes=["epsT"])
    sch.op("gpsimd", lambda g: g.dma_start(out=mask_triu[:, :], in_=tri[:, :]),
           writes=["mask_triu"], dma=True)
    sch.op("vector", lambda v: v.memset(ones64[:, :], 1.0), writes=["ones64"])
    sch.op("vector", lambda v: v.memset(ones1[:, :], 1.0), writes=["ones1"])
    sch.op("vector", lambda v: v.memset(onesrow[:, :], 1.0), writes=["onesrow"])
    sch.op("gpsimd", lambda g: g.memset(ident[:, :], 1.0), writes=["ident"])
    sch.op("gpsimd", lambda g: g.affine_select(
        out=ident[:, :], in_=ident[:, :], compare_op=ALU.is_ge,
        fill=0.0, base=0, pattern=[[-1, 128]], channel_multiplier=1),
        reads=["ident"], writes=["ident"])
    sch.op("gpsimd", lambda g: g.affine_select(
        out=ident[:, :], in_=ident[:, :], compare_op=ALU.is_ge, fill=0.0,
        base=0, pattern=[[1, 128]], channel_multiplier=-1),
        reads=["ident"], writes=["ident"])

    state = dict(nc=nc, sch=sch, T=T, epsT=epsT, ident=ident,
                 mask_triu=mask_triu, ones64=ones64, ones1=ones1,
                 onesrow=onesrow)

    xcur = "x_in"
    if "l0a" in phases:
        emit_l0a(state, xcur, "x1")
        xcur = "x1"
    if "l0f" in phases:
        emit_l0f(state, xcur, "x2")
        xcur = "x2"
    if "l1a" in phases:
        emit_l1a(state, xcur, "x3")
        xcur = "x3"
    if "l1f" in phases:
        emit_l1f(state, xcur, "xout")
        xcur = "xout"

    if xcur != "xout":
        emit_copyout(state, xcur)
    sch.barrier()
    sch.finalize()
    return nc


def emit_copyout(st, xcur):
    nc, sch, T = st["nc"], st["sch"], st["T"]
    if xcur == "x_in":
        src = T["x_in"]
    else:
        src = T[xcur]
    xout = T["xout"]
    buf = nc.alloc_sbuf_tensor("cp_buf", [128, 4 * D], F32)
    for t0 in range(0, NT, 4):
        for j in range(4):
            t = t0 + j
            sch.op("sync", (lambda t=t, j=j: lambda s: s.dma_start(
                out=buf[:, j * D:(j + 1) * D], in_=src[t]))(),
                reads=[xcur], writes=[f"cp{j}"], dma=True, tag="cpin")
        for j in range(4):
            t = t0 + j
            sch.op("sync", (lambda t=t, j=j: lambda s: s.dma_start(
                out=xout[t], in_=buf[:, j * D:(j + 1) * D]))(),
                reads=[f"cp{j}"], writes=["xout"], dma=True, tag="cpout")



def emit_ln_chunk(st, xsrc_name, xsrc, c8, xt, stats, lnt, psT, ln_ch,
                  ln_col0=0, ln_keys=None, xt_extra_reads=(), nslot=2,
                  units=None):
    nc, sch = st["nc"], st["sch"]
    epsT, ident = st["epsT"], st["ident"]

    def tile(j):
        t = c8 * 4 + j
        xbuf = xt[:, j % nslot, :]
        kx = f"xt{j % nslot}"
        sch.op("sync", (lambda t=t, xbuf=xbuf: lambda s: s.dma_start(
            out=xbuf, in_=xsrc[t]))(),
            reads=[xsrc_name, *xt_extra_reads], writes=[kx], dma=True, tag="lnx")


        def bn_fn(v, xbuf=xbuf):
            v.bn_stats(stats[:, 0:6], xbuf[:, 0:512])
            return v.bn_stats(stats[:, 6:12], xbuf[:, 512:1024])
        sch.op("vector", bn_fn, reads=[kx], writes=["st_bn"], tag="lnbn")
        sch.op("vector", lambda v: v.bn_aggr(stats[:, 12:14], stats[:, 0:12]),
               reads=["st_bn"], writes=["st_mv"], tag="lnaggr")
        sch.op("scalar", lambda sc: sc.activation(
            stats[:, 14:15], stats[:, 13:14], AF.Sqrt, bias=epsT[:, :]),
            reads=["st_mv", "epsT"], writes=["st_sd"], tag="lnsqrt")
        sch.op("vector", lambda v: v.reciprocal(stats[:, 15:16],
                                                stats[:, 14:15]),
               reads=["st_sd"], writes=["st_rstd"], tag="lnrecip")
        sch.op("vector", lambda v: v.scalar_tensor_tensor(
            stats[:, 16:17], stats[:, 12:13], -1.0, stats[:, 15:16],
            ALU.mult, ALU.mult),
            reads=["st_mv", "st_rstd"], writes=["st_nm"], tag="lnnm")
        sch.op("scalar", (lambda xbuf=xbuf: lambda sc: sc.activation(
            lnt[:, :], xbuf, AF.Identity,
            bias=stats[:, 16:17], scale=stats[:, 15:16]))(),
            reads=["st_rstd", "st_nm", kx], writes=["lnt"], tag="lnapply")

        def tr_fn(te):
            for q in range(NC):
                ins = te.transpose(psT[:, q * 128:(q + 1) * 128],
                                   lnt[:, q * 128:(q + 1) * 128], ident[:, :])
            return ins
        sch.op("tensor", tr_fn, reads=["lnt", "ident"], writes=["psT"],
               tag="lntr")
        col = ln_col0 + 128 * j
        lk = ln_keys if ln_keys is not None else ["ln_ch"]
        sch.op("vector", (lambda col=col: lambda v: v.tensor_copy(
            ln_ch[:, :, col:col + 128],
            psT[:, :].rearrange("p (c f) -> p c f", c=NC)))(),
            reads=["psT"], writes=lk, tag="lnevac")

    for j in range(4):
        if units is None:
            tile(j)
        else:
            units.append(lambda j=j: tile(j))



def make_pt_units(st, alloc):
    nc, sch, T = st["nc"], st["sch"], st["T"]
    ppb = alloc("pt_pp", [128, 2, 1024], F32)
    ptA = alloc("pt_A", [128, 2, 1024], F32)
    pcb = alloc("pt_pc", [128, 2, 1024], BF16)
    psb = alloc("pt_ps", [128, 2, 1024], BF16)
    pc_sp, ps_sp = T["pc_sp"], T["ps_sp"]

    def pt_unit(u):
        Dc, Q = u // 4, u % 4
        W = 1024 * Q
        p = u % 2
        sch.op("gpsimd", (lambda Dc=Dc, W=W, p=p: lambda g: g.dma_start(
            out=ppb[:, p, :],
            in_=T["ppT"][Dc * 128:(Dc + 1) * 128, W:W + 1024]))(),
            reads=[T["ppT"]], writes=[f"ptpp{p}"], dma=True, tag="pt_l")
        sch.op("gpsimd", (lambda p=p: lambda g: g.tensor_scalar(
            ptA[:, p, :], ppb[:, p, :], 1.0 / TWO_PI, MAGIC,
            ALU.mult, ALU.add))(),
            reads=[f"ptpp{p}"], writes=[f"ptA{p}"], tag="pt_rr1")
        sch.op("gpsimd", (lambda p=p: lambda g: g.tensor_scalar(
            ptA[:, p, :], ptA[:, p, :], -MAGIC, -TWO_PI,
            ALU.add, ALU.mult))(),
            reads=[f"ptA{p}"], writes=[f"ptA{p}"], tag="pt_rr2")
        sch.op("gpsimd", (lambda p=p: lambda g: g.tensor_tensor(
            ptA[:, p, :], ptA[:, p, :], ppb[:, p, :], ALU.add))(),
            reads=[f"ptA{p}", f"ptpp{p}"], writes=[f"ptA{p}"], tag="pt_rr3")
        sch.op("scalar", (lambda p=p: lambda sc: sc.activation(
            psb[:, p, :], ptA[:, p, :], AF.Sin))(),
            reads=[f"ptA{p}"], writes=[f"ptps{p}"], tag="pt_sin")
        sch.op("scalar", (lambda p=p: lambda sc: sc.activation(
            ppb[:, p, :], ptA[:, p, :], AF.Sin, scale=0.5))(),
            reads=[f"ptA{p}"], writes=[f"ptpp{p}"], tag="pt_sin2")
        sch.op("scalar", (lambda p=p: lambda sc: sc.activation(
            ppb[:, p, :], ppb[:, p, :], AF.Square))(),
            reads=[f"ptpp{p}"], writes=[f"ptpp{p}"], tag="pt_sq")
        sch.op("gpsimd", (lambda p=p: lambda g: g.tensor_scalar(
            pcb[:, p, :], ppb[:, p, :], -2.0, 1.0, ALU.mult, ALU.add))(),
            reads=[f"ptpp{p}"], writes=[f"ptpc{p}"], tag="pt_cos")
        sch.op("gpsimd", (lambda Dc=Dc, W=W, p=p: lambda g: g.dma_start(
            out=pc_sp[Dc][:, W:W + 1024], in_=pcb[:, p, :]))(),
            reads=[f"ptpc{p}"], writes=["pcsp"], dma=True, tag="pt_st")
        sch.op("gpsimd", (lambda Dc=Dc, W=W, p=p: lambda g: g.dma_start(
            out=ps_sp[Dc][:, W:W + 1024], in_=psb[:, p, :]))(),
            reads=[f"ptps{p}"], writes=["pssp"], dma=True, tag="pt_st")

    return [(lambda u=u: pt_unit(u)) for u in range(32)]


def emit_l0a(st, xin_name, xout_name):
    from contextlib import ExitStack
    nc, sch, T = st["nc"], st["sch"], st["T"]
    x_in, x1 = T[xin_name], T[xout_name]
    wqkvT, wprojT = T["wqkvT"], T["wprojT"]
    q_sp, k_sp, v_sp, o_sp = T["q_sp"], T["k_sp"], T["v_sp"], T["o_sp"]
    mask_triu, ones64 = st["mask_triu"], st["ones64"]

    stk = ExitStack()
    alloc = lambda *a, **k: stk.enter_context(nc.sbuf_tensor(*a, **k))
    palloc = lambda *a, **k: stk.enter_context(nc.psum_tensor(*a, **k))
    wq_sb = alloc("l0a_wq", [128, NC * 3 * D], BF16)
    xt = alloc("l0a_xt", [128, 2, D], F32)
    stats = alloc("l0a_stats", [128, 18], F32)
    lnt = alloc("l0a_lnt", [128, D], BF16)
    ln_ch = alloc("l0a_lnch", [128, NC, 512], BF16)
    qkv_e = alloc("l0a_qkve", [128, 2 * 512], BF16)
    vsb = alloc("l0a_vsb", [128, 2, H * 65], BF16)

    psT = palloc("l0a_psT", [128, 1024], BF16)
    psA = palloc("l0a_psA", [128, 512], F32)
    psB = palloc("l0a_psB", [128, 512], F32)

    def vones(v):
        for h in range(H):
            for b in range(2):
                ins = v.memset(vsb[:, b, h * 65 + 64:h * 65 + 65], 1.0)
        return ins
    sch.op("vector", vones, writes=["vsb0", "vsb1"], tag="vones")

    for i in range(NC):
        sch.op("gpsimd", (lambda i=i: lambda g: g.dma_start(
            out=wq_sb[:, i * 3 * D:(i + 1) * 3 * D], in_=wqkvT[i * 128:(i + 1) * 128, :]))(),
            reads=["wqkvT"], writes=["wq_sb"], dma=True, tag="l0a_wload")

    pt_units = make_pt_units(st, alloc)
    for c8 in range(8):
        emit_ln_chunk(st, xin_name, x_in, c8, xt, stats, lnt, psT, ln_ch)
        for u in pt_units[c8 * 4:(c8 + 1) * 4]:
            u()
        for oc in range(16):
            ps = psA if oc % 2 == 0 else psB
            pk = "psA" if oc % 2 == 0 else "psB"

            def qk_fn(te, oc=oc, ps=ps):
                for i in range(NC):
                    ins = te.matmul(ps[:, :],
                                    wq_sb[:, i * 3 * D + oc * 128: i * 3 * D + (oc + 1) * 128],
                                    ln_ch[:, i, :],
                                    start=(i == 0), stop=(i == NC - 1))
                return ins
            sch.op("tensor", qk_fn, reads=["wq_sb", "ln_ch"], writes=[pk],
                   tag="l0a_qk")
            eb = qkv_e[:, (oc % 2) * 512:(oc % 2 + 1) * 512]
            ek = f"qkve{oc % 2}"
            sch.op("vector", (lambda ps=ps, eb=eb: lambda v: v.tensor_copy(
                eb, ps[:, :]))(), reads=[pk], writes=[ek],
                tag="l0a_qkevac")
            dst = q_sp if oc < 8 else k_sp
            p = oc % 8
            sch.op("sync", (lambda dst=dst, p=p, c8=c8, eb=eb: lambda s:
                   s.dma_start(out=dst[p][:, c8 * 512:(c8 + 1) * 512], in_=eb))(),
                   reads=[ek], writes=["q_sp" if oc < 8 else "k_sp"],
                   dma=True, tag="l0a_qkst")
        for j in range(4):
            t = c8 * 4 + j
            for jc in range(2):
                ps = psA if jc == 0 else psB
                pk = "psA" if jc == 0 else "psB"

                def v_fn(te, j=j, jc=jc, ps=ps):
                    for i in range(NC):
                        ins = te.matmul(
                            ps[:, :],
                            ln_ch[:, i, j * 128:(j + 1) * 128],
                            wq_sb[:, i * 3 * D + 2 * D + jc * 512: i * 3 * D + 2 * D + (jc + 1) * 512],
                            start=(i == 0), stop=(i == NC - 1))
                    return ins
                sch.op("tensor", v_fn, reads=["wq_sb", "ln_ch"], writes=[pk],
                       tag="l0a_v")
                vk = f"vsb{j % 2}"
                sch.op("vector", (lambda ps=ps, j=j, jc=jc: lambda v:
                       v.tensor_copy(
                           vsb[:, j % 2, jc * 8 * 65:(jc * 8 + 8) * 65].rearrange(
                               "p (h e) -> p h e", h=8)[:, :, 0:64],
                           ps[:, :].rearrange("p (h e) -> p h e", h=8)))(),
                       reads=[pk], writes=[vk], tag="l0a_vevac")
            sch.op("sync", (lambda t=t, j=j: lambda s: s.dma_start(
                out=v_sp[t], in_=vsb[:, j % 2, :]))(),
                reads=[f"vsb{j % 2}"], writes=["v_sp"], dma=True, tag="l0a_vst")

    sch.barrier()
    stk.close()

    stk = ExitStack()
    alloc = lambda *a, **k: stk.enter_context(nc.sbuf_tensor(*a, **k))
    palloc = lambda *a, **k: stk.enter_context(nc.psum_tensor(*a, **k))
    qp = alloc("l0a_qp", [128, 2, S], BF16)
    kp = alloc("l0a_kp", [128, 2, S], BF16)
    vp = alloc("l0a_vp", [128, 2, NT * 130], BF16)
    e_sb = alloc("l0a_e", [128, 2, 2, 1024], BF16)
    opair = alloc("l0a_opair", [128, 1024], BF16)
    zrow = alloc("l0a_zrow", [1, 4, 512], F32)
    rzb = alloc("l0a_rzb", [64, 4, 512], F32)

    psS = [palloc(f"l0a_psS{i}", [128, 1024], F32) for i in range(2)]
    psO = [palloc(f"l0a_psO{i}", [128, 512], F32) for i in range(4)]

    def emit_pair_loads(p):
        b = p % 2
        bk = f"pb{b}"
        sch.op("sync", (lambda p=p, b=b: lambda s: s.dma_start(
            out=qp[:, b, :], in_=q_sp[p]))(), reads=["q_sp"],
            writes=[bk + "q"], dma=True, tag="l0a_qld")
        sch.op("sync", (lambda p=p, b=b: lambda s: s.dma_start(
            out=kp[:, b, :], in_=k_sp[p]))(), reads=["k_sp"],
            writes=[bk + "k"], dma=True, tag="l0a_kld")
        vsrc = bass.AP(v_sp, p * 130,
                       [[H * 65, 128], [128 * H * 65, NT], [1, 130]])
        sch.op("sync", (lambda b=b, vsrc=vsrc: lambda s: s.dma_start(
            out=vp[:, b, :].rearrange("p (t e) -> p t e", t=NT),
            in_=vsrc))(),
            reads=["v_sp"], writes=[bk + "v"], dma=True, tag="l0a_vld")

    emit_pair_loads(0)
    for p in range(NPAIR):
        b = p % 2
        bk = f"pb{b}"
        if p + 1 < NPAIR:
            emit_pair_loads(p + 1)

        for sp in range(4):
            W = 1024 * sp
            imax = 8 * sp + 7
            pend_av = None
            for i in range(imax + 1):
                valid_c = [c for c in range(2)
                           if 128 * i < W + 512 * c + 512]
                pa = i % 2
                ek = (f"e{pa}0", f"e{pa}1")
                c0 = valid_c[0]
                cw = 512 * len(valid_c)
                for hh in range(2):
                    def sc_fn(te, i=i, valid_c=valid_c, W=W, b=b, hh=hh):
                        for c in valid_c:
                            ins = te.matmul(
                                psS[hh][:, 512 * c:512 * c + 512],
                                kp[64 * hh:64 + 64 * hh, b, 128 * i:128 * i + 128],
                                qp[64 * hh:64 + 64 * hh, b, W + 512 * c: W + 512 * c + 512],
                                start=True, stop=True,
                                tile_position=(64 * hh, 0))
                        return ins
                    sch.op("tensor", sc_fn, reads=[bk + "q", bk + "k"],
                           writes=[f"psS{hh}"], tag="l0a_sc")
                    sch.op("scalar", (lambda i=i, c0=c0, cw=cw, pa=pa, hh=hh:
                           lambda se: se.activation(
                               e_sb[:, pa, hh, 512 * c0:512 * c0 + cw],
                               psS[hh][:, 512 * c0:512 * c0 + cw],
                               AF.Exp, scale=0.125))(),
                           reads=[f"psS{hh}"], writes=[ek[hh]], tag="l0a_exp")

                off = 128 * i - W
                if 0 <= off < 1024:
                    def mask_fn(v, off=off, pa=pa):
                        for hh in range(2):
                            if off > 0:
                                v.memset(e_sb[:, pa, hh, 0:off], 0.0)
                            ins = v.tensor_mul(e_sb[:, pa, hh, off:off + 128],
                                               e_sb[:, pa, hh, off:off + 128],
                                               mask_triu[:, :])
                        return ins
                    sch.op("vector", mask_fn, reads=[*ek, "mask_triu"],
                           writes=[*ek], tag="l0a_mask")

                if pend_av is not None:
                    pend_av()

                def mk_av(i=i, valid_c=valid_c, imax=imax, b=b, W=W, pa=pa,
                          ek=ek):
                    def av_fn(te):
                        for c in valid_c:
                            ic_max = min(imax, (W + 512 * c + 511) // 128)
                            for hh in range(2):
                                ins = te.matmul(
                                    psO[2 * c + hh][0:65, :],
                                    vp[:, b, 130 * i + 65 * hh:130 * i + 65 * hh + 65],
                                    e_sb[:, pa, hh, 512 * c:512 * c + 512],
                                    start=(i == 0), stop=(i == ic_max))
                        return ins
                    def emit():
                        sch.op("tensor", av_fn, reads=[*ek, bk + "v"],
                               writes=[f"psO{2 * c + hh}"
                                       for c in valid_c for hh in range(2)],
                               tag="l0a_av")
                    return emit
                pend_av = mk_av()
            pend_av()

            for c in range(2):
                for hh in range(2):
                    pidx = 2 * c + hh
                    zsl = zrow[:, pidx, :]
                    zk = f"zrow{pidx}"
                    sch.op("vector", (lambda pidx=pidx, zsl=zsl: lambda v:
                           v.tensor_copy(zsl, psO[pidx][64:65, :]))(),
                           reads=[f"psO{pidx}"], writes=[zk], tag="l0a_zc")
                    sch.op("scalar", (lambda zsl=zsl: lambda se:
                           se.activation(zsl, zsl, AF.Ln))(),
                           reads=[zk], writes=[zk], tag="l0a_zl")
                    sch.op("scalar", (lambda zsl=zsl: lambda se:
                           se.activation(zsl, zsl, AF.Exp, scale=-1.0))(),
                           reads=[zk], writes=[zk], tag="l0a_ze")
                    sch.op("tensor", (lambda pidx=pidx, zsl=zsl: lambda te:
                           te.matmul(psS[pidx % 2][0:64, 512 * (pidx // 2):512 * (pidx // 2) + 512],
                                     ones64[:, :], zsl,
                                     start=True, stop=True))(),
                           reads=[zk, "ones64"], writes=[f"psS{pidx % 2}"],
                           tag="l0a_zb")
                    sch.op("vector", (lambda pidx=pidx: lambda v:
                           v.tensor_copy(rzb[:, pidx, :],
                                         psS[pidx % 2][0:64, 512 * (pidx // 2):512 * (pidx // 2) + 512]))(),
                           reads=[f"psS{pidx % 2}"], writes=[f"rzb{pidx}"],
                           tag="l0a_zbe")
                    sch.op("vector", (lambda pidx=pidx, c=c, hh=hh: lambda v:
                           v.tensor_mul(
                               opair[64 * hh:64 * hh + 64, 512 * c:512 * c + 512],
                               psO[pidx][0:64, :],
                               rzb[:, pidx, :]))(),
                           reads=[f"psO{pidx}", f"rzb{pidx}"], writes=["opair"],
                           tag="l0a_onorm")
            sch.op("sync", (lambda p=p, W=W: lambda s: s.dma_start(
                out=o_sp[p][:, W:W + 1024], in_=opair[:, :]))(),
                reads=["opair"], writes=["o_sp"], dma=True, tag="l0a_ost")

    sch.barrier()
    stk.close()

    stk = ExitStack()
    alloc = lambda *a, **k: stk.enter_context(nc.sbuf_tensor(*a, **k))
    palloc = lambda *a, **k: stk.enter_context(nc.psum_tensor(*a, **k))
    psC = [palloc(f"l0a_c_ps{i}", [128, 512], F32) for i in range(4)]
    wp_sb = alloc("l0a_wp", [128, NC * D], BF16)
    ofm = alloc("l0a_ofm", [128, 2, NC, 512], BF16)
    xt2 = alloc("l0a_xt2", [128, 2 * D], F32)
    xn = alloc("l0a_xn", [128, 2 * D], F32)

    for i in range(NC):
        sch.op("gpsimd", (lambda i=i: lambda g: g.dma_start(
            out=wp_sb[:, i * D:(i + 1) * D], in_=wprojT[i * 128:(i + 1) * 128, :]))(),
            reads=["wprojT"], writes=["wp_sb"], dma=True, tag="l0a_wpld")

    def emit_ofm_loads(c8):
        b = c8 % 2
        for p in range(NPAIR):
            sch.op("gpsimd", (lambda p=p, c8=c8, b=b: lambda g: g.dma_start(
                out=ofm[:, b, p, :], in_=o_sp[p][:, c8 * 512:(c8 + 1) * 512]))(),
                reads=["o_sp", f"ofm{b}"], writes=[f"ofm{b}"], dma=True,
                tag="l0a_old")

    emit_ofm_loads(0)
    for c8 in range(8):
        b = c8 % 2
        if c8 + 1 < 8:
            emit_ofm_loads(c8 + 1)
        for j in range(4):
            t = c8 * 4 + j
            jb = j % 2
            sch.op("sync", (lambda t=t, jb=jb: lambda s: s.dma_start(
                out=xt2[:, jb * D:(jb + 1) * D], in_=x_in[t]))(),
                reads=[xin_name, f"x2t{jb}"], writes=[f"x2t{jb}"], dma=True,
                tag="l0a_xld")
            for oc in range(2):
                ps = psC[2 * oc + (j % 2)]
                pk = f"psC{2 * oc + (j % 2)}"

                def pr_fn(te, j=j, oc=oc, b=b, ps=ps):
                    for p in range(NPAIR):
                        ins = te.matmul(
                            ps[:, :], ofm[:, b, p, j * 128:(j + 1) * 128],
                            wp_sb[:, p * D + oc * 512: p * D + (oc + 1) * 512],
                            start=(p == 0), stop=(p == NPAIR - 1))
                    return ins
                sch.op("tensor", pr_fn, reads=[f"ofm{b}", "wp_sb"],
                       writes=[pk], tag="l0a_proj")
                sch.op("vector", (lambda oc=oc, jb=jb, ps=ps: lambda v:
                       v.tensor_add(xn[:, jb * D + oc * 512: jb * D + oc * 512 + 512],
                                    ps[:, :],
                                    xt2[:, jb * D + oc * 512: jb * D + oc * 512 + 512]))(),
                       reads=[pk, f"x2t{jb}"], writes=[f"xn{jb}"],
                       tag="l0a_res")
            sch.op("sync", (lambda t=t, jb=jb: lambda s: s.dma_start(
                out=x1[t], in_=xn[:, jb * D:(jb + 1) * D]))(),
                reads=[f"xn{jb}"], writes=[xout_name], dma=True, tag="l0a_xst")

    sch.barrier()
    stk.close()



def emit_l0f(st, xin_name, xout_name):
    from contextlib import ExitStack
    nc, sch, T = st["nc"], st["sch"], st["T"]
    x_in, x2 = T[xin_name], T[xout_name]
    ones1, onesrow = st["ones1"], st["onesrow"]
    DFF, R = 4 * D, 256
    NF = DFF // 128

    stk = ExitStack()
    alloc = lambda *a, **k: stk.enter_context(nc.sbuf_tensor(*a, **k))
    palloc = lambda *a, **k: stk.enter_context(nc.psum_tensor(*a, **k))

    u1_sb = alloc("l0f_u1", [128, NC, 256], BF16)
    v1_sb = alloc("l0f_v1", [128, 2, DFF], BF16)
    u2_sb = alloc("l0f_u2", [128, NF, 256], BF16)
    v2_sb = alloc("l0f_v2", [128, 2, D], BF16)
    b1_sb = alloc("l0f_b1", [1, DFF], BF16)
    b2_sb = alloc("l0f_b2", [1, D], BF16)
    xt = alloc("l0f_xt", [128, 4, D], F32)
    stats = alloc("l0f_stats", [128, 18], F32)
    lnt = alloc("l0f_lnt", [128, D], BF16)
    ln_ch = alloc("l0f_lnch", [128, NC, 512], BF16)
    p_fm = alloc("l0f_pfm", [128, 2, 512], BF16)
    h_fm = alloc("l0f_hfm", [128, NF, 512], BF16)
    q_fm = alloc("l0f_qfm", [128, 2, 512], BF16)
    sq = alloc("l0f_sq", [128, 2, 512], F32)
    th = alloc("l0f_th", [128, 2, 512], F32)
    xn = alloc("l0f_xn", [128, 2, 512], F32)

    psT = palloc("l0f_psT", [128, 1024], BF16)
    psP = palloc("l0f_psP", [128, 512], F32)
    psH = [palloc(f"l0f_psH{i}", [128, 512], F32) for i in range(2)]
    psQ = palloc("l0f_psQ", [128, 512], F32)
    psO = [palloc(f"l0f_psO{i}", [128, 512], F32) for i in range(2)]

    def wload(name, dst, src_ap, key):
        sch.op("gpsimd", lambda g: g.dma_start(out=dst, in_=src_ap),
               reads=[name], writes=[key], dma=True, tag="l0f_wl")
    U1, V1, u2r, V2, b1, b2 = T["U1"], T["V1"], T["u2r"], T["V2"], T["b1"], T["b2"]
    for i in range(NC):
        wload("U1", u1_sb[:, i, :], U1[i * 128:(i + 1) * 128, :], "u1_sb")
    for rc in range(2):
        wload("V1", v1_sb[:, rc, :], V1[rc * 128:(rc + 1) * 128, :], "v1_sb")
        wload("V2", v2_sb[:, rc, :], V2[rc * 128:(rc + 1) * 128, :], "v2_sb")
    wload("u2r", u2_sb[:, :, :],
          bass.AP(u2r, 0, [[256, 128], [128 * 256, NF], [1, 256]]), "u2_sb")
    wload("b1", b1_sb[:, :], b1[:, :], "b1_sb")
    wload("b2", b2_sb[:, :], b2[:, :], "b2_sb")
    sch.op("vector", lambda v: v.tensor_scalar_mul(
        u2_sb[:, :, :], u2_sb[:, :, :], 0.5), reads=["u2_sb"],
        writes=["u2_sb"], tag="l0f_u2h")

    C1 = 0.044715
    C2 = 0.7978845608028654

    for c8 in range(8):
        emit_ln_chunk(st, xin_name, x_in, c8, xt, stats, lnt, psT, ln_ch,
                      nslot=4)
        for rc in range(2):
            def p_fn(te, rc=rc):
                for i in range(NC):
                    ins = te.matmul(psP[:, :], u1_sb[:, i, rc * 128:(rc + 1) * 128],
                                    ln_ch[:, i, :],
                                    start=(i == 0), stop=(i == NC - 1))
                return ins
            sch.op("tensor", p_fn, reads=["u1_sb", "ln_ch"], writes=["psP"],
                   tag="l0f_p")
            sch.op("vector", (lambda rc=rc: lambda v: v.tensor_copy(
                p_fm[:, rc, :], psP[:, :]))(),
                reads=["psP"], writes=["p_fm"], tag="l0f_pe")
        for fc in range(NF):
            g = fc % 2
            ps = psH[g]
            pk = f"psH{g}"

            def h_fn(te, fc=fc, ps=ps):
                for rc in range(2):
                    te.matmul(ps[:, :], v1_sb[:, rc, fc * 128:(fc + 1) * 128],
                              p_fm[:, rc, :], start=(rc == 0), stop=False)
                return te.matmul(ps[:, :], b1_sb[:, fc * 128:(fc + 1) * 128],
                                 onesrow[:, :], start=False, stop=True)
            sch.op("tensor", h_fn, reads=["v1_sb", "p_fm", "b1_sb", "onesrow"],
                   writes=[pk], tag="l0f_h")
            sch.op("scalar", (lambda ps=ps, g=g: lambda sc: sc.activation(
                sq[:, g, :], ps[:, :], AF.Square))(),
                reads=[pk], writes=[f"sq{g}"], tag="l0f_sq")
            sch.op("vector", (lambda ps=ps, g=g: lambda v: v.tensor_mul(
                sq[:, g, :], sq[:, g, :], ps[:, :]))(),
                reads=[pk, f"sq{g}"], writes=[f"sq{g}"], tag="l0f_cube")
            sch.op("vector", (lambda ps=ps, g=g: lambda v: v.scalar_tensor_tensor(
                th[:, g, :], sq[:, g, :], C1, ps[:, :], ALU.mult, ALU.add))(),
                reads=[pk, f"sq{g}"], writes=[f"th{g}"], tag="l0f_arg")
            sch.op("scalar", (lambda g=g: lambda sc: sc.activation(
                th[:, g, :], th[:, g, :], AF.Tanh, scale=C2))(),
                reads=[f"th{g}"], writes=[f"th{g}"], tag="l0f_tanh")
            sch.op("vector", (lambda ps=ps, g=g, fc=fc: lambda v:
                   v.scalar_tensor_tensor(h_fm[:, fc, :], th[:, g, :], 1.0,
                                          ps[:, :], ALU.add, ALU.mult))(),
                reads=[pk, f"th{g}"], writes=["h_fm"], tag="l0f_hout")
        for rc in range(2):
            def q_fn(te, rc=rc):
                for fc in range(NF):
                    ins = te.matmul(psQ[:, :], u2_sb[:, fc, rc * 128:(rc + 1) * 128],
                                    h_fm[:, fc, :],
                                    start=(fc == 0), stop=(fc == NF - 1))
                return ins
            sch.op("tensor", q_fn, reads=["u2_sb", "h_fm"], writes=["psQ"],
                   tag="l0f_q")
            sch.op("vector", (lambda rc=rc: lambda v: v.tensor_copy(
                q_fm[:, rc, :], psQ[:, :]))(),
                reads=["psQ"], writes=["q_fm"], tag="l0f_qe")
        for j in range(4):
            t = c8 * 4 + j
            for oc in range(2):
                ps = psO[oc]
                pk = f"psO{oc}"

                def o_fn(te, j=j, oc=oc, ps=ps):
                    for rc in range(2):
                        te.matmul(ps[:, :], q_fm[:, rc, j * 128:(j + 1) * 128],
                                  v2_sb[:, rc, oc * 512:(oc + 1) * 512],
                                  start=(rc == 0), stop=False)
                    return te.matmul(ps[:, :], ones1[:, :],
                                     b2_sb[:, oc * 512:(oc + 1) * 512],
                                     start=False, stop=True)
                sch.op("tensor", o_fn, reads=["q_fm", "v2_sb", "b2_sb", "ones1"],
                       writes=[pk], tag="l0f_o")
                sch.op("vector", (lambda j=j, oc=oc, ps=ps: lambda v:
                       v.tensor_add(xn[:, oc, :], ps[:, :],
                                    xt[:, j, oc * 512:(oc + 1) * 512]))(),
                       reads=[pk, f"xt{j}"], writes=[f"xn{oc}"], tag="l0f_res")
            sch.op("sync", (lambda t=t: lambda s: s.dma_start(
                out=x2[t], in_=xn[:, :, :]))(),
                reads=["xn0", "xn1"], writes=[xout_name], dma=True,
                tag="l0f_xst")

    sch.barrier()
    stk.close()



TWO_PI = 2.0 * math.pi
MAGIC = 12582912.0


def emit_l1a(st, xin_name, xout_name):
    from contextlib import ExitStack
    nc, sch, T = st["nc"], st["sch"], st["T"]
    x_in, x3 = T[xin_name], T[xout_name]
    ones1, onesrow = st["ones1"], st["onesrow"]

    stk = ExitStack()
    alloc = lambda *a, **k: stk.enter_context(nc.sbuf_tensor(*a, **k))
    palloc = lambda *a, **k: stk.enter_context(nc.psum_tensor(*a, **k))

    wpv_sb = alloc("a_wpv", [128, NC, D], BF16)
    wk_sb = alloc("a_wk", [128, NC, D], BF16)
    wkv_sb = alloc("a_wkv", [128, NC, D], BF16)
    wg1_sb = alloc("a_wg1", [128, 16, D], BF16)
    wb1_sb = alloc("a_wb1", [128, NC, 512], BF16)
    wb2_sb = alloc("a_wb2", [128, 4, 2], BF16)
    wb2d = alloc("a_wb2d", [128, 4], BF16)
    wg2_sb = alloc("a_wg2", [128, NC], BF16)
    wo_sb = alloc("a_wo", [128, NC, D], BF16)
    lng_sb = alloc("a_lng", [128, NC], F32)
    lnb_sb = alloc("a_lnb", [128, NC], BF16)
    brow = alloc("a_brow", [1, 4 * D + 512], BF16)
    bg2r = alloc("a_bg2", [1, 1], F32)
    bb2r = alloc("a_bb2", [1, 2], F32)
    bb2d = alloc("a_bb2d", [1, 1], F32)
    bo_row = alloc("a_borow", [1, D], F32)
    bo_bf = alloc("a_bobf", [1, D], BF16)
    onescol = alloc("a_onescol", [128, 1], BF16)
    onescolf = alloc("a_onescolf", [128, 1], F32)
    onesf = alloc("a_onesf", [1, 128], F32)
    epsrow = alloc("a_epsrow", [1, 1], F32)

    ln_w = alloc("a_lnw", [128, 2, NC, 513], BF16)
    xr = alloc("a_xr", [128, 2, D], F32)
    xg = alloc("a_xg", [128, D], F32)
    xo = alloc("a_xo", [128, D], F32)
    stats = alloc("a_stats", [128, 36], F32)
    lnt = alloc("a_lnt", [128, D], BF16)
    ghb = alloc("a_ghb", [128, 512], BF16)
    kc = alloc("a_kc", [128, 513], BF16)
    ks = alloc("a_ks", [128, 513], BF16)
    tl = {nm: alloc("a_" + nm, [128, 512],
                    BF16 if nm in ("pc", "ps", "kvg") else F32)
          for nm in ["A", "B", "C", "pc", "ps", "kvg",
                     "a1", "a2", "a3", "a4"]}
    TK = {"A": "tA", "B": "tB", "C": "tC", "pc": "tpc",
          "ps": "tps", "kvg": "tkvg", "a1": "ta1", "a2": "ta2",
          "a3": "ta3", "a4": "ta4"}
    blend = alloc("a_blend", [128, NC, 512], BF16)
    g_b = alloc("a_gb", [128, 512], BF16)
    w0_b = alloc("a_w0b", [128, 512], BF16)
    w1_b = alloc("a_w1b", [128, 512], BF16)
    rb_b = alloc("a_rbb", [128, 512], BF16)
    nm_b = alloc("a_nmb", [128, 512], BF16)
    zeros = alloc("a_zeros", [128, 512], F32)
    rows = alloc("a_rows", [1, 7, 512], F32)
    ROW = {"g": 0, "gc": 1, "sq": 2, "rsq": 3, "a0": 4, "d": 5, "bw0": 6,
           "rsq32": 2, "t0": 5, "w0": 4, "w1": 5,
           "mean": 0, "msq": 1, "var": 2, "sd": 3, "rstd": 6, "nm": 4}
    def row(nm):
        return rows[:, ROW[nm], :]
    def rk(nm):
        return f"row{ROW[nm]}"
    car_sc = alloc("a_carsc", [128, NC, 4], F32)
    car_kc = alloc("a_carkc", [128, NC, 2], F32)
    car_g = alloc("a_carg", [1, 1], F32)

    psT = palloc("a_psT", [128, 1024], BF16)
    ps1 = palloc("a_ps1", [128, 512], F32)
    ps2 = palloc("a_ps2", [128, 512], F32)
    ps3 = palloc("a_ps3", [128, 512], F32)
    ps4 = palloc("a_ps4", [128, 512], F32)
    psr1 = palloc("a_psr1", [128, 512], F32)
    psacc = palloc("a_psacc", [128, 512], F32)
    pgh = palloc("a_pgh", [128, 512], F32)

    V, SC, TE, SY, GP = "vector", "scalar", "tensor", "sync", "gpsimd"

    Wpv, Wk, Wkv = T["Wpv"], T["Wk1a"], T["Wkv"]
    Wg1, Wb1, Wo = T["Wg1"], T["Wb1"], T["Wo1a"]
    for i in range(NC):
        for wsb, wdr, key in ((wpv_sb, Wpv, "wpv"), (wk_sb, Wk, "wk"),
                              (wkv_sb, Wkv, "wkv"), (wo_sb, Wo, "wo")):
            sch.op(GP, (lambda wsb=wsb, wdr=wdr, i=i: lambda g: g.dma_start(
                out=wsb[:, i, :], in_=wdr[i * 128:(i + 1) * 128, :]))(),
                reads=[wdr], writes=[key], dma=True, tag="a_wl")
        sch.op(GP, (lambda i=i: lambda g: g.dma_start(
            out=wb1_sb[:, i, :], in_=Wb1[i * 128:(i + 1) * 128, :]))(),
            reads=[Wb1], writes=["wb1"], dma=True, tag="a_wl")
    for i2 in range(16):
        sch.op(GP, (lambda i2=i2: lambda g: g.dma_start(
            out=wg1_sb[:, i2, :], in_=Wg1[i2 * 128:(i2 + 1) * 128, :]))(),
            reads=[Wg1], writes=["wg1"], dma=True, tag="a_wl")
    sch.op(GP, lambda g: g.dma_start(
        out=wb2_sb[:, :, :],
        in_=bass.AP(T["wb2r"], 0, [[2, 128], [128 * 2, 4], [1, 2]])),
        reads=[T["wb2r"]], writes=["wb2"], dma=True, tag="a_wl")
    sch.op(GP, lambda g: g.dma_start(out=wg2_sb[:, :], in_=T["wg2c"][:, :]),
           reads=[T["wg2c"]], writes=["wg2"], dma=True, tag="a_wl")
    sch.op(SY, lambda s: s.dma_start(out=lng_sb[:, :], in_=T["lngc"][:, :]),
           reads=[T["lngc"]], writes=["lng"], dma=True, tag="a_wl")
    sch.op(GP, lambda g: g.dma_start(out=lnb_sb[:, :], in_=T["lnbc"][:, :]),
           reads=[T["lnbc"]], writes=["lnb"], dma=True, tag="a_wl")
    for off, nm in ((0, "bpv"), (D, "bk1a"), (2 * D, "bkv"), (3 * D, "bg1")):
        sch.op(GP, (lambda off=off, nm=nm: lambda g: g.dma_start(
            out=brow[:, off:off + D], in_=T[nm][:, :]))(),
            reads=[T[nm]], writes=["brow"], dma=True, tag="a_wl")
    sch.op(GP, lambda g: g.dma_start(out=brow[:, 4 * D:4 * D + 512],
                                     in_=T["bb1"][:, :]),
           reads=[T["bb1"]], writes=["brow"], dma=True, tag="a_wl")
    sch.op(SY, lambda s: s.dma_start(out=bg2r[:, :], in_=T["bg2"][:, :]),
           reads=[T["bg2"]], writes=["bg2r"], dma=True, tag="a_wl")
    sch.op(SY, lambda s: s.dma_start(out=bb2r[:, :], in_=T["bb2"][:, :]),
           reads=[T["bb2"]], writes=["bb2r"], dma=True, tag="a_wl")
    sch.op(SY, lambda s: s.dma_start(out=bo_row[:, :], in_=T["bo1a"][:, :]),
           reads=[T["bo1a"]], writes=["borow"], dma=True, tag="a_wl")

    sch.op(V, lambda v: v.tensor_sub(bb2d[:, :], bb2r[:, 0:1], bb2r[:, 1:2]),
           reads=["bb2r"], writes=["bb2d"], tag="a_pre")
    sch.op(V, lambda v: v.tensor_sub(wb2d[:, :], wb2_sb[:, :, 0],
                                     wb2_sb[:, :, 1]),
           reads=["wb2"], writes=["wb2d"], tag="a_pre")
    sch.op(V, lambda v: v.memset(onescol[:, :], 1.0), writes=["onescol"])
    sch.op(V, lambda v: v.memset(onescolf[:, :], 1.0), writes=["onescolf"])
    sch.op(V, lambda v: v.memset(onesf[:, :], 1.0), writes=["onesf"])
    sch.op(V, lambda v: v.memset(epsrow[:, :], 1e-5), writes=["epsrow"])
    sch.op(V, lambda v: v.memset(zeros[:, :], 0.0), writes=["zeros"])
    sch.op(V, lambda v: v.memset(ln_w[:, 0, :, 0:1], 0.0), writes=["lnw00"])
    sch.op(V, lambda v: v.memset(kc[:, 0:1], 0.0), writes=["kc0"])
    sch.op(V, lambda v: v.memset(ks[:, 0:1], 0.0), writes=["ks0"])
    sch.op(V, lambda v: v.memset(car_sc[:, :, :], 0.0), writes=["carsc"])
    sch.op(V, lambda v: v.memset(car_kc[:, :, :], 0.0), writes=["carkc"])
    sch.op(V, lambda v: v.memset(car_g[:, :], 0.0), writes=["carg"])

    for oc in range(2):
        def bo_fn(te, oc=oc):
            for i in range(NC):
                ins = te.matmul(psr1[0:1, :], lnb_sb[:, i:i + 1],
                                wo_sb[:, i, oc * 512:(oc + 1) * 512],
                                start=(i == 0), stop=(i == NC - 1))
            return ins
        sch.op(TE, bo_fn, reads=["lnb", "wo"], writes=["psr1"], tag="a_bo")
        sch.op(V, (lambda oc=oc: lambda v: v.tensor_add(
            bo_row[:, oc * 512:(oc + 1) * 512], psr1[0:1, :],
            bo_row[:, oc * 512:(oc + 1) * 512]))(),
            reads=["psr1", "borow"], writes=["borow"], tag="a_bo")
    sch.op(V, lambda v: v.tensor_copy(bo_bf[:, :], bo_row[:, :]),
           reads=["borow"], writes=["bobf"], tag="a_bo")
    for i in range(NC):
        sch.op(V, (lambda i=i: lambda v: v.tensor_scalar_mul(
            wo_sb[:, i, :], wo_sb[:, i, :], lng_sb[:, i:i + 1]))(),
            reads=["wo", "lng"], writes=["wo"], tag="a_wog")


    def lnv_for(wp):
        def lnv(i2):
            if i2 < NC:
                return ln_w[:, wp, i2, 1:513]
            return ln_w[:, wp, i2 - NC, 0:512]
        return lnv

    def sec_next(w):
        units = []
        wp = w % 2
        lnk = f"lnw{wp}"
        lnk0 = f"lnw0{wp}"
        lnv = lnv_for(wp)
        if w > 0:
            units.append(lambda wp=wp, lnk0=lnk0: sch.op(
                V, (lambda wp=wp: lambda v: v.tensor_copy(
                    ln_w[:, wp, :, 0:1], ln_w[:, 1 - wp, :, 512:513]))(),
                reads=[f"lnw{1 - wp}"], writes=[lnk0], tag="a_lnc"))
        emit_ln_chunk(st, xin_name, x_in, w, xr, stats, lnt, psT,
                      ln_w[:, wp, :, :], ln_col0=1, ln_keys=[lnk], nslot=2,
                      units=units)

        for Dc in range(NC):
            def gates_unit(Dc=Dc, lnv=lnv, lnk=lnk, lnk0=lnk0):
                def gh_fn(te, Dc=Dc, lnv=lnv):
                    for i2 in range(16):
                        te.matmul(pgh[:, :],
                                  wg1_sb[:, i2, Dc * 128:(Dc + 1) * 128],
                                  lnv(i2), start=(i2 == 0), stop=False)
                    return te.matmul(
                        pgh[:, :],
                        brow[:, 3 * D + Dc * 128:3 * D + (Dc + 1) * 128],
                        onesrow[:, :], start=False, stop=True)
                sch.op(TE, gh_fn, reads=["wg1", lnk, lnk0, "brow", "onesrow"],
                       writes=["pgh"], tag="a_gh")
                sch.op(SC, lambda sc: sc.activation(ghb[:, :], pgh[:, :],
                                                    AF.Gelu),
                       reads=["pgh"], writes=["ghb"], tag="a_ghg")
                sch.op(TE, (lambda Dc=Dc: lambda te: te.matmul(
                    psacc[64:65, :], wg2_sb[:, Dc:Dc + 1], ghb[:, :],
                    start=(Dc == 0), stop=(Dc == NC - 1)))(),
                    reads=["wg2", "ghb"], writes=["psum_g"], tag="a_gacc")
            units.append(gates_unit)

        def gates_rows():
            sch.op(SC, lambda sc: sc.activation(
                row("g"), psacc[64:65, :], AF.Sigmoid, bias=bg2r[0:1, 0:1]),
                reads=["psum_g", "bg2r"], writes=[rk("g")], tag="a_sig")
            sch.op(V, lambda v: v.tensor_tensor_scan(
                row("gc"), row("g"), zeros[0:1, :], car_g[0:1, 0:1],
                ALU.add, ALU.add), reads=[rk("g"), "zeros", "carg"],
                writes=[rk("gc")], tag="a_gc")
            sch.op(V, lambda v: v.tensor_copy(car_g[:, :],
                                              rows[:, ROW["gc"], 511:512]),
                   reads=[rk("gc")], writes=["carg"], tag="a_gc2")
            sch.op(V, lambda v: v.tensor_scalar_max(row("gc"), row("gc"), 1.0),
                   reads=[rk("gc")], writes=[rk("gc")], tag="a_gc3")
            sch.op(SC, lambda sc: sc.activation(row("sq"), row("gc"), AF.Sqrt),
                   reads=[rk("gc")], writes=[rk("sq")], tag="a_gc4")
            sch.op(V, lambda v: v.reciprocal(row("rsq"), row("sq")),
                   reads=[rk("sq")], writes=[rk("rsq")], tag="a_gc5")
        units.append(gates_rows)

        for Dp in range(4):
            def bh_unit(Dp=Dp, lnv=lnv, lnk=lnk):
                def bh_fn(te, Dp=Dp, lnv=lnv):
                    for i in range(NC):
                        te.matmul(pgh[:, :],
                                  wb1_sb[:, i, Dp * 128:(Dp + 1) * 128],
                                  lnv(i), start=(i == 0), stop=False)
                    return te.matmul(
                        pgh[:, :],
                        brow[:, 4 * D + Dp * 128:4 * D + (Dp + 1) * 128],
                        onesrow[:, :], start=False, stop=True)
                sch.op(TE, bh_fn, reads=["wb1", lnk, "brow", "onesrow"],
                       writes=["pgh"], tag="a_bh")
                sch.op(SC, lambda sc: sc.activation(ghb[:, :], pgh[:, :],
                                                    AF.Gelu),
                       reads=["pgh"], writes=["ghb"], tag="a_bhg")
                sch.op(TE, (lambda Dp=Dp: lambda te: te.matmul(
                    psacc[64:65, :], wb2d[:, Dp:Dp + 1], ghb[:, :],
                    start=(Dp == 0), stop=(Dp == 3)))(),
                    reads=["wb2d", "ghb"], writes=["psum_g"], tag="a_bacc")
            units.append(bh_unit)

        def bh_rows():
            sch.op(SC, lambda sc: sc.activation(
                row("bw0"), psacc[64:65, :], AF.Sigmoid,
                bias=bb2d[0:1, 0:1]),
                reads=["psum_g", "bb2d"], writes=[rk("bw0")], tag="a_bw3")
            sch.op(V, lambda v: v.tensor_scalar_mul(row("rsq32"), row("rsq"),
                                                    1.0 / 32.0),
                   reads=[rk("rsq")], writes=[rk("rsq32")], tag="a_bw5")
            sch.op(V, lambda v: v.tensor_mul(row("t0"), row("bw0"),
                                             row("rsq32")),
                   reads=[rk("bw0"), rk("rsq32")], writes=[rk("t0")],
                   tag="a_bw6")
            sch.op(V, lambda v: v.tensor_scalar_mul(row("w0"), row("bw0"),
                                                    1.0 / 32.0),
                   reads=[rk("bw0")], writes=[rk("w0")], tag="a_bw4")
            sch.op(V, lambda v: v.tensor_sub(row("w1"), row("rsq32"),
                                             row("t0")),
                   reads=[rk("rsq32"), rk("t0")], writes=[rk("w1")],
                   tag="a_bw7")
        units.append(bh_rows)

        def bcast():
            for rnm, dst, dk in (("g", g_b, "g_b"), ("w0", w0_b, "w0_b"),
                                 ("w1", w1_b, "w1_b")):
                sch.op(TE, (lambda rnm=rnm: lambda te: te.matmul(
                    pgh[:, :], onesf[:, :], row(rnm),
                    start=True, stop=True))(),
                    reads=[rk(rnm), "onesf"], writes=["pgh"], tag="a_bc")
                sch.op(V, (lambda dst=dst: lambda v: v.tensor_copy(
                    dst[:, :], pgh[:, :]))(),
                    reads=["pgh"], writes=[dk], tag="a_bc2")
        units.append(bcast)
        return units

    def emit_E(w, Dc):
        W = 512 * w
        wp = w % 2
        lnk = f"lnw{wp}"
        lnv = lnv_for(wp)
        sch.op(SY, (lambda Dc=Dc, W=W: lambda s: s.dma_start(
            out=tl["pc"][:, :], in_=T["pc_sp"][Dc][:, W:W + 512]))(),
            reads=[T["pc_sp"]], writes=["tpc"], dma=True, tag="a_pcl")
        sch.op(SY, (lambda Dc=Dc, W=W: lambda s: s.dma_start(
            out=tl["ps"][:, :], in_=T["ps_sp"][Dc][:, W:W + 512]))(),
            reads=[T["ps_sp"]], writes=["tps"], dma=True, tag="a_psl")

        def proj(te, wsb, boff, ps, Dc, lnv=lnv):
            for i in range(NC):
                te.matmul(ps[:, :], wsb[:, i, Dc * 128:(Dc + 1) * 128],
                          lnv(i), start=(i == 0), stop=False)
            return te.matmul(ps[:, :],
                             brow[:, boff + Dc * 128:boff + (Dc + 1) * 128],
                             onesrow[:, :], start=False, stop=True)
        pv = ps1 if Dc % 2 == 0 else ps3
        pvk = "ps1" if Dc % 2 == 0 else "ps3"
        kv = ps2 if Dc % 2 == 0 else ps4
        kvk = "ps2" if Dc % 2 == 0 else "ps4"
        sch.op(TE, (lambda Dc=Dc, pv=pv, proj=proj: lambda te: proj(
            te, wpv_sb, 0, pv, Dc))(),
            reads=["wpv", lnk, "brow", "onesrow"], writes=[pvk],
            tag="a_pv")
        sch.op(TE, (lambda Dc=Dc, kv=kv, proj=proj: lambda te: proj(
            te, wkv_sb, 2 * D, kv, Dc))(),
            reads=["wkv", lnk, "brow", "onesrow"], writes=[kvk],
            tag="a_kv")
        sch.op(TE, (lambda Dc=Dc, proj=proj: lambda te: proj(
            te, wk_sb, D, psr1, Dc))(),
            reads=["wk", lnk, "brow", "onesrow"], writes=["psr1"],
            tag="a_kp")
        sch.op(SC, lambda sc: sc.activation(tl["C"][:, :], psr1[:, :],
                                            AF.Tanh),
               reads=["psr1"], writes=["tC"], tag="a_tanh")
        if w > 0:
            sch.op(GP, (lambda Dc=Dc: lambda v: v.tensor_copy(
                kc[:, 0:1], car_kc[:, Dc, 0:1]))(),
                reads=["carkc"], writes=["kc0"], tag="a_kcc")
            sch.op(GP, (lambda Dc=Dc: lambda v: v.tensor_copy(
                ks[:, 0:1], car_kc[:, Dc, 1:2]))(),
                reads=["carkc"], writes=["ks0"], tag="a_ksc")
        sch.op(SC, lambda sc: sc.activation(ks[:, 1:513], tl["C"][:, :],
                                            AF.Sin, scale=math.pi),
               reads=["tC"], writes=["ksm"], tag="a_ksin")
        sch.op(SC, lambda sc: sc.activation(tl["A"][:, :], tl["C"][:, :],
                                            AF.Sin, scale=math.pi / 2),
               reads=["tC"], writes=["tA"], tag="a_ksin2")
        sch.op(GP, lambda v: v.tensor_mul(tl["A"][:, :], tl["A"][:, :],
                                          tl["A"][:, :]),
               reads=["tA"], writes=["tA"], tag="a_ksq")
        sch.op(GP, lambda v: v.tensor_scalar(kc[:, 1:513], tl["A"][:, :],
                                            -2.0, 1.0, ALU.mult, ALU.add),
               reads=["tA"], writes=["kcm"], tag="a_kcos")
        sch.op(V, (lambda pv=pv, pvk=pvk: lambda v: v.tensor_mul(
            tl["a1"][:, :], tl["pc"][:, :], pv[:, :]))(),
               reads=["tpc", pvk], writes=["ta1"], tag="a_m1")
        sch.op(V, (lambda pv=pv, pvk=pvk: lambda v: v.tensor_mul(
            tl["a2"][:, :], tl["ps"][:, :], pv[:, :]))(),
               reads=["tps", pvk], writes=["ta2"], tag="a_m2")
        sch.op(V, (lambda kv=kv, kvk=kvk: lambda v: v.tensor_mul(
            tl["kvg"][:, :], kv[:, :], g_b[:, :]))(),
               reads=[kvk, "g_b"], writes=["tkvg"], tag="a_m3")
        sch.op(GP, lambda v: v.tensor_mul(tl["a3"][:, :], kc[:, 0:512],
                                         tl["kvg"][:, :]),
               reads=["kcm", "kc0", "tkvg"], writes=["ta3"], tag="a_m4")
        sch.op(GP, lambda v: v.tensor_mul(tl["a4"][:, :], ks[:, 0:512],
                                         tl["kvg"][:, :]),
               reads=["ksm", "ks0", "tkvg"], writes=["ta4"], tag="a_m5")
        sch.op(V, (lambda Dc=Dc: lambda v: v.tensor_copy(
            car_kc[:, Dc, 0:1], kc[:, 512:513]))(),
            reads=["kcm"], writes=["carkc"], tag="a_kcs")
        sch.op(V, (lambda Dc=Dc: lambda v: v.tensor_copy(
            car_kc[:, Dc, 1:2], ks[:, 512:513]))(),
            reads=["ksm"], writes=["carkc"], tag="a_kss")
        for si, anm in enumerate(["a1", "a2", "a3", "a4"]):
            sch.op(V, (lambda si=si, anm=anm, Dc=Dc: lambda v:
                   v.tensor_tensor_scan(
                       tl[anm][:, :], tl[anm][:, :], zeros[:, :],
                       car_sc[:, Dc, si:si + 1], ALU.add, ALU.add))(),
                reads=[f"t{anm}", "zeros", "carsc"], writes=[f"t{anm}"],
                tag="a_scan")
            sch.op(V, (lambda si=si, anm=anm, Dc=Dc: lambda v:
                   v.tensor_copy(car_sc[:, Dc, si:si + 1],
                                 tl[anm][:, 511:512]))(),
                reads=[f"t{anm}"], writes=["carsc"], tag="a_scar")
        sch.op(V, lambda v: v.tensor_mul(tl["A"][:, :], tl["pc"][:, :],
                                         tl["a1"][:, :]),
               reads=["tpc", "ta1"], writes=["tA"], tag="a_c1")
        sch.op(V, lambda v: v.tensor_mul(tl["C"][:, :], tl["ps"][:, :],
                                         tl["a2"][:, :]),
               reads=["tps", "ta2"], writes=["tC"], tag="a_c2")
        sch.op(V, lambda v: v.tensor_add(tl["A"][:, :], tl["A"][:, :],
                                         tl["C"][:, :]),
               reads=["tA", "tC"], writes=["tA"], tag="a_c3")
        sch.op(GP, lambda v: v.tensor_mul(tl["B"][:, :], kc[:, 1:513],
                                         tl["a3"][:, :]),
               reads=["kcm", "ta3"], writes=["tB"], tag="a_c4")
        sch.op(GP, lambda v: v.tensor_mul(tl["C"][:, :], ks[:, 1:513],
                                         tl["a4"][:, :]),
               reads=["ksm", "ta4"], writes=["tC"], tag="a_c5")
        sch.op(GP, lambda v: v.tensor_add(tl["B"][:, :], tl["B"][:, :],
                                         tl["C"][:, :]),
               reads=["tB", "tC"], writes=["tB"], tag="a_c6")
        sch.op(V, lambda v: v.tensor_mul(tl["A"][:, :], tl["A"][:, :],
                                         w0_b[:, :]),
               reads=["tA", "w0_b"], writes=["tA"], tag="a_c7")
        sch.op(GP, lambda v: v.tensor_mul(tl["B"][:, :], tl["B"][:, :],
                                         w1_b[:, :]),
               reads=["tB", "w1_b"], writes=["tB"], tag="a_c8")
        sch.op(V, (lambda Dc=Dc: lambda v: v.tensor_add(
            blend[:, Dc, :], tl["A"][:, :], tl["B"][:, :]))(),
            reads=["tA", "tB"], writes=["blend"], tag="a_c9")
        def stats_unit(Dc=Dc):
            sch.op(TE, (lambda Dc=Dc: lambda te: te.matmul(
                psacc[0:1, :], onescol[:, :], blend[:, Dc, :],
                start=(Dc == 0), stop=(Dc == NC - 1)))(),
                reads=["blend", "onescol"], writes=["psum_s"], tag="a_sum")
            sq = tl["A"] if Dc % 2 == 0 else tl["C"]
            sqk = "tA" if Dc % 2 == 0 else "tC"
            sch.op(GP, (lambda Dc=Dc, sq=sq: lambda g: g.tensor_mul(
                sq[:, :], blend[:, Dc, :], blend[:, Dc, :]))(),
                reads=["blend"], writes=[sqk], tag="a_bsq")
            sch.op(TE, (lambda Dc=Dc, sq=sq, sqk=sqk: lambda te: te.matmul(
                psacc[32:33, :], onescolf[:, :], sq[:, :],
                start=(Dc == 0), stop=(Dc == NC - 1)))(),
                reads=[sqk, "onescolf"], writes=["psum_q"], tag="a_ssq")
        return stats_unit

    def emit_lnrows(w):
        sch.op(V, lambda v: v.tensor_scalar_mul(row("mean"), psacc[0:1, :],
                                                1.0 / D),
               reads=["psum_s"], writes=[rk("mean")], tag="a_ln1")
        sch.op(V, lambda v: v.tensor_scalar_mul(row("msq"), psacc[32:33, :],
                                                1.0 / D),
               reads=["psum_q"], writes=[rk("msq")], tag="a_ln2")
        sch.op(V, lambda v: v.tensor_mul(row("t0"), row("mean"), row("mean")),
               reads=[rk("mean")], writes=[rk("t0")], tag="a_ln3")
        sch.op(V, lambda v: v.tensor_sub(row("var"), row("msq"), row("t0")),
               reads=[rk("msq"), rk("t0")], writes=[rk("var")], tag="a_ln4")
        sch.op(SC, lambda sc: sc.activation(row("sd"), row("var"), AF.Sqrt,
                                            bias=epsrow[0:1, 0:1]),
               reads=[rk("var"), "epsrow"], writes=[rk("sd")], tag="a_ln5")
        sch.op(V, lambda v: v.reciprocal(row("rstd"), row("sd")),
               reads=[rk("sd")], writes=[rk("rstd")], tag="a_ln6")
        sch.op(V, lambda v: v.scalar_tensor_tensor(
            row("nm"), row("mean"), -1.0, row("rstd"), ALU.mult, ALU.mult),
            reads=[rk("mean"), rk("rstd")], writes=[rk("nm")], tag="a_ln7")
        for rnm, dst, dk in (("rstd", rb_b, "rb_b"), ("nm", nm_b, "nm_b")):
            sch.op(TE, (lambda rnm=rnm: lambda te: te.matmul(
                pgh[:, :], onesf[:, :], row(rnm), start=True, stop=True))(),
                reads=[rk(rnm), "onesf"], writes=["pgh"], tag="a_lnb1")
            sch.op(V, (lambda dst=dst: lambda v: v.tensor_copy(
                dst[:, :], pgh[:, :]))(),
                reads=["pgh"], writes=[dk], tag="a_lnb2")

    def emit_apply(w):
        for Dc in range(NC):
            sch.op(GP, (lambda Dc=Dc: lambda g: g.tensor_mul(
                ghb[:, :], blend[:, Dc, :], rb_b[:, :]))(),
                reads=["blend", "rb_b"], writes=["ghb"], tag="a_ap1")
            sch.op(V, (lambda Dc=Dc: lambda v: v.tensor_add(
                blend[:, Dc, :], ghb[:, :], nm_b[:, :]))(),
                reads=["ghb", "nm_b"], writes=["blend"], tag="a_ap2")

    def emit_G(w):
        for j in range(4):
            t = 4 * w + j
            sch.op(SY, (lambda t=t: lambda s: s.dma_start(
                out=xg[:, :], in_=x_in[t]))(),
                reads=[xin_name], writes=["xg"], dma=True, tag="a_xld")
            def bn_fn(v):
                v.bn_stats(stats[:, 18:24], xg[:, 0:512])
                return v.bn_stats(stats[:, 24:30], xg[:, 512:1024])
            sch.op(V, bn_fn, reads=["xg"], writes=["st2_bn"], tag="a_gbn")
            sch.op(V, lambda v: v.bn_aggr(stats[:, 30:32], stats[:, 18:30]),
                   reads=["st2_bn"], writes=["st2_mv"], tag="a_gag")
            sch.op(SC, lambda sc: sc.activation(
                stats[:, 32:33], stats[:, 31:32], AF.Sqrt,
                bias=st["epsT"][:, :]),
                reads=["st2_mv", "epsT"], writes=["st2_sd"], tag="a_gsq")
            sch.op(V, lambda v: v.reciprocal(stats[:, 33:34], stats[:, 32:33]),
                   reads=["st2_sd"], writes=["st2_rstd"], tag="a_grc")
            sch.op(V, lambda v: v.scalar_tensor_tensor(
                stats[:, 34:35], stats[:, 30:31], -1.0, stats[:, 33:34],
                ALU.mult, ALU.mult),
                reads=["st2_mv", "st2_rstd"], writes=["st2_nm"], tag="a_gnm")
            for oc in range(2):
                def wo_fn(te, j=j, oc=oc):
                    for i in range(NC):
                        te.matmul(pgh[:, :],
                                  blend[:, i, j * 128:(j + 1) * 128],
                                  wo_sb[:, i, oc * 512:(oc + 1) * 512],
                                  start=(i == 0), stop=False)
                    return te.matmul(pgh[:, :], ones1[:, :],
                                     bo_bf[:, oc * 512:(oc + 1) * 512],
                                     start=False, stop=True)
                xok = f"xo{oc}"
                sch.op(TE, wo_fn, reads=["blend", "wo", "bobf", "ones1"],
                       writes=["pgh"], tag="a_wo2")
                sch.op(SC, (lambda oc=oc: lambda sc: sc.activation(
                    xo[:, oc * 512:(oc + 1) * 512], pgh[:, :], AF.Identity,
                    bias=stats[:, 34:35]))(),
                    reads=["pgh", "st2_nm"], writes=[xok], tag="a_res2")
                sch.op(V, (lambda oc=oc: lambda v: v.tensor_add(
                    xo[:, oc * 512:(oc + 1) * 512],
                    xo[:, oc * 512:(oc + 1) * 512],
                    xg[:, oc * 512:(oc + 1) * 512]))(),
                    reads=[xok, "xg"], writes=[xok], tag="a_res")
                sch.op(V, (lambda oc=oc: lambda v: v.scalar_tensor_tensor(
                    xo[:, oc * 512:(oc + 1) * 512],
                    xg[:, oc * 512:(oc + 1) * 512],
                    stats[:, 33:34],
                    xo[:, oc * 512:(oc + 1) * 512], ALU.mult, ALU.add))(),
                    reads=["xg", "st2_rstd", xok], writes=[xok],
                    tag="a_res3")
                sch.op(SY, (lambda t=t, oc=oc: lambda s: s.dma_start(
                    out=x3[t][:, oc * 512:(oc + 1) * 512],
                    in_=xo[:, oc * 512:(oc + 1) * 512]))(),
                    reads=[xok], writes=[xout_name], dma=True, tag="a_xst")

    nxt = sec_next(0)
    for u in nxt:
        u()
    for w in range(8):
        nxt = sec_next(w + 1) if w < 7 else []
        pos = 0
        stats_units = []
        for Dc in range(NC):
            take = (len(nxt) * Dc) // NC - pos
            for u in nxt[pos:pos + take]:
                u()
            pos += take
            stats_units.append(emit_E(w, Dc))
        for u in nxt[pos:]:
            u()
        pos = len(nxt)
        for u in stats_units:
            u()
        emit_lnrows(w)
        emit_apply(w)
        emit_G(w)

    sch.barrier()
    stk.close()



def emit_l1f(st, xin_name, xout_name):
    from contextlib import ExitStack
    nc, sch, T = st["nc"], st["sch"], st["T"]
    x_in, xout = T[xin_name], T[xout_name]
    q_sp, k_sp, v_sp, o_sp = T["q_sp"], T["k_sp"], T["v_sp"], T["o_sp"]

    stk = ExitStack()
    alloc = lambda *a, **k: stk.enter_context(nc.sbuf_tensor(*a, **k))
    palloc = lambda *a, **k: stk.enter_context(nc.psum_tensor(*a, **k))
    wq_sb = alloc("f_wq", [128, NC, D], BF16)
    wk_sb = alloc("f_wk", [128, NC, D], BF16)
    wv_sb = alloc("f_wv", [128, NC, D], BF16)
    xt = alloc("f_xt", [128, 2, D], F32)
    stats = alloc("f_stats", [128, 18], F32)
    lnt = alloc("f_lnt", [128, D], BF16)
    ln_ch = alloc("f_lnch", [128, NC, 512], BF16)
    qkv_e = alloc("f_qkve", [128, 2, 512], BF16)
    vsb = alloc("f_vsb", [128, 2, D], BF16)
    psT = palloc("f_psT", [128, 1024], BF16)
    psA = palloc("f_psA", [128, 512], F32)
    psB = palloc("f_psB", [128, 512], F32)

    for i in range(NC):
        for wsb, nm, key in ((wq_sb, "Wq1f", "wq"), (wk_sb, "Wk1f", "wk"),
                             (wv_sb, "Wv1f", "wv")):
            sch.op("gpsimd", (lambda wsb=wsb, nm=nm, i=i: lambda g:
                   g.dma_start(out=wsb[:, i, :],
                               in_=T[nm][i * 128:(i + 1) * 128, :]))(),
                reads=[T[nm]], writes=[key], dma=True, tag="f_wl")

    for c8 in range(8):
        emit_ln_chunk(st, xin_name, x_in, c8, xt, stats, lnt, psT, ln_ch,
                      nslot=2)
        for oc in range(16):
            ps = psA if oc % 2 == 0 else psB
            pk = "psA" if oc % 2 == 0 else "psB"
            wsb = wq_sb if oc < 8 else wk_sb
            wkey = "wq" if oc < 8 else "wk"
            p = oc % 8

            def qk_fn(te, p=p, ps=ps, wsb=wsb):
                for i in range(NC):
                    ins = te.matmul(ps[:, :], wsb[:, i, p * 128:(p + 1) * 128],
                                    ln_ch[:, i, :],
                                    start=(i == 0), stop=(i == NC - 1))
                return ins
            sch.op("tensor", qk_fn, reads=[wkey, "ln_ch"], writes=[pk],
                   tag="f_qk")
            eb = qkv_e[:, oc % 2, :]
            ek = f"qkve{oc % 2}"
            sch.op("vector", (lambda ps=ps, eb=eb: lambda v: v.tensor_copy(
                eb, ps[:, :]))(), reads=[pk], writes=[ek],
                tag="f_qkev")
            dst = q_sp if oc < 8 else k_sp
            sch.op("sync", (lambda dst=dst, p=p, c8=c8, eb=eb: lambda s:
                   s.dma_start(out=dst[p][:, c8 * 512:(c8 + 1) * 512], in_=eb))(),
                   reads=[ek], writes=["q_sp" if oc < 8 else "k_sp"],
                   dma=True, tag="f_qkst")
        for j in range(4):
            t = c8 * 4 + j
            for jc in range(2):
                ps = psA if jc == 0 else psB
                pk = "psA" if jc == 0 else "psB"

                def v_fn(te, j=j, jc=jc, ps=ps):
                    for i in range(NC):
                        ins = te.matmul(
                            ps[:, :], ln_ch[:, i, j * 128:(j + 1) * 128],
                            wv_sb[:, i, jc * 512:(jc + 1) * 512],
                            start=(i == 0), stop=(i == NC - 1))
                    return ins
                sch.op("tensor", v_fn, reads=["wv", "ln_ch"], writes=[pk],
                       tag="f_v")
                vk = f"vsb{j % 2}"
                sch.op("vector", (lambda ps=ps, j=j, jc=jc: lambda v:
                       v.tensor_copy(vsb[:, j % 2, jc * 512:(jc + 1) * 512],
                                     ps[:, :]))(),
                       reads=[pk], writes=[vk], tag="f_vev")
            sch.op("sync", (lambda t=t, j=j: lambda s: s.dma_start(
                out=v_sp[t][:, 0:D], in_=vsb[:, j % 2, :]))(),
                reads=[f"vsb{j % 2}"], writes=["v_sp"], dma=True, tag="f_vst")

    sch.barrier()
    stk.close()

    stk = ExitStack()
    alloc = lambda *a, **k: stk.enter_context(nc.sbuf_tensor(*a, **k))
    palloc = lambda *a, **k: stk.enter_context(nc.psum_tensor(*a, **k))
    qp = alloc("f_qp", [128, 2, S], BF16)
    kp = alloc("f_kp", [128, 2, S], BF16)
    vp = alloc("f_vp", [128, 2, NT * 128], BF16)
    e_sb = alloc("f_e", [128, 2, 2, S], BF16)
    zparts = alloc("f_zp", [128, 2, 2, 4], F32)
    zrec = alloc("f_zr", [128, 2, 2], F32)
    vsc = alloc("f_vsc", [128, 2, 128], BF16)
    acc = alloc("f_acc", [128, 6, 512], F32)
    opair = alloc("f_opair", [128, 2, 512], BF16)

    psS = [palloc(f"f_psS{i}", [128, 1024], F32) for i in range(2)]
    psSS = [palloc(f"f_psSS{i}", [128, 512], F32) for i in range(2)]
    psO = [palloc(f"f_psO{i}", [128, 512], F32) for i in range(2)]

    def emit_pair_loads(p):
        b = p % 2
        bk = f"pb{b}"
        sch.op("sync", (lambda p=p, b=b: lambda s: s.dma_start(
            out=qp[:, b, :], in_=q_sp[p]))(), reads=["q_sp"],
            writes=[bk + "q"], dma=True, tag="f_qld")
        sch.op("sync", (lambda p=p, b=b: lambda s: s.dma_start(
            out=kp[:, b, :], in_=k_sp[p]))(), reads=["k_sp"],
            writes=[bk + "k"], dma=True, tag="f_kld")
        vsrc = bass.AP(v_sp, p * 128,
                       [[H * 65, 128], [128 * H * 65, NT], [1, 128]])
        sch.op("sync", (lambda b=b, vsrc=vsrc: lambda s: s.dma_start(
            out=vp[:, b, :].rearrange("p (t e) -> p t e", t=NT),
            in_=vsrc))(),
            reads=["v_sp"], writes=[bk + "v"], dma=True, tag="f_vld")

    emit_pair_loads(0)
    for p in range(NPAIR):
        b = p % 2
        bk = f"pb{b}"
        if p + 1 < NPAIR:
            emit_pair_loads(p + 1)

        sch.op("vector", lambda v: v.memset(acc[:, :, :], 0.0),
               writes=["facc"], tag="f_acc0")

        def make_av_parts(i):
            pa = i % 2
            ek = (f"e{pa}0", f"e{pa}1")

            def psum_part(i=i, pa=pa, ek=ek):
                def av_fn(te):
                    for c in range(2):
                        bank = psO[c]
                        for hh in range(2):
                            ins = te.matmul(
                                bank[64 * hh:64 * hh + 64, :],
                                vsc[:, pa, 64 * hh:64 * hh + 64],
                                e_sb[:, pa, hh, 512 * c:512 * c + 512],
                                start=(i == 0), stop=(i == NT - 1),
                                tile_position=(0, 64 * hh))
                    return ins
                sch.op("tensor", av_fn, reads=[f"vsc{pa}", *ek],
                       writes=[f"psO{c}" for c in range(2)], tag="f_av")

            def scratch_part(cs, i=i, pa=pa, ek=ek):
                for c in cs:
                    sb = c % 2
                    def av2_fn(te, c=c, sb=sb):
                        for hh in range(2):
                            ins = te.matmul(
                                psSS[sb][64 * hh:64 * hh + 64, :],
                                vsc[:, pa, 64 * hh:64 * hh + 64],
                                e_sb[:, pa, hh, 512 * c:512 * c + 512],
                                start=True, stop=True,
                                tile_position=(0, 64 * hh))
                        return ins
                    sch.op("tensor", av2_fn, reads=[f"vsc{pa}", *ek],
                           writes=[f"psSS{sb}"], tag="f_av2")
                    sch.op("vector", (lambda c=c, sb=sb: lambda v:
                           v.tensor_add(acc[:, c - 2, :], acc[:, c - 2, :],
                                        psSS[sb][:, :]))(),
                        reads=[f"psSS{sb}", "facc"], writes=["facc"],
                        tag="f_av3")

            return [lambda: psum_part(),
                    lambda: scratch_part((2, 3)),
                    lambda: scratch_part((4, 5)),
                    lambda: scratch_part((6, 7))]

        pending = []
        for i in range(NT):
            pa = i % 2
            ek = (f"e{pa}0", f"e{pa}1")
            for j in range(4):
                for hh in range(2):
                    def sc2_fn(te, i=i, j=j, b=b, hh=hh):
                        for q2 in range(2):
                            ins = te.matmul(
                                psS[hh][:, q2 * 512:(q2 + 1) * 512],
                                kp[64 * hh:64 + 64 * hh, b,
                                   128 * i:128 * i + 128],
                                qp[64 * hh:64 + 64 * hh, b,
                                   1024 * j + 512 * q2:1024 * j + 512 * q2 + 512],
                                start=True, stop=True,
                                tile_position=(64 * hh, 0))
                        return ins
                    sch.op("tensor", sc2_fn,
                           reads=[bk + "q", bk + "k"], writes=[f"psS{hh}"],
                           tag="f_sc")
                    sch.op("scalar", (lambda j=j, hh=hh, pa=pa:
                           lambda se: se.activation(
                               e_sb[:, pa, hh, 1024 * j:1024 * j + 1024],
                               psS[hh][:, :], AF.Exp, scale=1.0 / 32.0,
                               accum_out=zparts[:, pa, hh, j:j + 1]))(),
                           reads=[f"psS{hh}"], writes=[ek[hh], f"zp{pa}"],
                           tag="f_exp")
                if pending:
                    pending.pop(0)()
            def z_fn(v, pa=pa):
                v.reduce_sum(zrec[:, pa, 0:1], zparts[:, pa, 0, 0:4], AX.X)
                return v.reduce_sum(zrec[:, pa, 1:2], zparts[:, pa, 1, 0:4],
                                    AX.X)
            sch.op("vector", z_fn, reads=[f"zp{pa}"], writes=[f"zrec{pa}"],
                   tag="f_z")
            sch.op("vector", (lambda pa=pa: lambda v: v.reciprocal(
                zrec[:, pa, :], zrec[:, pa, :]))(),
                reads=[f"zrec{pa}"], writes=[f"zrec{pa}"], tag="f_zr")
            def vs_fn(v, i=i, b=b, pa=pa):
                v.tensor_scalar_mul(vsc[:, pa, 0:64],
                                    vp[:, b, 128 * i:128 * i + 64],
                                    zrec[:, pa, 0:1])
                return v.tensor_scalar_mul(
                    vsc[:, pa, 64:128],
                    vp[:, b, 128 * i + 64:128 * i + 128],
                    zrec[:, pa, 1:2])
            sch.op("vector", vs_fn, reads=[f"zrec{pa}", bk + "v"],
                   writes=[f"vsc{pa}"], tag="f_vs")
            assert not pending
            pending = make_av_parts(i)
        for part in pending:
            part()
        for c in range(2):
            def ev_fn(v, c=c):
                return v.tensor_copy(
                    opair[:, c % 2, 0:512], psO[c][:, :])
            sch.op("vector", ev_fn, reads=[f"psO{c}"],
                   writes=[f"op{c % 2}"], tag="f_ev")
            sch.op("sync", (lambda p=p, c=c: lambda s: s.dma_start(
                out=o_sp[p][:, 512 * c:512 * c + 512],
                in_=opair[:, c % 2, 0:512]))(),
                reads=[f"op{c % 2}"], writes=["o_sp"], dma=True, tag="f_est")
        for c in range(2, 8):
            def ev2_fn(v, c=c):
                return v.tensor_copy(opair[:, c % 2, 0:512],
                                     acc[:, c - 2, :])
            sch.op("vector", ev2_fn, reads=["facc"], writes=[f"op{c % 2}"],
                   tag="f_ev2")
            sch.op("sync", (lambda p=p, c=c: lambda s: s.dma_start(
                out=o_sp[p][:, 512 * c:512 * c + 512],
                in_=opair[:, c % 2, 0:512]))(),
                reads=[f"op{c % 2}"], writes=["o_sp"], dma=True, tag="f_est2")

    sch.barrier()
    stk.close()

    stk = ExitStack()
    alloc = lambda *a, **k: stk.enter_context(nc.sbuf_tensor(*a, **k))
    palloc = lambda *a, **k: stk.enter_context(nc.psum_tensor(*a, **k))
    wo_sb = alloc("f_wo", [128, NC, D], BF16)
    wf_sb = alloc("f_wf", [128, NC, D], BF16)
    ofm = alloc("f_ofm", [128, 2, NC, 512], BF16)
    xt2 = alloc("f_xt2", [128, 2, D], F32)
    zb = alloc("f_zb", [128, 2, D], BF16)
    z2q = alloc("f_z2q", [128, 2, 2, 512], F32)
    nrm1 = alloc("f_nrm1", [128, 2, 4], F32)
    z1f = alloc("f_z1f", [128, 2, NC, 128], BF16)
    gout = alloc("f_gout", [128, 2, 2, 512], F32)
    psT2 = palloc("f_psT2", [128, 1024], BF16)
    psZ = [palloc(f"f_psZ{i}", [128, 512], F32) for i in range(4)]
    psW = [palloc(f"f_psW{i}", [128, 512], F32) for i in range(2)]

    for i in range(NC):
        sch.op("gpsimd", (lambda i=i: lambda g: g.dma_start(
            out=wo_sb[:, i, :], in_=T["Wo1f"][i * 128:(i + 1) * 128, :]))(),
            reads=[T["Wo1f"]], writes=["fwo"], dma=True, tag="f_wl2")
        sch.op("gpsimd", (lambda i=i: lambda g: g.dma_start(
            out=wf_sb[:, i, :], in_=T["Wff"][i * 128:(i + 1) * 128, :]))(),
            reads=[T["Wff"]], writes=["fwf"], dma=True, tag="f_wl2")

    for c8 in range(8):
        b = c8 % 2
        for p in range(NPAIR):
            sch.op("sync", (lambda p=p, c8=c8, b=b: lambda s: s.dma_start(
                out=ofm[:, b, p, :], in_=o_sp[p][:, c8 * 512:(c8 + 1) * 512]))(),
                reads=["o_sp", f"ofm{b}"], writes=[f"ofm{b}"], dma=True,
                tag="f_old")
        for j in range(4):
            t = c8 * 4 + j
            jb = j % 2
            zZ = [psZ[2 * jb], psZ[2 * jb + 1]]
            zZk = [f"psZ{2 * jb}", f"psZ{2 * jb + 1}"]
            sch.op("sync", (lambda t=t, jb=jb: lambda s: s.dma_start(
                out=xt2[:, jb, :], in_=x_in[t]))(),
                reads=[xin_name], writes=[f"x2t{jb}"], dma=True, tag="f_xld")
            for oc in range(2):
                def z_mm(te, j=j, oc=oc, b=b, zZ=zZ):
                    for pi in range(NPAIR):
                        ins = te.matmul(
                            zZ[oc][:, :], ofm[:, b, pi, j * 128:(j + 1) * 128],
                            wo_sb[:, pi, oc * 512:(oc + 1) * 512],
                            start=(pi == 0), stop=(pi == NPAIR - 1))
                    return ins
                sch.op("tensor", z_mm, reads=[f"ofm{b}", "fwo"],
                       writes=[zZk[oc]], tag="f_zmm")
            def sq_fn(sc, jb=jb, zZ=zZ):
                sc.activation(z2q[:, jb, 0, :], zZ[0][:, :], AF.Square)
                return sc.activation(z2q[:, jb, 1, :], zZ[1][:, :], AF.Square)
            sch.op("scalar", sq_fn, reads=zZk, writes=[f"z2q{jb}"],
                   tag="f_sq")
            sch.op("vector", (lambda jb=jb: lambda v: v.reduce_sum(
                nrm1[:, jb, 0:1],
                z2q[:, jb, :, :].rearrange("p a f -> p (a f)"), AX.X))(),
                   reads=[f"z2q{jb}"], writes=[f"fn1{jb}"], tag="f_n1")
            sch.op("scalar", (lambda jb=jb: lambda sc: sc.activation(
                nrm1[:, jb, 1:2], nrm1[:, jb, 0:1], AF.Sqrt))(),
                   reads=[f"fn1{jb}"], writes=[f"fn2{jb}"], tag="f_n2")
            sch.op("vector", (lambda jb=jb: lambda v: v.tensor_scalar_max(
                nrm1[:, jb, 1:2], nrm1[:, jb, 1:2], 1e-12))(),
                reads=[f"fn2{jb}"], writes=[f"fn2{jb}"], tag="f_n3")
            sch.op("vector", (lambda jb=jb: lambda v: v.reciprocal(
                nrm1[:, jb, 2:3], nrm1[:, jb, 1:2]))(),
                   reads=[f"fn2{jb}"], writes=[f"fn3{jb}"], tag="f_n4")
            def z1_fn(sc, jb=jb, zZ=zZ):
                sc.activation(zb[:, jb, 0:512], zZ[0][:, :], AF.Identity,
                              scale=nrm1[:, jb, 2:3])
                return sc.activation(zb[:, jb, 512:1024], zZ[1][:, :],
                                     AF.Identity, scale=nrm1[:, jb, 2:3])
            sch.op("scalar", z1_fn, reads=[*zZk, f"fn3{jb}"],
                   writes=[f"zb{jb}"], tag="f_z1")
            def tr_fn(te, jb=jb):
                for q in range(NC):
                    ins = te.transpose(psT2[:, q * 128:(q + 1) * 128],
                                       zb[:, jb, q * 128:(q + 1) * 128],
                                       st["ident"][:, :])
                return ins
            sch.op("tensor", tr_fn, reads=[f"zb{jb}", "ident"],
                   writes=["psT2"], tag="f_tr")
            sch.op("vector", (lambda jb=jb: lambda v: v.tensor_copy(
                z1f[:, jb, :, :],
                psT2[:, :].rearrange("p (c f) -> p c f", c=NC)))(),
                reads=["psT2"], writes=[f"z1f{jb}"], tag="f_trev")
            for oc in range(2):
                def w_mm(te, oc=oc, jb=jb):
                    for i in range(NC):
                        ins = te.matmul(
                            psW[oc][:, :], z1f[:, jb, i, :],
                            wf_sb[:, i, oc * 512:(oc + 1) * 512],
                            start=(i == 0), stop=(i == NC - 1))
                    return ins
                sch.op("tensor", w_mm, reads=[f"z1f{jb}", "fwf"],
                       writes=[f"psW{oc}"], tag="f_wmm")
            def sq2_fn(sc, jb=jb):
                sc.activation(z2q[:, jb, 0, :], psW[0][:, :], AF.Square)
                return sc.activation(z2q[:, jb, 1, :], psW[1][:, :], AF.Square)
            sch.op("scalar", sq2_fn, reads=["psW0", "psW1"],
                   writes=[f"z2q{jb}"], tag="f_sq2")
            sch.op("vector", (lambda jb=jb: lambda v: v.reduce_sum(
                nrm1[:, jb, 0:1],
                z2q[:, jb, :, :].rearrange("p a f -> p (a f)"), AX.X))(),
                   reads=[f"z2q{jb}"], writes=[f"fn1{jb}"], tag="f_n5")
            sch.op("scalar", (lambda jb=jb: lambda sc: sc.activation(
                nrm1[:, jb, 1:2], nrm1[:, jb, 0:1], AF.Sqrt))(),
                   reads=[f"fn1{jb}"], writes=[f"fn2{jb}"], tag="f_n6")
            sch.op("vector", (lambda jb=jb: lambda v: v.tensor_scalar_max(
                nrm1[:, jb, 1:2], nrm1[:, jb, 1:2], 1e-12))(),
                reads=[f"fn2{jb}"], writes=[f"fn2{jb}"], tag="f_n7")
            sch.op("vector", (lambda jb=jb: lambda v: v.reciprocal(
                nrm1[:, jb, 3:4], nrm1[:, jb, 1:2]))(),
                   reads=[f"fn2{jb}"], writes=[f"fn4{jb}"], tag="f_n8")
            def gel_fn(sc, jb=jb):
                sc.activation(gout[:, jb, 0, :], psW[0][:, :], AF.Gelu,
                              scale=nrm1[:, jb, 3:4])
                return sc.activation(gout[:, jb, 1, :], psW[1][:, :], AF.Gelu,
                                     scale=nrm1[:, jb, 3:4])
            sch.op("scalar", gel_fn, reads=["psW0", "psW1", f"fn4{jb}"],
                   writes=[f"fgout{jb}"], tag="f_gel")
            for oc in range(2):
                sch.op("vector", (lambda oc=oc, jb=jb: lambda v:
                       v.tensor_add(z2q[:, jb, oc, 0:512], gout[:, jb, oc, :],
                                    xt2[:, jb, oc * 512:(oc + 1) * 512]))(),
                    reads=[f"fgout{jb}", f"x2t{jb}"], writes=[f"z2q{jb}"],
                    tag="f_res")
            sch.op("sync", (lambda t=t, jb=jb: lambda s: s.dma_start(
                out=xout[t],
                in_=z2q[:, jb, :, 0:512].rearrange("p a f -> p (a f)")))(),
                reads=[f"z2q{jb}"], writes=[xout_name], dma=True, tag="f_ost")

    sch.barrier()
    stk.close()



def _prep_inputs(inputs):
    f32c = lambda a: np.ascontiguousarray(a, dtype=np.float32)
    shared = {
        "wqkvT": f32c(inputs["L0_attn_w_qkv"].T),
        "wprojT": f32c(inputs["L0_attn_w_proj"].T),
        "U1": f32c(inputs["L0_ffn_U1"]),
        "V1": f32c(inputs["L0_ffn_V1"]),
        "u2r": f32c(inputs["L0_ffn_U2"]).reshape(32, 128, 256),
        "V2": f32c(inputs["L0_ffn_V2"]),
        "b1": f32c(inputs["L0_ffn_b1"]).reshape(1, 4 * D),
        "b2": f32c(inputs["L0_ffn_b2"]).reshape(1, D),
        "ppT": f32c(inputs["L1_attn_pos_phases"][:S].T),
        "Wpv": f32c(inputs["L1_attn_Wpv"]),
        "Wk1a": f32c(inputs["L1_attn_Wk"]),
        "Wkv": f32c(inputs["L1_attn_Wkv"]),
        "Wg1": f32c(inputs["L1_attn_Wg1"]),
        "Wb1": f32c(inputs["L1_attn_Wb1"]),
        "wb2r": f32c(inputs["L1_attn_Wb2"]).reshape(4, 128, 2),
        "wg2c": f32c(inputs["L1_attn_Wg2"].reshape(NC, 128).T),
        "lngc": f32c(inputs["L1_attn_ln_g"].reshape(NC, 128).T),
        "lnbc": f32c(inputs["L1_attn_ln_b"].reshape(NC, 128).T),
        "Wo1a": f32c(inputs["L1_attn_Wo"]),
        "bpv": f32c(inputs["L1_attn_bpv"]).reshape(1, D),
        "bk1a": f32c(inputs["L1_attn_bk"]).reshape(1, D),
        "bkv": f32c(inputs["L1_attn_bkv"]).reshape(1, D),
        "bg1": f32c(inputs["L1_attn_bg1"]).reshape(1, D),
        "bg2": f32c(inputs["L1_attn_bg2"]).reshape(1, 1),
        "bb1": f32c(inputs["L1_attn_bb1"]).reshape(1, 512),
        "bb2": f32c(inputs["L1_attn_bb2"]).reshape(1, 2),
        "bo1a": f32c(inputs["L1_attn_bo"]).reshape(1, D),
        "Wq1f": f32c(inputs["L1_ffn_Wq"]),
        "Wk1f": f32c(inputs["L1_ffn_Wk"]),
        "Wv1f": f32c(inputs["L1_ffn_Wv"]),
        "Wo1f": f32c(inputs["L1_ffn_Wo"]),
        "Wff": f32c(inputs["L1_ffn_Wff"]),
    }
    x = inputs["L0_attn_x"]
    in_maps = []
    for c in range(8):
        m = dict(shared)
        m["x_in"] = f32c(x[c]).reshape(NT, 128, D)
        in_maps.append(m)
    return in_maps


def kernel(**inputs):
    nc = build_program()
    in_maps = _prep_inputs(inputs)
    res = run_bass_kernel_spmd(nc, in_maps, core_ids=list(range(8)))
    out = np.stack([r["xout"].reshape(S, D) for r in res.results], axis=0)
    return out


# version 31 (final)
# speedup vs baseline: 1.0192x; 1.0192x over previous
import math
import numpy as np
import concourse.bass as bass
import concourse.mybir as mybir
from concourse.bass_utils import run_bass_kernel_spmd

F32 = mybir.dt.float32
BF16 = mybir.dt.bfloat16
AF = mybir.ActivationFunctionType
ALU = mybir.AluOpType
AX = mybir.AxisListType

S = 4096
D = 1024
NT = S // 128
NC = D // 128
H = 16
HD = 64
NPAIR = 8

PHASES = ("l0a", "l0f", "l1a", "l1f")
DUMP_INTERNALS = False



class _Op:
    __slots__ = ("eng", "fn", "dma", "deps", "idx", "need_inc", "sem", "val",
                 "inc", "waits", "tag", "slot")

    def __init__(self, eng, fn, dma, idx, tag):
        self.eng = eng
        self.fn = fn
        self.dma = dma
        self.idx = idx
        self.deps = set()
        self.need_inc = False
        self.sem = None
        self.val = 0
        self.inc = 16 if dma else 1
        self.waits = []
        self.tag = tag


def _key(x):
    if isinstance(x, str):
        return x
    if isinstance(x, bass.AP):
        return x.tensor.name
    return x.name


class Sched:
    ENGINES = ("tensor", "scalar", "vector", "gpsimd", "sync")

    def __init__(self, nc):
        self.nc = nc
        self.ops = []
        self.last_writer = {}
        self.readers = {}
        self.epoch = 0
        self.epoch_of = []

    def op(self, eng, fn, reads=(), writes=(), dma=False, tag=""):
        o = _Op(eng, fn, dma, len(self.ops), tag)
        raw_waw = set()
        war = set()
        for k in reads:
            k = _key(k)
            w = self.last_writer.get(k)
            if w is not None:
                raw_waw.add(w)
        for k in writes:
            k = _key(k)
            w = self.last_writer.get(k)
            if w is not None:
                raw_waw.add(w)
            for r in self.readers.get(k, ()):
                war.add(r)
        def keep(d):
            if d.dma or d.eng != eng:
                return True
            return eng not in ("tensor", "sync")
        o.deps = {d for d in (raw_waw | war) if keep(d)}
        for k in reads:
            self.readers.setdefault(_key(k), []).append(o)
        for k in writes:
            k = _key(k)
            self.last_writer[k] = o
            self.readers[k] = []
        self.ops.append(o)
        self.epoch_of.append(self.epoch)
        return o

    def barrier(self):
        tails = {}
        for o in self.ops:
            if self.epoch_of[o.idx] != self.epoch:
                continue
            tails[(o.eng, o.dma)] = o
        deps = set(tails.values())
        for eng in self.ENGINES:
            mydeps = {d for d in deps if d.dma or d.eng != eng}
            o = _Op(eng, lambda e: e.nop(), False, len(self.ops), "barrier")
            o.deps = mydeps
            self.ops.append(o)
            self.epoch_of.append(self.epoch)
        self.last_writer = {}
        self.readers = {}
        self.epoch += 1

    DMA_SLOTS = 24

    def finalize(self):
        nc = self.nc
        for o in self.ops:
            if o.dma:
                o.need_inc = True
            for d in o.deps:
                d.need_inc = True
        waiters = {}
        for o in self.ops:
            for d in o.deps:
                waiters.setdefault(d.idx, []).append(o)
        slot_last = {}
        dma_count = {}
        for o in self.ops:
            if not o.dma:
                continue
            nslot = self.DMA_SLOTS
            base = dma_count.get(o.eng, 0)
            chosen = None
            for k in range(nslot):
                s = (base + k) % nslot
                prev = slot_last.get((o.eng, s))
                if prev is None or all(w.idx < o.idx
                                       for w in waiters.get(prev.idx, ())):
                    chosen = s
                    if prev is not None:
                        for w in waiters.get(prev.idx, ()):
                            if w is not o:
                                o.deps.add(w)
                    break
            assert chosen is not None, f"no free dma slot for op {o.idx}"
            o.slot = (o.eng, chosen)
            slot_last[o.slot] = o
            dma_count[o.eng] = base + 1
            for d in o.deps:
                d.need_inc = True
        for o in self.ops:
            if o.tag != "barrier":
                continue
            for key, last in slot_last.items():
                if last.idx < o.idx and (last.dma or last.eng != o.eng):
                    o.deps.add(last)
                    last.need_inc = True
        sems = {}
        counters = {}

        def get_sem(key, name):
            if key not in sems:
                sems[key] = nc.alloc_semaphore(name)
                counters[key] = 0
            return key

        for o in self.ops:
            if o.need_inc:
                if o.dma:
                    k = get_sem(o.slot, f"d_{o.slot[0]}_{o.slot[1]}")
                else:
                    k = get_sem(o.eng, f"c_{o.eng}")
                counters[k] += o.inc
                o.sem = sems[k]
                o.val = counters[k]
        for k, v in counters.items():
            assert v < 2 ** 31, (k, v)
        marks = {e: {} for e in self.ENGINES}
        for o in self.ops:
            m = marks[o.eng]
            for d in sorted(o.deps, key=lambda d: d.idx):
                assert d.idx < o.idx
                assert d.sem is not None
                if m.get(d.sem.num, -1) >= d.val:
                    continue
                o.waits.append((d.sem, d.val))
                m[d.sem.num] = d.val
        by_eng = {e: [] for e in self.ENGINES}
        for o in self.ops:
            by_eng[o.eng].append(o)
        with nc.Block() as block:
            def make(eng_name):
                def run(engine):
                    for o in by_eng[eng_name]:
                        for sem, val in o.waits:
                            engine.wait_ge(sem, val)
                        ins = o.fn(engine)
                        if o.need_inc:
                            ins.then_inc(o.sem, o.inc)
                return run
            block.tensor(make("tensor"))
            block.scalar(make("scalar"))
            block.vector(make("vector"))
            block.gpsimd(make("gpsimd"))
            block.sync(make("sync"))



def _triu_ones(n):
    return np.triu(np.ones((n, n), np.float32))


def build_program(phases=None):
    phases = PHASES if phases is None else phases
    nc = bass.Bass("TRN2", target_bir_lowering=False)
    sch = Sched(nc)

    T = {}

    def din(name, shape, dtype=F32):
        T[name] = nc.dram_tensor(name, list(shape), dtype, kind="ExternalInput")
        return T[name]

    def dint(name, shape, dtype=F32):
        kind = "ExternalOutput" if DUMP_INTERNALS else "Internal"
        T[name] = nc.dram_tensor(name, list(shape), dtype, kind=kind)
        return T[name]

    din("x_in", [NT, 128, D])
    xout = nc.dram_tensor("xout", [NT, 128, D], F32, kind="ExternalOutput")
    T["xout"] = xout

    din("wqkvT", [D, 3 * D])
    din("wprojT", [D, D])
    din("U1", [D, 256])
    din("V1", [256, 4 * D])
    din("u2r", [32, 128, 256])
    din("V2", [256, D])
    din("b1", [1, 4 * D])
    din("b2", [1, D])
    din("ppT", [D, S])
    din("Wpv", [D, D])
    din("Wk1a", [D, D])
    din("Wkv", [D, D])
    din("Wg1", [2 * D, D])
    din("Wb1", [D, 512])
    din("wb2r", [4, 128, 2])
    din("wg2c", [128, NC])
    din("lngc", [128, NC])
    din("lnbc", [128, NC])
    din("Wo1a", [D, D])
    din("Wq1f", [D, D])
    din("Wk1f", [D, D])
    din("Wv1f", [D, D])
    din("Wo1f", [D, D])
    din("Wff", [D, D])
    din("bpv", [1, D])
    din("bk1a", [1, D])
    din("bkv", [1, D])
    din("bg1", [1, D])
    din("bg2", [1, 1])
    din("bb1", [1, 512])
    din("bb2", [1, 2])
    din("bo1a", [1, D])

    dint("pc_sp", [NC, 128, S], BF16)
    dint("ps_sp", [NC, 128, S], BF16)
    dint("q_sp", [NPAIR, 128, S], BF16)
    dint("k_sp", [NPAIR, 128, S], BF16)
    dint("v_sp", [NT, 128, H * 65], BF16)
    dint("o_sp", [NPAIR, 128, S], BF16)
    dint("x1", [NT, 128, D])
    dint("x2", [NT, 128, D])
    dint("x3", [NT, 128, D])

    tri = nc.inline_tensor(_triu_ones(128).astype(np.float32), name="c_triu")
    T["c_triu"] = tri

    alloc = nc.alloc_sbuf_tensor

    epsT = alloc("epsT", [128, 1], F32)
    ident = alloc("ident", [128, 128], BF16)
    mask_triu = alloc("mask_triu", [128, 128], BF16)
    ones64 = alloc("ones64", [1, 64], F32)

    ones1 = alloc("ones1", [1, 128], BF16)
    onesrow = alloc("onesrow", [1, 512], BF16)
    sch.op("vector", lambda v: v.memset(epsT[:, :], 1e-5), writes=["epsT"])
    sch.op("gpsimd", lambda g: g.dma_start(out=mask_triu[:, :], in_=tri[:, :]),
           writes=["mask_triu"], dma=True)
    sch.op("vector", lambda v: v.memset(ones64[:, :], 1.0), writes=["ones64"])
    sch.op("vector", lambda v: v.memset(ones1[:, :], 1.0), writes=["ones1"])
    sch.op("vector", lambda v: v.memset(onesrow[:, :], 1.0), writes=["onesrow"])
    sch.op("gpsimd", lambda g: g.memset(ident[:, :], 1.0), writes=["ident"])
    sch.op("gpsimd", lambda g: g.affine_select(
        out=ident[:, :], in_=ident[:, :], compare_op=ALU.is_ge,
        fill=0.0, base=0, pattern=[[-1, 128]], channel_multiplier=1),
        reads=["ident"], writes=["ident"])
    sch.op("gpsimd", lambda g: g.affine_select(
        out=ident[:, :], in_=ident[:, :], compare_op=ALU.is_ge, fill=0.0,
        base=0, pattern=[[1, 128]], channel_multiplier=-1),
        reads=["ident"], writes=["ident"])

    state = dict(nc=nc, sch=sch, T=T, epsT=epsT, ident=ident,
                 mask_triu=mask_triu, ones64=ones64, ones1=ones1,
                 onesrow=onesrow)

    xcur = "x_in"
    if "l0a" in phases:
        emit_l0a(state, xcur, "x1")
        xcur = "x1"
    if "l0f" in phases:
        emit_l0f(state, xcur, "x2")
        xcur = "x2"
    if "l1a" in phases:
        emit_l1a(state, xcur, "x3")
        xcur = "x3"
    if "l1f" in phases:
        emit_l1f(state, xcur, "xout")
        xcur = "xout"

    if xcur != "xout":
        emit_copyout(state, xcur)
    sch.barrier()
    sch.finalize()
    return nc


def emit_copyout(st, xcur):
    nc, sch, T = st["nc"], st["sch"], st["T"]
    if xcur == "x_in":
        src = T["x_in"]
    else:
        src = T[xcur]
    xout = T["xout"]
    buf = nc.alloc_sbuf_tensor("cp_buf", [128, 4 * D], F32)
    for t0 in range(0, NT, 4):
        for j in range(4):
            t = t0 + j
            sch.op("sync", (lambda t=t, j=j: lambda s: s.dma_start(
                out=buf[:, j * D:(j + 1) * D], in_=src[t]))(),
                reads=[xcur], writes=[f"cp{j}"], dma=True, tag="cpin")
        for j in range(4):
            t = t0 + j
            sch.op("sync", (lambda t=t, j=j: lambda s: s.dma_start(
                out=xout[t], in_=buf[:, j * D:(j + 1) * D]))(),
                reads=[f"cp{j}"], writes=["xout"], dma=True, tag="cpout")



def emit_ln_chunk(st, xsrc_name, xsrc, c8, xt, stats, lnt, psT, ln_ch,
                  ln_col0=0, ln_keys=None, xt_extra_reads=(), nslot=2,
                  units=None):
    nc, sch = st["nc"], st["sch"]
    epsT, ident = st["epsT"], st["ident"]

    def tile(j):
        t = c8 * 4 + j
        xbuf = xt[:, j % nslot, :]
        kx = f"xt{j % nslot}"
        sch.op("sync", (lambda t=t, xbuf=xbuf: lambda s: s.dma_start(
            out=xbuf, in_=xsrc[t]))(),
            reads=[xsrc_name, *xt_extra_reads], writes=[kx], dma=True, tag="lnx")


        def bn_fn(v, xbuf=xbuf):
            v.bn_stats(stats[:, 0:6], xbuf[:, 0:512])
            return v.bn_stats(stats[:, 6:12], xbuf[:, 512:1024])
        sch.op("vector", bn_fn, reads=[kx], writes=["st_bn"], tag="lnbn")
        sch.op("vector", lambda v: v.bn_aggr(stats[:, 12:14], stats[:, 0:12]),
               reads=["st_bn"], writes=["st_mv"], tag="lnaggr")
        sch.op("scalar", lambda sc: sc.activation(
            stats[:, 14:15], stats[:, 13:14], AF.Sqrt, bias=epsT[:, :]),
            reads=["st_mv", "epsT"], writes=["st_sd"], tag="lnsqrt")
        sch.op("vector", lambda v: v.reciprocal(stats[:, 15:16],
                                                stats[:, 14:15]),
               reads=["st_sd"], writes=["st_rstd"], tag="lnrecip")
        sch.op("vector", lambda v: v.scalar_tensor_tensor(
            stats[:, 16:17], stats[:, 12:13], -1.0, stats[:, 15:16],
            ALU.mult, ALU.mult),
            reads=["st_mv", "st_rstd"], writes=["st_nm"], tag="lnnm")
        sch.op("scalar", (lambda xbuf=xbuf: lambda sc: sc.activation(
            lnt[:, :], xbuf, AF.Identity,
            bias=stats[:, 16:17], scale=stats[:, 15:16]))(),
            reads=["st_rstd", "st_nm", kx], writes=["lnt"], tag="lnapply")

        def tr_fn(te):
            for q in range(NC):
                ins = te.transpose(psT[:, q * 128:(q + 1) * 128],
                                   lnt[:, q * 128:(q + 1) * 128], ident[:, :])
            return ins
        sch.op("tensor", tr_fn, reads=["lnt", "ident"], writes=["psT"],
               tag="lntr")
        col = ln_col0 + 128 * j
        lk = ln_keys if ln_keys is not None else ["ln_ch"]
        sch.op("vector", (lambda col=col: lambda v: v.tensor_copy(
            ln_ch[:, :, col:col + 128],
            psT[:, :].rearrange("p (c f) -> p c f", c=NC)))(),
            reads=["psT"], writes=lk, tag="lnevac")

    for j in range(4):
        if units is None:
            tile(j)
        else:
            units.append(lambda j=j: tile(j))



def make_pt_units(st, alloc):
    nc, sch, T = st["nc"], st["sch"], st["T"]
    ppb = alloc("pt_pp", [128, 2, 1024], F32)
    ptA = alloc("pt_A", [128, 2, 1024], F32)
    pcb = alloc("pt_pc", [128, 2, 1024], BF16)
    psb = alloc("pt_ps", [128, 2, 1024], BF16)
    pc_sp, ps_sp = T["pc_sp"], T["ps_sp"]

    def pt_unit(u):
        Dc, Q = u // 4, u % 4
        W = 1024 * Q
        p = u % 2
        sch.op("gpsimd", (lambda Dc=Dc, W=W, p=p: lambda g: g.dma_start(
            out=ppb[:, p, :],
            in_=T["ppT"][Dc * 128:(Dc + 1) * 128, W:W + 1024]))(),
            reads=[T["ppT"]], writes=[f"ptpp{p}"], dma=True, tag="pt_l")
        sch.op("gpsimd", (lambda p=p: lambda g: g.tensor_scalar(
            ptA[:, p, :], ppb[:, p, :], 1.0 / TWO_PI, MAGIC,
            ALU.mult, ALU.add))(),
            reads=[f"ptpp{p}"], writes=[f"ptA{p}"], tag="pt_rr1")
        sch.op("gpsimd", (lambda p=p: lambda g: g.tensor_scalar(
            ptA[:, p, :], ptA[:, p, :], -MAGIC, -TWO_PI,
            ALU.add, ALU.mult))(),
            reads=[f"ptA{p}"], writes=[f"ptA{p}"], tag="pt_rr2")
        sch.op("gpsimd", (lambda p=p: lambda g: g.tensor_tensor(
            ptA[:, p, :], ptA[:, p, :], ppb[:, p, :], ALU.add))(),
            reads=[f"ptA{p}", f"ptpp{p}"], writes=[f"ptA{p}"], tag="pt_rr3")
        sch.op("scalar", (lambda p=p: lambda sc: sc.activation(
            psb[:, p, :], ptA[:, p, :], AF.Sin))(),
            reads=[f"ptA{p}"], writes=[f"ptps{p}"], tag="pt_sin")
        sch.op("scalar", (lambda p=p: lambda sc: sc.activation(
            ppb[:, p, :], ptA[:, p, :], AF.Sin, scale=0.5))(),
            reads=[f"ptA{p}"], writes=[f"ptpp{p}"], tag="pt_sin2")
        sch.op("scalar", (lambda p=p: lambda sc: sc.activation(
            ppb[:, p, :], ppb[:, p, :], AF.Square))(),
            reads=[f"ptpp{p}"], writes=[f"ptpp{p}"], tag="pt_sq")
        sch.op("gpsimd", (lambda p=p: lambda g: g.tensor_scalar(
            pcb[:, p, :], ppb[:, p, :], -2.0, 1.0, ALU.mult, ALU.add))(),
            reads=[f"ptpp{p}"], writes=[f"ptpc{p}"], tag="pt_cos")
        sch.op("gpsimd", (lambda Dc=Dc, W=W, p=p: lambda g: g.dma_start(
            out=pc_sp[Dc][:, W:W + 1024], in_=pcb[:, p, :]))(),
            reads=[f"ptpc{p}"], writes=["pcsp"], dma=True, tag="pt_st")
        sch.op("gpsimd", (lambda Dc=Dc, W=W, p=p: lambda g: g.dma_start(
            out=ps_sp[Dc][:, W:W + 1024], in_=psb[:, p, :]))(),
            reads=[f"ptps{p}"], writes=["pssp"], dma=True, tag="pt_st")

    return [(lambda u=u: pt_unit(u)) for u in range(32)]


def emit_l0a(st, xin_name, xout_name):
    from contextlib import ExitStack
    nc, sch, T = st["nc"], st["sch"], st["T"]
    x_in, x1 = T[xin_name], T[xout_name]
    wqkvT, wprojT = T["wqkvT"], T["wprojT"]
    q_sp, k_sp, v_sp, o_sp = T["q_sp"], T["k_sp"], T["v_sp"], T["o_sp"]
    mask_triu, ones64 = st["mask_triu"], st["ones64"]

    stk = ExitStack()
    alloc = lambda *a, **k: stk.enter_context(nc.sbuf_tensor(*a, **k))
    palloc = lambda *a, **k: stk.enter_context(nc.psum_tensor(*a, **k))
    wq_sb = alloc("l0a_wq", [128, NC * 3 * D], BF16)
    xt = alloc("l0a_xt", [128, 2, D], F32)
    stats = alloc("l0a_stats", [128, 18], F32)
    lnt = alloc("l0a_lnt", [128, D], BF16)
    ln_ch = alloc("l0a_lnch", [128, NC, 512], BF16)
    qkv_e = alloc("l0a_qkve", [128, 2 * 512], BF16)
    vsb = alloc("l0a_vsb", [128, 2, H * 65], BF16)

    psT = palloc("l0a_psT", [128, 1024], BF16)
    psA = palloc("l0a_psA", [128, 512], F32)
    psB = palloc("l0a_psB", [128, 512], F32)

    def vones(v):
        for h in range(H):
            for b in range(2):
                ins = v.memset(vsb[:, b, h * 65 + 64:h * 65 + 65], 1.0)
        return ins
    sch.op("vector", vones, writes=["vsb0", "vsb1"], tag="vones")

    for i in range(NC):
        sch.op("gpsimd", (lambda i=i: lambda g: g.dma_start(
            out=wq_sb[:, i * 3 * D:(i + 1) * 3 * D], in_=wqkvT[i * 128:(i + 1) * 128, :]))(),
            reads=["wqkvT"], writes=["wq_sb"], dma=True, tag="l0a_wload")

    pt_units = make_pt_units(st, alloc)
    for c8 in range(8):
        emit_ln_chunk(st, xin_name, x_in, c8, xt, stats, lnt, psT, ln_ch)
        for u in pt_units[c8 * 4:(c8 + 1) * 4]:
            u()
        for oc in range(16):
            ps = psA if oc % 2 == 0 else psB
            pk = "psA" if oc % 2 == 0 else "psB"

            def qk_fn(te, oc=oc, ps=ps):
                for i in range(NC):
                    ins = te.matmul(ps[:, :],
                                    wq_sb[:, i * 3 * D + oc * 128: i * 3 * D + (oc + 1) * 128],
                                    ln_ch[:, i, :],
                                    start=(i == 0), stop=(i == NC - 1))
                return ins
            sch.op("tensor", qk_fn, reads=["wq_sb", "ln_ch"], writes=[pk],
                   tag="l0a_qk")
            eb = qkv_e[:, (oc % 2) * 512:(oc % 2 + 1) * 512]
            ek = f"qkve{oc % 2}"
            sch.op("vector", (lambda ps=ps, eb=eb: lambda v: v.tensor_copy(
                eb, ps[:, :]))(), reads=[pk], writes=[ek],
                tag="l0a_qkevac")
            dst = q_sp if oc < 8 else k_sp
            p = oc % 8
            sch.op("sync", (lambda dst=dst, p=p, c8=c8, eb=eb: lambda s:
                   s.dma_start(out=dst[p][:, c8 * 512:(c8 + 1) * 512], in_=eb))(),
                   reads=[ek], writes=["q_sp" if oc < 8 else "k_sp"],
                   dma=True, tag="l0a_qkst")
        for j in range(4):
            t = c8 * 4 + j
            for jc in range(2):
                ps = psA if jc == 0 else psB
                pk = "psA" if jc == 0 else "psB"

                def v_fn(te, j=j, jc=jc, ps=ps):
                    for i in range(NC):
                        ins = te.matmul(
                            ps[:, :],
                            ln_ch[:, i, j * 128:(j + 1) * 128],
                            wq_sb[:, i * 3 * D + 2 * D + jc * 512: i * 3 * D + 2 * D + (jc + 1) * 512],
                            start=(i == 0), stop=(i == NC - 1))
                    return ins
                sch.op("tensor", v_fn, reads=["wq_sb", "ln_ch"], writes=[pk],
                       tag="l0a_v")
                vk = f"vsb{j % 2}"
                sch.op("vector", (lambda ps=ps, j=j, jc=jc: lambda v:
                       v.tensor_copy(
                           vsb[:, j % 2, jc * 8 * 65:(jc * 8 + 8) * 65].rearrange(
                               "p (h e) -> p h e", h=8)[:, :, 0:64],
                           ps[:, :].rearrange("p (h e) -> p h e", h=8)))(),
                       reads=[pk], writes=[vk], tag="l0a_vevac")
            sch.op("sync", (lambda t=t, j=j: lambda s: s.dma_start(
                out=v_sp[t], in_=vsb[:, j % 2, :]))(),
                reads=[f"vsb{j % 2}"], writes=["v_sp"], dma=True, tag="l0a_vst")

    sch.barrier()
    stk.close()

    stk = ExitStack()
    alloc = lambda *a, **k: stk.enter_context(nc.sbuf_tensor(*a, **k))
    palloc = lambda *a, **k: stk.enter_context(nc.psum_tensor(*a, **k))
    qp = alloc("l0a_qp", [128, 2, S], BF16)
    kp = alloc("l0a_kp", [128, 2, S], BF16)
    vp = alloc("l0a_vp", [128, 2, NT * 130], BF16)
    e_sb = alloc("l0a_e", [128, 2, 2, 1024], BF16)
    opair = alloc("l0a_opair", [128, 1024], BF16)
    zrow = alloc("l0a_zrow", [1, 4, 512], F32)
    rzb = alloc("l0a_rzb", [64, 4, 512], F32)

    psS = [palloc(f"l0a_psS{i}", [128, 1024], F32) for i in range(2)]
    psO = [palloc(f"l0a_psO{i}", [128, 512], F32) for i in range(4)]

    def emit_pair_loads(p):
        b = p % 2
        bk = f"pb{b}"
        sch.op("sync", (lambda p=p, b=b: lambda s: s.dma_start(
            out=qp[:, b, :], in_=q_sp[p]))(), reads=["q_sp"],
            writes=[bk + "q"], dma=True, tag="l0a_qld")
        sch.op("sync", (lambda p=p, b=b: lambda s: s.dma_start(
            out=kp[:, b, :], in_=k_sp[p]))(), reads=["k_sp"],
            writes=[bk + "k"], dma=True, tag="l0a_kld")
        vsrc = bass.AP(v_sp, p * 130,
                       [[H * 65, 128], [128 * H * 65, NT], [1, 130]])
        sch.op("sync", (lambda b=b, vsrc=vsrc: lambda s: s.dma_start(
            out=vp[:, b, :].rearrange("p (t e) -> p t e", t=NT),
            in_=vsrc))(),
            reads=["v_sp"], writes=[bk + "v"], dma=True, tag="l0a_vld")

    emit_pair_loads(0)
    for p in range(NPAIR):
        b = p % 2
        bk = f"pb{b}"
        if p + 1 < NPAIR:
            emit_pair_loads(p + 1)

        for sp in range(4):
            W = 1024 * sp
            imax = 8 * sp + 7
            pend_av = None
            for i in range(imax + 1):
                valid_c = [c for c in range(2)
                           if 128 * i < W + 512 * c + 512]
                pa = i % 2
                ek = (f"e{pa}0", f"e{pa}1")
                c0 = valid_c[0]
                cw = 512 * len(valid_c)
                for hh in range(2):
                    def sc_fn(te, i=i, valid_c=valid_c, W=W, b=b, hh=hh):
                        for c in valid_c:
                            ins = te.matmul(
                                psS[hh][:, 512 * c:512 * c + 512],
                                kp[64 * hh:64 + 64 * hh, b, 128 * i:128 * i + 128],
                                qp[64 * hh:64 + 64 * hh, b, W + 512 * c: W + 512 * c + 512],
                                start=True, stop=True,
                                tile_position=(64 * hh, 0))
                        return ins
                    sch.op("tensor", sc_fn, reads=[bk + "q", bk + "k"],
                           writes=[f"psS{hh}"], tag="l0a_sc")
                    sch.op("scalar", (lambda i=i, c0=c0, cw=cw, pa=pa, hh=hh:
                           lambda se: se.activation(
                               e_sb[:, pa, hh, 512 * c0:512 * c0 + cw],
                               psS[hh][:, 512 * c0:512 * c0 + cw],
                               AF.Exp, scale=0.125))(),
                           reads=[f"psS{hh}"], writes=[ek[hh]], tag="l0a_exp")

                off = 128 * i - W
                if 0 <= off < 1024:
                    def mask_fn(v, off=off, pa=pa):
                        for hh in range(2):
                            if off > 0:
                                v.memset(e_sb[:, pa, hh, 0:off], 0.0)
                            ins = v.tensor_mul(e_sb[:, pa, hh, off:off + 128],
                                               e_sb[:, pa, hh, off:off + 128],
                                               mask_triu[:, :])
                        return ins
                    sch.op("vector", mask_fn, reads=[*ek, "mask_triu"],
                           writes=[*ek], tag="l0a_mask")

                if pend_av is not None:
                    pend_av()

                def mk_av(i=i, valid_c=valid_c, imax=imax, b=b, W=W, pa=pa,
                          ek=ek):
                    def av_fn(te):
                        for c in valid_c:
                            ic_max = min(imax, (W + 512 * c + 511) // 128)
                            for hh in range(2):
                                ins = te.matmul(
                                    psO[2 * c + hh][0:65, :],
                                    vp[:, b, 130 * i + 65 * hh:130 * i + 65 * hh + 65],
                                    e_sb[:, pa, hh, 512 * c:512 * c + 512],
                                    start=(i == 0), stop=(i == ic_max))
                        return ins
                    def emit():
                        sch.op("tensor", av_fn, reads=[*ek, bk + "v"],
                               writes=[f"psO{2 * c + hh}"
                                       for c in valid_c for hh in range(2)],
                               tag="l0a_av")
                    return emit
                pend_av = mk_av()
            pend_av()

            for c in range(2):
                for hh in range(2):
                    pidx = 2 * c + hh
                    zsl = zrow[:, pidx, :]
                    zk = f"zrow{pidx}"
                    sch.op("vector", (lambda pidx=pidx, zsl=zsl: lambda v:
                           v.tensor_copy(zsl, psO[pidx][64:65, :]))(),
                           reads=[f"psO{pidx}"], writes=[zk], tag="l0a_zc")
                    sch.op("scalar", (lambda zsl=zsl: lambda se:
                           se.activation(zsl, zsl, AF.Ln))(),
                           reads=[zk], writes=[zk], tag="l0a_zl")
                    sch.op("scalar", (lambda zsl=zsl: lambda se:
                           se.activation(zsl, zsl, AF.Exp, scale=-1.0))(),
                           reads=[zk], writes=[zk], tag="l0a_ze")
                    sch.op("tensor", (lambda pidx=pidx, zsl=zsl: lambda te:
                           te.matmul(psS[pidx % 2][0:64, 512 * (pidx // 2):512 * (pidx // 2) + 512],
                                     ones64[:, :], zsl,
                                     start=True, stop=True))(),
                           reads=[zk, "ones64"], writes=[f"psS{pidx % 2}"],
                           tag="l0a_zb")
                    sch.op("vector", (lambda pidx=pidx: lambda v:
                           v.tensor_copy(rzb[:, pidx, :],
                                         psS[pidx % 2][0:64, 512 * (pidx // 2):512 * (pidx // 2) + 512]))(),
                           reads=[f"psS{pidx % 2}"], writes=[f"rzb{pidx}"],
                           tag="l0a_zbe")
                    sch.op("vector", (lambda pidx=pidx, c=c, hh=hh: lambda v:
                           v.tensor_mul(
                               opair[64 * hh:64 * hh + 64, 512 * c:512 * c + 512],
                               psO[pidx][0:64, :],
                               rzb[:, pidx, :]))(),
                           reads=[f"psO{pidx}", f"rzb{pidx}"], writes=["opair"],
                           tag="l0a_onorm")
            sch.op("sync", (lambda p=p, W=W: lambda s: s.dma_start(
                out=o_sp[p][:, W:W + 1024], in_=opair[:, :]))(),
                reads=["opair"], writes=["o_sp"], dma=True, tag="l0a_ost")

    sch.barrier()
    stk.close()

    stk = ExitStack()
    alloc = lambda *a, **k: stk.enter_context(nc.sbuf_tensor(*a, **k))
    palloc = lambda *a, **k: stk.enter_context(nc.psum_tensor(*a, **k))
    psC = [palloc(f"l0a_c_ps{i}", [128, 512], F32) for i in range(4)]
    wp_sb = alloc("l0a_wp", [128, NC * D], BF16)
    ofm = alloc("l0a_ofm", [128, 2, NC, 512], BF16)
    xt2 = alloc("l0a_xt2", [128, 2 * D], F32)
    xn = alloc("l0a_xn", [128, 2 * D], F32)

    for i in range(NC):
        sch.op("gpsimd", (lambda i=i: lambda g: g.dma_start(
            out=wp_sb[:, i * D:(i + 1) * D], in_=wprojT[i * 128:(i + 1) * 128, :]))(),
            reads=["wprojT"], writes=["wp_sb"], dma=True, tag="l0a_wpld")

    def emit_ofm_loads(c8):
        b = c8 % 2
        for p in range(NPAIR):
            sch.op("gpsimd", (lambda p=p, c8=c8, b=b: lambda g: g.dma_start(
                out=ofm[:, b, p, :], in_=o_sp[p][:, c8 * 512:(c8 + 1) * 512]))(),
                reads=["o_sp", f"ofm{b}"], writes=[f"ofm{b}"], dma=True,
                tag="l0a_old")

    emit_ofm_loads(0)
    for c8 in range(8):
        b = c8 % 2
        if c8 + 1 < 8:
            emit_ofm_loads(c8 + 1)
        for j in range(4):
            t = c8 * 4 + j
            jb = j % 2
            sch.op("sync", (lambda t=t, jb=jb: lambda s: s.dma_start(
                out=xt2[:, jb * D:(jb + 1) * D], in_=x_in[t]))(),
                reads=[xin_name, f"x2t{jb}"], writes=[f"x2t{jb}"], dma=True,
                tag="l0a_xld")
            for oc in range(2):
                ps = psC[2 * oc + (j % 2)]
                pk = f"psC{2 * oc + (j % 2)}"

                def pr_fn(te, j=j, oc=oc, b=b, ps=ps):
                    for p in range(NPAIR):
                        ins = te.matmul(
                            ps[:, :], ofm[:, b, p, j * 128:(j + 1) * 128],
                            wp_sb[:, p * D + oc * 512: p * D + (oc + 1) * 512],
                            start=(p == 0), stop=(p == NPAIR - 1))
                    return ins
                sch.op("tensor", pr_fn, reads=[f"ofm{b}", "wp_sb"],
                       writes=[pk], tag="l0a_proj")
                sch.op("vector", (lambda oc=oc, jb=jb, ps=ps: lambda v:
                       v.tensor_add(xn[:, jb * D + oc * 512: jb * D + oc * 512 + 512],
                                    ps[:, :],
                                    xt2[:, jb * D + oc * 512: jb * D + oc * 512 + 512]))(),
                       reads=[pk, f"x2t{jb}"], writes=[f"xn{jb}"],
                       tag="l0a_res")
            sch.op("sync", (lambda t=t, jb=jb: lambda s: s.dma_start(
                out=x1[t], in_=xn[:, jb * D:(jb + 1) * D]))(),
                reads=[f"xn{jb}"], writes=[xout_name], dma=True, tag="l0a_xst")

    sch.barrier()
    stk.close()



def emit_l0f(st, xin_name, xout_name):
    from contextlib import ExitStack
    nc, sch, T = st["nc"], st["sch"], st["T"]
    x_in, x2 = T[xin_name], T[xout_name]
    ones1, onesrow = st["ones1"], st["onesrow"]
    DFF, R = 4 * D, 256
    NF = DFF // 128

    stk = ExitStack()
    alloc = lambda *a, **k: stk.enter_context(nc.sbuf_tensor(*a, **k))
    palloc = lambda *a, **k: stk.enter_context(nc.psum_tensor(*a, **k))

    u1_sb = alloc("l0f_u1", [128, NC, 256], BF16)
    v1_sb = alloc("l0f_v1", [128, 2, DFF], BF16)
    u2_sb = alloc("l0f_u2", [128, NF, 256], BF16)
    v2_sb = alloc("l0f_v2", [128, 2, D], BF16)
    b1_sb = alloc("l0f_b1", [1, DFF], BF16)
    b2_sb = alloc("l0f_b2", [1, D], BF16)
    xt = alloc("l0f_xt", [128, 4, D], F32)
    stats = alloc("l0f_stats", [128, 18], F32)
    lnt = alloc("l0f_lnt", [128, D], BF16)
    ln_ch = alloc("l0f_lnch", [128, NC, 512], BF16)
    p_fm = alloc("l0f_pfm", [128, 2, 512], BF16)
    h_fm = alloc("l0f_hfm", [128, NF, 512], BF16)
    q_fm = alloc("l0f_qfm", [128, 2, 512], BF16)
    sq = alloc("l0f_sq", [128, 2, 512], F32)
    th = alloc("l0f_th", [128, 2, 512], F32)
    xn = alloc("l0f_xn", [128, 2, 512], F32)

    psT = palloc("l0f_psT", [128, 1024], BF16)
    psP = palloc("l0f_psP", [128, 512], F32)
    psH = [palloc(f"l0f_psH{i}", [128, 512], F32) for i in range(2)]
    psQ = palloc("l0f_psQ", [128, 512], F32)
    psO = [palloc(f"l0f_psO{i}", [128, 512], F32) for i in range(2)]

    def wload(name, dst, src_ap, key):
        sch.op("gpsimd", lambda g: g.dma_start(out=dst, in_=src_ap),
               reads=[name], writes=[key], dma=True, tag="l0f_wl")
    U1, V1, u2r, V2, b1, b2 = T["U1"], T["V1"], T["u2r"], T["V2"], T["b1"], T["b2"]
    for i in range(NC):
        wload("U1", u1_sb[:, i, :], U1[i * 128:(i + 1) * 128, :], "u1_sb")
    for rc in range(2):
        wload("V1", v1_sb[:, rc, :], V1[rc * 128:(rc + 1) * 128, :], "v1_sb")
        wload("V2", v2_sb[:, rc, :], V2[rc * 128:(rc + 1) * 128, :], "v2_sb")
    wload("u2r", u2_sb[:, :, :],
          bass.AP(u2r, 0, [[256, 128], [128 * 256, NF], [1, 256]]), "u2_sb")
    wload("b1", b1_sb[:, :], b1[:, :], "b1_sb")
    wload("b2", b2_sb[:, :], b2[:, :], "b2_sb")
    sch.op("vector", lambda v: v.tensor_scalar_mul(
        u2_sb[:, :, :], u2_sb[:, :, :], 0.5), reads=["u2_sb"],
        writes=["u2_sb"], tag="l0f_u2h")

    C1 = 0.044715
    C2 = 0.7978845608028654

    for c8 in range(8):
        emit_ln_chunk(st, xin_name, x_in, c8, xt, stats, lnt, psT, ln_ch,
                      nslot=4)
        for rc in range(2):
            def p_fn(te, rc=rc):
                for i in range(NC):
                    ins = te.matmul(psP[:, :], u1_sb[:, i, rc * 128:(rc + 1) * 128],
                                    ln_ch[:, i, :],
                                    start=(i == 0), stop=(i == NC - 1))
                return ins
            sch.op("tensor", p_fn, reads=["u1_sb", "ln_ch"], writes=["psP"],
                   tag="l0f_p")
            sch.op("vector", (lambda rc=rc: lambda v: v.tensor_copy(
                p_fm[:, rc, :], psP[:, :]))(),
                reads=["psP"], writes=["p_fm"], tag="l0f_pe")
        for fc in range(NF):
            g = fc % 2
            ps = psH[g]
            pk = f"psH{g}"

            def h_fn(te, fc=fc, ps=ps):
                for rc in range(2):
                    te.matmul(ps[:, :], v1_sb[:, rc, fc * 128:(fc + 1) * 128],
                              p_fm[:, rc, :], start=(rc == 0), stop=False)
                return te.matmul(ps[:, :], b1_sb[:, fc * 128:(fc + 1) * 128],
                                 onesrow[:, :], start=False, stop=True)
            sch.op("tensor", h_fn, reads=["v1_sb", "p_fm", "b1_sb", "onesrow"],
                   writes=[pk], tag="l0f_h")
            sch.op("scalar", (lambda ps=ps, g=g: lambda sc: sc.activation(
                sq[:, g, :], ps[:, :], AF.Square))(),
                reads=[pk], writes=[f"sq{g}"], tag="l0f_sq")
            sch.op("vector", (lambda ps=ps, g=g: lambda v: v.tensor_mul(
                sq[:, g, :], sq[:, g, :], ps[:, :]))(),
                reads=[pk, f"sq{g}"], writes=[f"sq{g}"], tag="l0f_cube")
            sch.op("vector", (lambda ps=ps, g=g: lambda v: v.scalar_tensor_tensor(
                th[:, g, :], sq[:, g, :], C1, ps[:, :], ALU.mult, ALU.add))(),
                reads=[pk, f"sq{g}"], writes=[f"th{g}"], tag="l0f_arg")
            sch.op("scalar", (lambda g=g: lambda sc: sc.activation(
                th[:, g, :], th[:, g, :], AF.Tanh, scale=C2))(),
                reads=[f"th{g}"], writes=[f"th{g}"], tag="l0f_tanh")
            sch.op("vector", (lambda ps=ps, g=g, fc=fc: lambda v:
                   v.scalar_tensor_tensor(h_fm[:, fc, :], th[:, g, :], 1.0,
                                          ps[:, :], ALU.add, ALU.mult))(),
                reads=[pk, f"th{g}"], writes=["h_fm"], tag="l0f_hout")
        for rc in range(2):
            def q_fn(te, rc=rc):
                for fc in range(NF):
                    ins = te.matmul(psQ[:, :], u2_sb[:, fc, rc * 128:(rc + 1) * 128],
                                    h_fm[:, fc, :],
                                    start=(fc == 0), stop=(fc == NF - 1))
                return ins
            sch.op("tensor", q_fn, reads=["u2_sb", "h_fm"], writes=["psQ"],
                   tag="l0f_q")
            sch.op("vector", (lambda rc=rc: lambda v: v.tensor_copy(
                q_fm[:, rc, :], psQ[:, :]))(),
                reads=["psQ"], writes=["q_fm"], tag="l0f_qe")
        for j in range(4):
            t = c8 * 4 + j
            for oc in range(2):
                ps = psO[oc]
                pk = f"psO{oc}"

                def o_fn(te, j=j, oc=oc, ps=ps):
                    for rc in range(2):
                        te.matmul(ps[:, :], q_fm[:, rc, j * 128:(j + 1) * 128],
                                  v2_sb[:, rc, oc * 512:(oc + 1) * 512],
                                  start=(rc == 0), stop=False)
                    return te.matmul(ps[:, :], ones1[:, :],
                                     b2_sb[:, oc * 512:(oc + 1) * 512],
                                     start=False, stop=True)
                sch.op("tensor", o_fn, reads=["q_fm", "v2_sb", "b2_sb", "ones1"],
                       writes=[pk], tag="l0f_o")
                sch.op("vector", (lambda j=j, oc=oc, ps=ps: lambda v:
                       v.tensor_add(xn[:, oc, :], ps[:, :],
                                    xt[:, j, oc * 512:(oc + 1) * 512]))(),
                       reads=[pk, f"xt{j}"], writes=[f"xn{oc}"], tag="l0f_res")
            sch.op("sync", (lambda t=t: lambda s: s.dma_start(
                out=x2[t], in_=xn[:, :, :]))(),
                reads=["xn0", "xn1"], writes=[xout_name], dma=True,
                tag="l0f_xst")

    sch.barrier()
    stk.close()



TWO_PI = 2.0 * math.pi
MAGIC = 12582912.0


def emit_l1a(st, xin_name, xout_name):
    from contextlib import ExitStack
    nc, sch, T = st["nc"], st["sch"], st["T"]
    x_in, x3 = T[xin_name], T[xout_name]
    ones1, onesrow = st["ones1"], st["onesrow"]

    stk = ExitStack()
    alloc = lambda *a, **k: stk.enter_context(nc.sbuf_tensor(*a, **k))
    palloc = lambda *a, **k: stk.enter_context(nc.psum_tensor(*a, **k))

    wpv_sb = alloc("a_wpv", [128, NC, D], BF16)
    wk_sb = alloc("a_wk", [128, NC, D], BF16)
    wkv_sb = alloc("a_wkv", [128, NC, D], BF16)
    wg1_sb = alloc("a_wg1", [128, 16, D], BF16)
    wb1_sb = alloc("a_wb1", [128, NC, 512], BF16)
    wb2_sb = alloc("a_wb2", [128, 4, 2], BF16)
    wb2d = alloc("a_wb2d", [128, 4], BF16)
    wg2_sb = alloc("a_wg2", [128, NC], BF16)
    wo_sb = alloc("a_wo", [128, NC, D], BF16)
    lng_sb = alloc("a_lng", [128, NC], F32)
    lnb_sb = alloc("a_lnb", [128, NC], BF16)
    brow = alloc("a_brow", [1, 4 * D + 512], BF16)
    bg2r = alloc("a_bg2", [1, 1], F32)
    bb2r = alloc("a_bb2", [1, 2], F32)
    bb2d = alloc("a_bb2d", [1, 1], F32)
    bo_row = alloc("a_borow", [1, D], F32)
    bo_bf = alloc("a_bobf", [1, D], BF16)
    onescol = alloc("a_onescol", [128, 1], BF16)
    onescolf = alloc("a_onescolf", [128, 1], F32)
    onesf = alloc("a_onesf", [1, 128], F32)
    epsrow = alloc("a_epsrow", [1, 1], F32)

    ln_w = alloc("a_lnw", [128, 2, NC, 513], BF16)
    xr = alloc("a_xr", [128, 2, D], F32)
    xg = alloc("a_xg", [128, D], F32)
    xo = alloc("a_xo", [128, D], F32)
    stats = alloc("a_stats", [128, 36], F32)
    lnt = alloc("a_lnt", [128, D], BF16)
    ghb = alloc("a_ghb", [128, 512], BF16)
    kc = alloc("a_kc", [128, 513], BF16)
    ks = alloc("a_ks", [128, 513], BF16)
    tl = {nm: alloc("a_" + nm, [128, 512],
                    BF16 if nm in ("pc", "ps", "kvg") else F32)
          for nm in ["A", "B", "C", "pc", "ps", "kvg",
                     "a1", "a2", "a3", "a4"]}
    TK = {"A": "tA", "B": "tB", "C": "tC", "pc": "tpc",
          "ps": "tps", "kvg": "tkvg", "a1": "ta1", "a2": "ta2",
          "a3": "ta3", "a4": "ta4"}
    blend = alloc("a_blend", [128, NC, 512], BF16)
    g_b = alloc("a_gb", [128, 512], BF16)
    w0_b = alloc("a_w0b", [128, 512], BF16)
    w1_b = alloc("a_w1b", [128, 512], BF16)
    rb_b = alloc("a_rbb", [128, 512], BF16)
    nm_b = alloc("a_nmb", [128, 512], BF16)
    zeros = alloc("a_zeros", [128, 512], F32)
    rows = alloc("a_rows", [1, 7, 512], F32)
    ROW = {"g": 0, "gc": 1, "sq": 2, "rsq": 3, "a0": 4, "d": 5, "bw0": 6,
           "rsq32": 2, "t0": 5, "w0": 4, "w1": 5,
           "mean": 0, "msq": 1, "var": 2, "sd": 3, "rstd": 6, "nm": 4}
    def row(nm):
        return rows[:, ROW[nm], :]
    def rk(nm):
        return f"row{ROW[nm]}"
    car_sc = alloc("a_carsc", [128, NC, 4], F32)
    car_kc = alloc("a_carkc", [128, NC, 2], F32)
    car_g = alloc("a_carg", [1, 1], F32)

    psT = palloc("a_psT", [128, 1024], BF16)
    ps1 = palloc("a_ps1", [128, 512], F32)
    ps2 = palloc("a_ps2", [128, 512], F32)
    ps3 = palloc("a_ps3", [128, 512], F32)
    ps4 = palloc("a_ps4", [128, 512], F32)
    psr1 = palloc("a_psr1", [128, 512], F32)
    psacc = palloc("a_psacc", [128, 512], F32)
    pgh = palloc("a_pgh", [128, 512], F32)

    V, SC, TE, SY, GP = "vector", "scalar", "tensor", "sync", "gpsimd"

    Wpv, Wk, Wkv = T["Wpv"], T["Wk1a"], T["Wkv"]
    Wg1, Wb1, Wo = T["Wg1"], T["Wb1"], T["Wo1a"]
    for i in range(NC):
        for wsb, wdr, key in ((wpv_sb, Wpv, "wpv"), (wk_sb, Wk, "wk"),
                              (wkv_sb, Wkv, "wkv"), (wo_sb, Wo, "wo")):
            sch.op(GP, (lambda wsb=wsb, wdr=wdr, i=i: lambda g: g.dma_start(
                out=wsb[:, i, :], in_=wdr[i * 128:(i + 1) * 128, :]))(),
                reads=[wdr], writes=[key], dma=True, tag="a_wl")
        sch.op(GP, (lambda i=i: lambda g: g.dma_start(
            out=wb1_sb[:, i, :], in_=Wb1[i * 128:(i + 1) * 128, :]))(),
            reads=[Wb1], writes=["wb1"], dma=True, tag="a_wl")
    for i2 in range(16):
        sch.op(GP, (lambda i2=i2: lambda g: g.dma_start(
            out=wg1_sb[:, i2, :], in_=Wg1[i2 * 128:(i2 + 1) * 128, :]))(),
            reads=[Wg1], writes=["wg1"], dma=True, tag="a_wl")
    sch.op(GP, lambda g: g.dma_start(
        out=wb2_sb[:, :, :],
        in_=bass.AP(T["wb2r"], 0, [[2, 128], [128 * 2, 4], [1, 2]])),
        reads=[T["wb2r"]], writes=["wb2"], dma=True, tag="a_wl")
    sch.op(GP, lambda g: g.dma_start(out=wg2_sb[:, :], in_=T["wg2c"][:, :]),
           reads=[T["wg2c"]], writes=["wg2"], dma=True, tag="a_wl")
    sch.op(SY, lambda s: s.dma_start(out=lng_sb[:, :], in_=T["lngc"][:, :]),
           reads=[T["lngc"]], writes=["lng"], dma=True, tag="a_wl")
    sch.op(GP, lambda g: g.dma_start(out=lnb_sb[:, :], in_=T["lnbc"][:, :]),
           reads=[T["lnbc"]], writes=["lnb"], dma=True, tag="a_wl")
    for off, nm in ((0, "bpv"), (D, "bk1a"), (2 * D, "bkv"), (3 * D, "bg1")):
        sch.op(GP, (lambda off=off, nm=nm: lambda g: g.dma_start(
            out=brow[:, off:off + D], in_=T[nm][:, :]))(),
            reads=[T[nm]], writes=["brow"], dma=True, tag="a_wl")
    sch.op(GP, lambda g: g.dma_start(out=brow[:, 4 * D:4 * D + 512],
                                     in_=T["bb1"][:, :]),
           reads=[T["bb1"]], writes=["brow"], dma=True, tag="a_wl")
    sch.op(SY, lambda s: s.dma_start(out=bg2r[:, :], in_=T["bg2"][:, :]),
           reads=[T["bg2"]], writes=["bg2r"], dma=True, tag="a_wl")
    sch.op(SY, lambda s: s.dma_start(out=bb2r[:, :], in_=T["bb2"][:, :]),
           reads=[T["bb2"]], writes=["bb2r"], dma=True, tag="a_wl")
    sch.op(SY, lambda s: s.dma_start(out=bo_row[:, :], in_=T["bo1a"][:, :]),
           reads=[T["bo1a"]], writes=["borow"], dma=True, tag="a_wl")

    sch.op(V, lambda v: v.tensor_sub(bb2d[:, :], bb2r[:, 0:1], bb2r[:, 1:2]),
           reads=["bb2r"], writes=["bb2d"], tag="a_pre")
    sch.op(V, lambda v: v.tensor_sub(wb2d[:, :], wb2_sb[:, :, 0],
                                     wb2_sb[:, :, 1]),
           reads=["wb2"], writes=["wb2d"], tag="a_pre")
    sch.op(V, lambda v: v.memset(onescol[:, :], 1.0), writes=["onescol"])
    sch.op(V, lambda v: v.memset(onescolf[:, :], 1.0), writes=["onescolf"])
    sch.op(V, lambda v: v.memset(onesf[:, :], 1.0), writes=["onesf"])
    sch.op(V, lambda v: v.memset(epsrow[:, :], 1e-5), writes=["epsrow"])
    sch.op(V, lambda v: v.memset(zeros[:, :], 0.0), writes=["zeros"])
    sch.op(V, lambda v: v.memset(ln_w[:, 0, :, 0:1], 0.0), writes=["lnw00"])
    sch.op(V, lambda v: v.memset(kc[:, 0:1], 0.0), writes=["kc0"])
    sch.op(V, lambda v: v.memset(ks[:, 0:1], 0.0), writes=["ks0"])
    sch.op(V, lambda v: v.memset(car_sc[:, :, :], 0.0), writes=["carsc"])
    sch.op(V, lambda v: v.memset(car_kc[:, :, :], 0.0), writes=["carkc"])
    sch.op(V, lambda v: v.memset(car_g[:, :], 0.0), writes=["carg"])

    for oc in range(2):
        def bo_fn(te, oc=oc):
            for i in range(NC):
                ins = te.matmul(psr1[0:1, :], lnb_sb[:, i:i + 1],
                                wo_sb[:, i, oc * 512:(oc + 1) * 512],
                                start=(i == 0), stop=(i == NC - 1))
            return ins
        sch.op(TE, bo_fn, reads=["lnb", "wo"], writes=["psr1"], tag="a_bo")
        sch.op(V, (lambda oc=oc: lambda v: v.tensor_add(
            bo_row[:, oc * 512:(oc + 1) * 512], psr1[0:1, :],
            bo_row[:, oc * 512:(oc + 1) * 512]))(),
            reads=["psr1", "borow"], writes=["borow"], tag="a_bo")
    sch.op(V, lambda v: v.tensor_copy(bo_bf[:, :], bo_row[:, :]),
           reads=["borow"], writes=["bobf"], tag="a_bo")
    for i in range(NC):
        sch.op(V, (lambda i=i: lambda v: v.tensor_scalar_mul(
            wo_sb[:, i, :], wo_sb[:, i, :], lng_sb[:, i:i + 1]))(),
            reads=["wo", "lng"], writes=["wo"], tag="a_wog")


    def lnv_for(wp):
        def lnv(i2):
            if i2 < NC:
                return ln_w[:, wp, i2, 1:513]
            return ln_w[:, wp, i2 - NC, 0:512]
        return lnv

    def sec_next(w):
        units = []
        wp = w % 2
        lnk = f"lnw{wp}"
        lnk0 = f"lnw0{wp}"
        lnv = lnv_for(wp)
        if w > 0:
            units.append(lambda wp=wp, lnk0=lnk0: sch.op(
                V, (lambda wp=wp: lambda v: v.tensor_copy(
                    ln_w[:, wp, :, 0:1], ln_w[:, 1 - wp, :, 512:513]))(),
                reads=[f"lnw{1 - wp}"], writes=[lnk0], tag="a_lnc"))
        emit_ln_chunk(st, xin_name, x_in, w, xr, stats, lnt, psT,
                      ln_w[:, wp, :, :], ln_col0=1, ln_keys=[lnk], nslot=2,
                      units=units)

        for Dc in range(NC):
            def gates_unit(Dc=Dc, lnv=lnv, lnk=lnk, lnk0=lnk0):
                def gh_fn(te, Dc=Dc, lnv=lnv):
                    for i2 in range(16):
                        te.matmul(pgh[:, :],
                                  wg1_sb[:, i2, Dc * 128:(Dc + 1) * 128],
                                  lnv(i2), start=(i2 == 0), stop=False)
                    return te.matmul(
                        pgh[:, :],
                        brow[:, 3 * D + Dc * 128:3 * D + (Dc + 1) * 128],
                        onesrow[:, :], start=False, stop=True)
                sch.op(TE, gh_fn, reads=["wg1", lnk, lnk0, "brow", "onesrow"],
                       writes=["pgh"], tag="a_gh")
                sch.op(SC, lambda sc: sc.activation(ghb[:, :], pgh[:, :],
                                                    AF.Gelu),
                       reads=["pgh"], writes=["ghb"], tag="a_ghg")
                sch.op(TE, (lambda Dc=Dc: lambda te: te.matmul(
                    psacc[64:65, :], wg2_sb[:, Dc:Dc + 1], ghb[:, :],
                    start=(Dc == 0), stop=(Dc == NC - 1)))(),
                    reads=["wg2", "ghb"], writes=["psum_g"], tag="a_gacc")
            units.append(gates_unit)

        def gates_rows():
            sch.op(SC, lambda sc: sc.activation(
                row("g"), psacc[64:65, :], AF.Sigmoid, bias=bg2r[0:1, 0:1]),
                reads=["psum_g", "bg2r"], writes=[rk("g")], tag="a_sig")
            sch.op(V, lambda v: v.tensor_tensor_scan(
                row("gc"), row("g"), zeros[0:1, :], car_g[0:1, 0:1],
                ALU.add, ALU.add), reads=[rk("g"), "zeros", "carg"],
                writes=[rk("gc")], tag="a_gc")
            sch.op(V, lambda v: v.tensor_copy(car_g[:, :],
                                              rows[:, ROW["gc"], 511:512]),
                   reads=[rk("gc")], writes=["carg"], tag="a_gc2")
            sch.op(V, lambda v: v.tensor_scalar_max(row("gc"), row("gc"), 1.0),
                   reads=[rk("gc")], writes=[rk("gc")], tag="a_gc3")
            sch.op(SC, lambda sc: sc.activation(row("sq"), row("gc"), AF.Sqrt),
                   reads=[rk("gc")], writes=[rk("sq")], tag="a_gc4")
            sch.op(V, lambda v: v.reciprocal(row("rsq"), row("sq")),
                   reads=[rk("sq")], writes=[rk("rsq")], tag="a_gc5")
        units.append(gates_rows)

        for Dp in range(4):
            def bh_unit(Dp=Dp, lnv=lnv, lnk=lnk):
                def bh_fn(te, Dp=Dp, lnv=lnv):
                    for i in range(NC):
                        te.matmul(pgh[:, :],
                                  wb1_sb[:, i, Dp * 128:(Dp + 1) * 128],
                                  lnv(i), start=(i == 0), stop=False)
                    return te.matmul(
                        pgh[:, :],
                        brow[:, 4 * D + Dp * 128:4 * D + (Dp + 1) * 128],
                        onesrow[:, :], start=False, stop=True)
                sch.op(TE, bh_fn, reads=["wb1", lnk, "brow", "onesrow"],
                       writes=["pgh"], tag="a_bh")
                sch.op(SC, lambda sc: sc.activation(ghb[:, :], pgh[:, :],
                                                    AF.Gelu),
                       reads=["pgh"], writes=["ghb"], tag="a_bhg")
                sch.op(TE, (lambda Dp=Dp: lambda te: te.matmul(
                    psacc[64:65, :], wb2d[:, Dp:Dp + 1], ghb[:, :],
                    start=(Dp == 0), stop=(Dp == 3)))(),
                    reads=["wb2d", "ghb"], writes=["psum_g"], tag="a_bacc")
            units.append(bh_unit)

        def bh_rows():
            sch.op(SC, lambda sc: sc.activation(
                row("bw0"), psacc[64:65, :], AF.Sigmoid,
                bias=bb2d[0:1, 0:1]),
                reads=["psum_g", "bb2d"], writes=[rk("bw0")], tag="a_bw3")
            sch.op(V, lambda v: v.tensor_scalar_mul(row("rsq32"), row("rsq"),
                                                    1.0 / 32.0),
                   reads=[rk("rsq")], writes=[rk("rsq32")], tag="a_bw5")
            sch.op(V, lambda v: v.tensor_mul(row("t0"), row("bw0"),
                                             row("rsq32")),
                   reads=[rk("bw0"), rk("rsq32")], writes=[rk("t0")],
                   tag="a_bw6")
            sch.op(V, lambda v: v.tensor_scalar_mul(row("w0"), row("bw0"),
                                                    1.0 / 32.0),
                   reads=[rk("bw0")], writes=[rk("w0")], tag="a_bw4")
            sch.op(V, lambda v: v.tensor_sub(row("w1"), row("rsq32"),
                                             row("t0")),
                   reads=[rk("rsq32"), rk("t0")], writes=[rk("w1")],
                   tag="a_bw7")
        units.append(bh_rows)

        def bcast():
            for rnm, dst, dk in (("g", g_b, "g_b"), ("w0", w0_b, "w0_b"),
                                 ("w1", w1_b, "w1_b")):
                sch.op(TE, (lambda rnm=rnm: lambda te: te.matmul(
                    pgh[:, :], onesf[:, :], row(rnm),
                    start=True, stop=True))(),
                    reads=[rk(rnm), "onesf"], writes=["pgh"], tag="a_bc")
                sch.op(V, (lambda dst=dst: lambda v: v.tensor_copy(
                    dst[:, :], pgh[:, :]))(),
                    reads=["pgh"], writes=[dk], tag="a_bc2")
        units.append(bcast)
        return units

    def emit_E(w, Dc):
        W = 512 * w
        wp = w % 2
        lnk = f"lnw{wp}"
        lnv = lnv_for(wp)
        sch.op(SY, (lambda Dc=Dc, W=W: lambda s: s.dma_start(
            out=tl["pc"][:, :], in_=T["pc_sp"][Dc][:, W:W + 512]))(),
            reads=[T["pc_sp"]], writes=["tpc"], dma=True, tag="a_pcl")
        sch.op(SY, (lambda Dc=Dc, W=W: lambda s: s.dma_start(
            out=tl["ps"][:, :], in_=T["ps_sp"][Dc][:, W:W + 512]))(),
            reads=[T["ps_sp"]], writes=["tps"], dma=True, tag="a_psl")

        def proj(te, wsb, boff, ps, Dc, lnv=lnv):
            for i in range(NC):
                te.matmul(ps[:, :], wsb[:, i, Dc * 128:(Dc + 1) * 128],
                          lnv(i), start=(i == 0), stop=False)
            return te.matmul(ps[:, :],
                             brow[:, boff + Dc * 128:boff + (Dc + 1) * 128],
                             onesrow[:, :], start=False, stop=True)
        pv = ps1 if Dc % 2 == 0 else ps3
        pvk = "ps1" if Dc % 2 == 0 else "ps3"
        kv = ps2 if Dc % 2 == 0 else ps4
        kvk = "ps2" if Dc % 2 == 0 else "ps4"
        sch.op(TE, (lambda Dc=Dc, pv=pv, proj=proj: lambda te: proj(
            te, wpv_sb, 0, pv, Dc))(),
            reads=["wpv", lnk, "brow", "onesrow"], writes=[pvk],
            tag="a_pv")
        sch.op(TE, (lambda Dc=Dc, kv=kv, proj=proj: lambda te: proj(
            te, wkv_sb, 2 * D, kv, Dc))(),
            reads=["wkv", lnk, "brow", "onesrow"], writes=[kvk],
            tag="a_kv")
        sch.op(TE, (lambda Dc=Dc, proj=proj: lambda te: proj(
            te, wk_sb, D, psr1, Dc))(),
            reads=["wk", lnk, "brow", "onesrow"], writes=["psr1"],
            tag="a_kp")
        sch.op(SC, lambda sc: sc.activation(tl["C"][:, :], psr1[:, :],
                                            AF.Tanh),
               reads=["psr1"], writes=["tC"], tag="a_tanh")
        if w > 0:
            sch.op(GP, (lambda Dc=Dc: lambda v: v.tensor_copy(
                kc[:, 0:1], car_kc[:, Dc, 0:1]))(),
                reads=["carkc"], writes=["kc0"], tag="a_kcc")
            sch.op(GP, (lambda Dc=Dc: lambda v: v.tensor_copy(
                ks[:, 0:1], car_kc[:, Dc, 1:2]))(),
                reads=["carkc"], writes=["ks0"], tag="a_ksc")
        sch.op(SC, lambda sc: sc.activation(ks[:, 1:513], tl["C"][:, :],
                                            AF.Sin, scale=math.pi),
               reads=["tC"], writes=["ksm"], tag="a_ksin")
        sch.op(SC, lambda sc: sc.activation(tl["A"][:, :], tl["C"][:, :],
                                            AF.Sin, scale=math.pi / 2),
               reads=["tC"], writes=["tA"], tag="a_ksin2")
        sch.op(GP, lambda v: v.tensor_mul(tl["A"][:, :], tl["A"][:, :],
                                          tl["A"][:, :]),
               reads=["tA"], writes=["tA"], tag="a_ksq")
        sch.op(GP, lambda v: v.tensor_scalar(kc[:, 1:513], tl["A"][:, :],
                                            -2.0, 1.0, ALU.mult, ALU.add),
               reads=["tA"], writes=["kcm"], tag="a_kcos")
        sch.op(V, (lambda pv=pv, pvk=pvk: lambda v: v.tensor_mul(
            tl["a1"][:, :], tl["pc"][:, :], pv[:, :]))(),
               reads=["tpc", pvk], writes=["ta1"], tag="a_m1")
        sch.op(V, (lambda pv=pv, pvk=pvk: lambda v: v.tensor_mul(
            tl["a2"][:, :], tl["ps"][:, :], pv[:, :]))(),
               reads=["tps", pvk], writes=["ta2"], tag="a_m2")
        sch.op(V, (lambda kv=kv, kvk=kvk: lambda v: v.tensor_mul(
            tl["kvg"][:, :], kv[:, :], g_b[:, :]))(),
               reads=[kvk, "g_b"], writes=["tkvg"], tag="a_m3")
        sch.op(GP, lambda v: v.tensor_mul(tl["a3"][:, :], kc[:, 0:512],
                                         tl["kvg"][:, :]),
               reads=["kcm", "kc0", "tkvg"], writes=["ta3"], tag="a_m4")
        sch.op(GP, lambda v: v.tensor_mul(tl["a4"][:, :], ks[:, 0:512],
                                         tl["kvg"][:, :]),
               reads=["ksm", "ks0", "tkvg"], writes=["ta4"], tag="a_m5")
        sch.op(V, (lambda Dc=Dc: lambda v: v.tensor_copy(
            car_kc[:, Dc, 0:1], kc[:, 512:513]))(),
            reads=["kcm"], writes=["carkc"], tag="a_kcs")
        sch.op(V, (lambda Dc=Dc: lambda v: v.tensor_copy(
            car_kc[:, Dc, 1:2], ks[:, 512:513]))(),
            reads=["ksm"], writes=["carkc"], tag="a_kss")
        for si, anm in enumerate(["a1", "a2", "a3", "a4"]):
            sch.op(V, (lambda si=si, anm=anm, Dc=Dc: lambda v:
                   v.tensor_tensor_scan(
                       tl[anm][:, :], tl[anm][:, :], zeros[:, :],
                       car_sc[:, Dc, si:si + 1], ALU.add, ALU.add))(),
                reads=[f"t{anm}", "zeros", "carsc"], writes=[f"t{anm}"],
                tag="a_scan")
            sch.op(V, (lambda si=si, anm=anm, Dc=Dc: lambda v:
                   v.tensor_copy(car_sc[:, Dc, si:si + 1],
                                 tl[anm][:, 511:512]))(),
                reads=[f"t{anm}"], writes=["carsc"], tag="a_scar")
        sch.op(V, lambda v: v.tensor_mul(tl["A"][:, :], tl["pc"][:, :],
                                         tl["a1"][:, :]),
               reads=["tpc", "ta1"], writes=["tA"], tag="a_c1")
        sch.op(V, lambda v: v.tensor_mul(tl["C"][:, :], tl["ps"][:, :],
                                         tl["a2"][:, :]),
               reads=["tps", "ta2"], writes=["tC"], tag="a_c2")
        sch.op(V, lambda v: v.tensor_add(tl["A"][:, :], tl["A"][:, :],
                                         tl["C"][:, :]),
               reads=["tA", "tC"], writes=["tA"], tag="a_c3")
        sch.op(GP, lambda v: v.tensor_mul(tl["B"][:, :], kc[:, 1:513],
                                         tl["a3"][:, :]),
               reads=["kcm", "ta3"], writes=["tB"], tag="a_c4")
        sch.op(GP, lambda v: v.tensor_mul(tl["C"][:, :], ks[:, 1:513],
                                         tl["a4"][:, :]),
               reads=["ksm", "ta4"], writes=["tC"], tag="a_c5")
        sch.op(GP, lambda v: v.tensor_add(tl["B"][:, :], tl["B"][:, :],
                                         tl["C"][:, :]),
               reads=["tB", "tC"], writes=["tB"], tag="a_c6")
        sch.op(V, lambda v: v.tensor_mul(tl["A"][:, :], tl["A"][:, :],
                                         w0_b[:, :]),
               reads=["tA", "w0_b"], writes=["tA"], tag="a_c7")
        sch.op(GP, lambda v: v.tensor_mul(tl["B"][:, :], tl["B"][:, :],
                                         w1_b[:, :]),
               reads=["tB", "w1_b"], writes=["tB"], tag="a_c8")
        sch.op(V, (lambda Dc=Dc: lambda v: v.tensor_add(
            blend[:, Dc, :], tl["A"][:, :], tl["B"][:, :]))(),
            reads=["tA", "tB"], writes=["blend"], tag="a_c9")
        def stats_unit(Dc=Dc):
            sch.op(TE, (lambda Dc=Dc: lambda te: te.matmul(
                psacc[0:1, :], onescol[:, :], blend[:, Dc, :],
                start=(Dc == 0), stop=(Dc == NC - 1)))(),
                reads=["blend", "onescol"], writes=["psum_s"], tag="a_sum")
            sq = tl["A"] if Dc % 2 == 0 else tl["C"]
            sqk = "tA" if Dc % 2 == 0 else "tC"
            sch.op(GP, (lambda Dc=Dc, sq=sq: lambda g: g.tensor_mul(
                sq[:, :], blend[:, Dc, :], blend[:, Dc, :]))(),
                reads=["blend"], writes=[sqk], tag="a_bsq")
            sch.op(TE, (lambda Dc=Dc, sq=sq, sqk=sqk: lambda te: te.matmul(
                psacc[32:33, :], onescolf[:, :], sq[:, :],
                start=(Dc == 0), stop=(Dc == NC - 1)))(),
                reads=[sqk, "onescolf"], writes=["psum_q"], tag="a_ssq")
        return stats_unit

    def emit_lnrows(w):
        sch.op(V, lambda v: v.tensor_scalar_mul(row("mean"), psacc[0:1, :],
                                                1.0 / D),
               reads=["psum_s"], writes=[rk("mean")], tag="a_ln1")
        sch.op(V, lambda v: v.tensor_scalar_mul(row("msq"), psacc[32:33, :],
                                                1.0 / D),
               reads=["psum_q"], writes=[rk("msq")], tag="a_ln2")
        sch.op(V, lambda v: v.tensor_mul(row("t0"), row("mean"), row("mean")),
               reads=[rk("mean")], writes=[rk("t0")], tag="a_ln3")
        sch.op(V, lambda v: v.tensor_sub(row("var"), row("msq"), row("t0")),
               reads=[rk("msq"), rk("t0")], writes=[rk("var")], tag="a_ln4")
        sch.op(SC, lambda sc: sc.activation(row("sd"), row("var"), AF.Sqrt,
                                            bias=epsrow[0:1, 0:1]),
               reads=[rk("var"), "epsrow"], writes=[rk("sd")], tag="a_ln5")
        sch.op(V, lambda v: v.reciprocal(row("rstd"), row("sd")),
               reads=[rk("sd")], writes=[rk("rstd")], tag="a_ln6")
        sch.op(V, lambda v: v.scalar_tensor_tensor(
            row("nm"), row("mean"), -1.0, row("rstd"), ALU.mult, ALU.mult),
            reads=[rk("mean"), rk("rstd")], writes=[rk("nm")], tag="a_ln7")
        for rnm, dst, dk in (("rstd", rb_b, "rb_b"), ("nm", nm_b, "nm_b")):
            sch.op(TE, (lambda rnm=rnm: lambda te: te.matmul(
                pgh[:, :], onesf[:, :], row(rnm), start=True, stop=True))(),
                reads=[rk(rnm), "onesf"], writes=["pgh"], tag="a_lnb1")
            sch.op(V, (lambda dst=dst: lambda v: v.tensor_copy(
                dst[:, :], pgh[:, :]))(),
                reads=["pgh"], writes=[dk], tag="a_lnb2")

    def emit_apply(w):
        for Dc in range(NC):
            sch.op(GP, (lambda Dc=Dc: lambda g: g.tensor_mul(
                ghb[:, :], blend[:, Dc, :], rb_b[:, :]))(),
                reads=["blend", "rb_b"], writes=["ghb"], tag="a_ap1")
            sch.op(V, (lambda Dc=Dc: lambda v: v.tensor_add(
                blend[:, Dc, :], ghb[:, :], nm_b[:, :]))(),
                reads=["ghb", "nm_b"], writes=["blend"], tag="a_ap2")

    def emit_G(w):
        for j in range(4):
            t = 4 * w + j
            sch.op(SY, (lambda t=t: lambda s: s.dma_start(
                out=xg[:, :], in_=x_in[t]))(),
                reads=[xin_name], writes=["xg"], dma=True, tag="a_xld")
            def bn_fn(v):
                v.bn_stats(stats[:, 18:24], xg[:, 0:512])
                return v.bn_stats(stats[:, 24:30], xg[:, 512:1024])
            sch.op(V, bn_fn, reads=["xg"], writes=["st2_bn"], tag="a_gbn")
            sch.op(V, lambda v: v.bn_aggr(stats[:, 30:32], stats[:, 18:30]),
                   reads=["st2_bn"], writes=["st2_mv"], tag="a_gag")
            sch.op(SC, lambda sc: sc.activation(
                stats[:, 32:33], stats[:, 31:32], AF.Sqrt,
                bias=st["epsT"][:, :]),
                reads=["st2_mv", "epsT"], writes=["st2_sd"], tag="a_gsq")
            sch.op(V, lambda v: v.reciprocal(stats[:, 33:34], stats[:, 32:33]),
                   reads=["st2_sd"], writes=["st2_rstd"], tag="a_grc")
            sch.op(V, lambda v: v.scalar_tensor_tensor(
                stats[:, 34:35], stats[:, 30:31], -1.0, stats[:, 33:34],
                ALU.mult, ALU.mult),
                reads=["st2_mv", "st2_rstd"], writes=["st2_nm"], tag="a_gnm")
            for oc in range(2):
                def wo_fn(te, j=j, oc=oc):
                    for i in range(NC):
                        te.matmul(pgh[:, :],
                                  blend[:, i, j * 128:(j + 1) * 128],
                                  wo_sb[:, i, oc * 512:(oc + 1) * 512],
                                  start=(i == 0), stop=False)
                    return te.matmul(pgh[:, :], ones1[:, :],
                                     bo_bf[:, oc * 512:(oc + 1) * 512],
                                     start=False, stop=True)
                xok = f"xo{oc}"
                sch.op(TE, wo_fn, reads=["blend", "wo", "bobf", "ones1"],
                       writes=["pgh"], tag="a_wo2")
                sch.op(SC, (lambda oc=oc: lambda sc: sc.activation(
                    xo[:, oc * 512:(oc + 1) * 512], pgh[:, :], AF.Identity,
                    bias=stats[:, 34:35]))(),
                    reads=["pgh", "st2_nm"], writes=[xok], tag="a_res2")
                sch.op(V, (lambda oc=oc: lambda v: v.tensor_add(
                    xo[:, oc * 512:(oc + 1) * 512],
                    xo[:, oc * 512:(oc + 1) * 512],
                    xg[:, oc * 512:(oc + 1) * 512]))(),
                    reads=[xok, "xg"], writes=[xok], tag="a_res")
                sch.op(V, (lambda oc=oc: lambda v: v.scalar_tensor_tensor(
                    xo[:, oc * 512:(oc + 1) * 512],
                    xg[:, oc * 512:(oc + 1) * 512],
                    stats[:, 33:34],
                    xo[:, oc * 512:(oc + 1) * 512], ALU.mult, ALU.add))(),
                    reads=["xg", "st2_rstd", xok], writes=[xok],
                    tag="a_res3")
                sch.op(SY, (lambda t=t, oc=oc: lambda s: s.dma_start(
                    out=x3[t][:, oc * 512:(oc + 1) * 512],
                    in_=xo[:, oc * 512:(oc + 1) * 512]))(),
                    reads=[xok], writes=[xout_name], dma=True, tag="a_xst")

    nxt = sec_next(0)
    for u in nxt:
        u()
    for w in range(8):
        nxt = sec_next(w + 1) if w < 7 else []
        pos = 0
        stats_units = []
        for Dc in range(NC):
            take = (len(nxt) * Dc) // NC - pos
            for u in nxt[pos:pos + take]:
                u()
            pos += take
            stats_units.append(emit_E(w, Dc))
        for u in nxt[pos:]:
            u()
        pos = len(nxt)
        for u in stats_units:
            u()
        emit_lnrows(w)
        emit_apply(w)
        emit_G(w)

    sch.barrier()
    stk.close()



def emit_l1f(st, xin_name, xout_name):
    from contextlib import ExitStack
    nc, sch, T = st["nc"], st["sch"], st["T"]
    x_in, xout = T[xin_name], T[xout_name]
    q_sp, k_sp, v_sp, o_sp = T["q_sp"], T["k_sp"], T["v_sp"], T["o_sp"]

    stk = ExitStack()
    alloc = lambda *a, **k: stk.enter_context(nc.sbuf_tensor(*a, **k))
    palloc = lambda *a, **k: stk.enter_context(nc.psum_tensor(*a, **k))
    wq_sb = alloc("f_wq", [128, NC, D], BF16)
    wk_sb = alloc("f_wk", [128, NC, D], BF16)
    wv_sb = alloc("f_wv", [128, NC, D], BF16)
    xt = alloc("f_xt", [128, 2, D], F32)
    stats = alloc("f_stats", [128, 18], F32)
    lnt = alloc("f_lnt", [128, D], BF16)
    ln_ch = alloc("f_lnch", [128, NC, 512], BF16)
    qkv_e = alloc("f_qkve", [128, 2, 512], BF16)
    vsb = alloc("f_vsb", [128, 2, D], BF16)
    psT = palloc("f_psT", [128, 1024], BF16)
    psA = palloc("f_psA", [128, 512], F32)
    psB = palloc("f_psB", [128, 512], F32)

    for i in range(NC):
        for wsb, nm, key in ((wq_sb, "Wq1f", "wq"), (wk_sb, "Wk1f", "wk"),
                             (wv_sb, "Wv1f", "wv")):
            sch.op("gpsimd", (lambda wsb=wsb, nm=nm, i=i: lambda g:
                   g.dma_start(out=wsb[:, i, :],
                               in_=T[nm][i * 128:(i + 1) * 128, :]))(),
                reads=[T[nm]], writes=[key], dma=True, tag="f_wl")

    for c8 in range(8):
        emit_ln_chunk(st, xin_name, x_in, c8, xt, stats, lnt, psT, ln_ch,
                      nslot=2)
        for oc in range(16):
            ps = psA if oc % 2 == 0 else psB
            pk = "psA" if oc % 2 == 0 else "psB"
            wsb = wq_sb if oc < 8 else wk_sb
            wkey = "wq" if oc < 8 else "wk"
            p = oc % 8

            def qk_fn(te, p=p, ps=ps, wsb=wsb):
                for i in range(NC):
                    ins = te.matmul(ps[:, :], wsb[:, i, p * 128:(p + 1) * 128],
                                    ln_ch[:, i, :],
                                    start=(i == 0), stop=(i == NC - 1))
                return ins
            sch.op("tensor", qk_fn, reads=[wkey, "ln_ch"], writes=[pk],
                   tag="f_qk")
            eb = qkv_e[:, oc % 2, :]
            ek = f"qkve{oc % 2}"
            sch.op("vector", (lambda ps=ps, eb=eb: lambda v: v.tensor_copy(
                eb, ps[:, :]))(), reads=[pk], writes=[ek],
                tag="f_qkev")
            dst = q_sp if oc < 8 else k_sp
            sch.op("sync", (lambda dst=dst, p=p, c8=c8, eb=eb: lambda s:
                   s.dma_start(out=dst[p][:, c8 * 512:(c8 + 1) * 512], in_=eb))(),
                   reads=[ek], writes=["q_sp" if oc < 8 else "k_sp"],
                   dma=True, tag="f_qkst")
        for j in range(4):
            t = c8 * 4 + j
            for jc in range(2):
                ps = psA if jc == 0 else psB
                pk = "psA" if jc == 0 else "psB"

                def v_fn(te, j=j, jc=jc, ps=ps):
                    for i in range(NC):
                        ins = te.matmul(
                            ps[:, :], ln_ch[:, i, j * 128:(j + 1) * 128],
                            wv_sb[:, i, jc * 512:(jc + 1) * 512],
                            start=(i == 0), stop=(i == NC - 1))
                    return ins
                sch.op("tensor", v_fn, reads=["wv", "ln_ch"], writes=[pk],
                       tag="f_v")
                vk = f"vsb{j % 2}"
                sch.op("vector", (lambda ps=ps, j=j, jc=jc: lambda v:
                       v.tensor_copy(vsb[:, j % 2, jc * 512:(jc + 1) * 512],
                                     ps[:, :]))(),
                       reads=[pk], writes=[vk], tag="f_vev")
            sch.op("sync", (lambda t=t, j=j: lambda s: s.dma_start(
                out=v_sp[t][:, 0:D], in_=vsb[:, j % 2, :]))(),
                reads=[f"vsb{j % 2}"], writes=["v_sp"], dma=True, tag="f_vst")

    sch.barrier()
    stk.close()

    stk = ExitStack()
    alloc = lambda *a, **k: stk.enter_context(nc.sbuf_tensor(*a, **k))
    palloc = lambda *a, **k: stk.enter_context(nc.psum_tensor(*a, **k))
    qp = alloc("f_qp", [128, 2, S], BF16)
    kp = alloc("f_kp", [128, 2, S], BF16)
    vp = alloc("f_vp", [128, 2, NT * 128], BF16)
    e_sb = alloc("f_e", [128, 2, 2, S], BF16)
    zparts = alloc("f_zp", [128, 2, 2, 4], F32)
    zrec = alloc("f_zr", [128, 2, 2], F32)
    vsc = alloc("f_vsc", [128, 2, 128], BF16)
    acc = alloc("f_acc", [128, 6, 512], F32)
    opair = alloc("f_opair", [128, 2, 512], BF16)

    psS = [palloc(f"f_psS{i}", [128, 1024], F32) for i in range(2)]
    psSS = [palloc(f"f_psSS{i}", [128, 512], F32) for i in range(2)]
    psO = [palloc(f"f_psO{i}", [128, 512], F32) for i in range(2)]

    def emit_pair_loads(p):
        b = p % 2
        bk = f"pb{b}"
        sch.op("sync", (lambda p=p, b=b: lambda s: s.dma_start(
            out=qp[:, b, :], in_=q_sp[p]))(), reads=["q_sp"],
            writes=[bk + "q"], dma=True, tag="f_qld")
        sch.op("sync", (lambda p=p, b=b: lambda s: s.dma_start(
            out=kp[:, b, :], in_=k_sp[p]))(), reads=["k_sp"],
            writes=[bk + "k"], dma=True, tag="f_kld")
        vsrc = bass.AP(v_sp, p * 128,
                       [[H * 65, 128], [128 * H * 65, NT], [1, 128]])
        sch.op("sync", (lambda b=b, vsrc=vsrc: lambda s: s.dma_start(
            out=vp[:, b, :].rearrange("p (t e) -> p t e", t=NT),
            in_=vsrc))(),
            reads=["v_sp"], writes=[bk + "v"], dma=True, tag="f_vld")

    emit_pair_loads(0)
    for p in range(NPAIR):
        b = p % 2
        bk = f"pb{b}"
        if p + 1 < NPAIR:
            emit_pair_loads(p + 1)

        sch.op("vector", lambda v: v.memset(acc[:, :, :], 0.0),
               writes=["facc"], tag="f_acc0")

        def make_av_parts(i):
            pa = i % 2
            ek = (f"e{pa}0", f"e{pa}1")

            def psum_part(i=i, pa=pa, ek=ek):
                def av_fn(te):
                    for c in range(2):
                        bank = psO[c]
                        for hh in range(2):
                            ins = te.matmul(
                                bank[64 * hh:64 * hh + 64, :],
                                vsc[:, pa, 64 * hh:64 * hh + 64],
                                e_sb[:, pa, hh, 512 * c:512 * c + 512],
                                start=(i == 0), stop=(i == NT - 1),
                                tile_position=(0, 64 * hh))
                    return ins
                sch.op("tensor", av_fn, reads=[f"vsc{pa}", *ek],
                       writes=[f"psO{c}" for c in range(2)], tag="f_av")

            def scratch_part(cs, i=i, pa=pa, ek=ek):
                for c in cs:
                    sb = c % 2
                    def av2_fn(te, c=c, sb=sb):
                        for hh in range(2):
                            ins = te.matmul(
                                psSS[sb][64 * hh:64 * hh + 64, :],
                                vsc[:, pa, 64 * hh:64 * hh + 64],
                                e_sb[:, pa, hh, 512 * c:512 * c + 512],
                                start=True, stop=True,
                                tile_position=(0, 64 * hh))
                        return ins
                    sch.op("tensor", av2_fn, reads=[f"vsc{pa}", *ek],
                           writes=[f"psSS{sb}"], tag="f_av2")
                    sch.op("vector", (lambda c=c, sb=sb: lambda v:
                           v.tensor_add(acc[:, c - 2, :], acc[:, c - 2, :],
                                        psSS[sb][:, :]))(),
                        reads=[f"psSS{sb}", "facc"], writes=["facc"],
                        tag="f_av3")

            return [lambda: psum_part(),
                    lambda: scratch_part((2, 3)),
                    lambda: scratch_part((4, 5)),
                    lambda: scratch_part((6, 7))]

        pending = []
        for i in range(NT):
            pa = i % 2
            ek = (f"e{pa}0", f"e{pa}1")
            for j in range(4):
                for hh in range(2):
                    def sc2_fn(te, i=i, j=j, b=b, hh=hh):
                        for q2 in range(2):
                            ins = te.matmul(
                                psS[hh][:, q2 * 512:(q2 + 1) * 512],
                                kp[64 * hh:64 + 64 * hh, b,
                                   128 * i:128 * i + 128],
                                qp[64 * hh:64 + 64 * hh, b,
                                   1024 * j + 512 * q2:1024 * j + 512 * q2 + 512],
                                start=True, stop=True,
                                tile_position=(64 * hh, 0))
                        return ins
                    sch.op("tensor", sc2_fn,
                           reads=[bk + "q", bk + "k"], writes=[f"psS{hh}"],
                           tag="f_sc")
                    sch.op("scalar", (lambda j=j, hh=hh, pa=pa:
                           lambda se: se.activation(
                               e_sb[:, pa, hh, 1024 * j:1024 * j + 1024],
                               psS[hh][:, :], AF.Exp, scale=1.0 / 32.0,
                               accum_out=zparts[:, pa, hh, j:j + 1]))(),
                           reads=[f"psS{hh}"], writes=[ek[hh], f"zp{pa}"],
                           tag="f_exp")
                if pending:
                    pending.pop(0)()
            def z_fn(v, pa=pa):
                v.reduce_sum(zrec[:, pa, 0:1], zparts[:, pa, 0, 0:4], AX.X)
                return v.reduce_sum(zrec[:, pa, 1:2], zparts[:, pa, 1, 0:4],
                                    AX.X)
            sch.op("vector", z_fn, reads=[f"zp{pa}"], writes=[f"zrec{pa}"],
                   tag="f_z")
            sch.op("vector", (lambda pa=pa: lambda v: v.reciprocal(
                zrec[:, pa, :], zrec[:, pa, :]))(),
                reads=[f"zrec{pa}"], writes=[f"zrec{pa}"], tag="f_zr")
            def vs_fn(v, i=i, b=b, pa=pa):
                v.tensor_scalar_mul(vsc[:, pa, 0:64],
                                    vp[:, b, 128 * i:128 * i + 64],
                                    zrec[:, pa, 0:1])
                return v.tensor_scalar_mul(
                    vsc[:, pa, 64:128],
                    vp[:, b, 128 * i + 64:128 * i + 128],
                    zrec[:, pa, 1:2])
            sch.op("vector", vs_fn, reads=[f"zrec{pa}", bk + "v"],
                   writes=[f"vsc{pa}"], tag="f_vs")
            assert not pending
            pending = make_av_parts(i)
        for part in pending:
            part()
        for c in range(2):
            def ev_fn(v, c=c):
                return v.tensor_copy(
                    opair[:, c % 2, 0:512], psO[c][:, :])
            sch.op("vector", ev_fn, reads=[f"psO{c}"],
                   writes=[f"op{c % 2}"], tag="f_ev")
            sch.op("sync", (lambda p=p, c=c: lambda s: s.dma_start(
                out=o_sp[p][:, 512 * c:512 * c + 512],
                in_=opair[:, c % 2, 0:512]))(),
                reads=[f"op{c % 2}"], writes=["o_sp"], dma=True, tag="f_est")
        for c in range(2, 8):
            def ev2_fn(v, c=c):
                return v.tensor_copy(opair[:, c % 2, 0:512],
                                     acc[:, c - 2, :])
            sch.op("vector", ev2_fn, reads=["facc"], writes=[f"op{c % 2}"],
                   tag="f_ev2")
            sch.op("sync", (lambda p=p, c=c: lambda s: s.dma_start(
                out=o_sp[p][:, 512 * c:512 * c + 512],
                in_=opair[:, c % 2, 0:512]))(),
                reads=[f"op{c % 2}"], writes=["o_sp"], dma=True, tag="f_est2")

    sch.barrier()
    stk.close()

    stk = ExitStack()
    alloc = lambda *a, **k: stk.enter_context(nc.sbuf_tensor(*a, **k))
    palloc = lambda *a, **k: stk.enter_context(nc.psum_tensor(*a, **k))
    wo_sb = alloc("f_wo", [128, NC, D], BF16)
    wf_sb = alloc("f_wf", [128, NC, D], BF16)
    ofm = alloc("f_ofm", [128, 2, NC, 512], BF16)
    xt2 = alloc("f_xt2", [128, 2, D], F32)
    zb = alloc("f_zb", [128, 2, D], BF16)
    z2q = alloc("f_z2q", [128, 2, 2, 512], F32)
    nrm1 = alloc("f_nrm1", [128, 2, 4], F32)
    z1f = alloc("f_z1f", [128, 2, NC, 128], BF16)
    gout = alloc("f_gout", [128, 2, 2, 512], F32)
    psT2 = palloc("f_psT2", [128, 1024], BF16)
    psZ = [palloc(f"f_psZ{i}", [128, 512], F32) for i in range(4)]
    psW = [palloc(f"f_psW{i}", [128, 512], F32) for i in range(2)]

    for i in range(NC):
        sch.op("gpsimd", (lambda i=i: lambda g: g.dma_start(
            out=wo_sb[:, i, :], in_=T["Wo1f"][i * 128:(i + 1) * 128, :]))(),
            reads=[T["Wo1f"]], writes=["fwo"], dma=True, tag="f_wl2")
        sch.op("gpsimd", (lambda i=i: lambda g: g.dma_start(
            out=wf_sb[:, i, :], in_=T["Wff"][i * 128:(i + 1) * 128, :]))(),
            reads=[T["Wff"]], writes=["fwf"], dma=True, tag="f_wl2")

    def emit_fofm_loads(c8):
        b = c8 % 2
        for p in range(NPAIR):
            sch.op("gpsimd", (lambda p=p, c8=c8, b=b: lambda g: g.dma_start(
                out=ofm[:, b, p, :], in_=o_sp[p][:, c8 * 512:(c8 + 1) * 512]))(),
                reads=["o_sp", f"ofm{b}"], writes=[f"ofm{b}"], dma=True,
                tag="f_old")

    emit_fofm_loads(0)
    for c8 in range(8):
        b = c8 % 2
        if c8 + 1 < 8:
            emit_fofm_loads(c8 + 1)
        for j in range(4):
            t = c8 * 4 + j
            jb = j % 2
            zZ = [psZ[2 * jb], psZ[2 * jb + 1]]
            zZk = [f"psZ{2 * jb}", f"psZ{2 * jb + 1}"]
            sch.op("sync", (lambda t=t, jb=jb: lambda s: s.dma_start(
                out=xt2[:, jb, :], in_=x_in[t]))(),
                reads=[xin_name], writes=[f"x2t{jb}"], dma=True, tag="f_xld")
            for oc in range(2):
                def z_mm(te, j=j, oc=oc, b=b, zZ=zZ):
                    for pi in range(NPAIR):
                        ins = te.matmul(
                            zZ[oc][:, :], ofm[:, b, pi, j * 128:(j + 1) * 128],
                            wo_sb[:, pi, oc * 512:(oc + 1) * 512],
                            start=(pi == 0), stop=(pi == NPAIR - 1))
                    return ins
                sch.op("tensor", z_mm, reads=[f"ofm{b}", "fwo"],
                       writes=[zZk[oc]], tag="f_zmm")
            def sq_fn(sc, jb=jb, zZ=zZ):
                sc.activation(z2q[:, jb, 0, :], zZ[0][:, :], AF.Square)
                return sc.activation(z2q[:, jb, 1, :], zZ[1][:, :], AF.Square)
            sch.op("scalar", sq_fn, reads=zZk, writes=[f"z2q{jb}"],
                   tag="f_sq")
            sch.op("vector", (lambda jb=jb: lambda v: v.reduce_sum(
                nrm1[:, jb, 0:1],
                z2q[:, jb, :, :].rearrange("p a f -> p (a f)"), AX.X))(),
                   reads=[f"z2q{jb}"], writes=[f"fn1{jb}"], tag="f_n1")
            sch.op("scalar", (lambda jb=jb: lambda sc: sc.activation(
                nrm1[:, jb, 1:2], nrm1[:, jb, 0:1], AF.Sqrt))(),
                   reads=[f"fn1{jb}"], writes=[f"fn2{jb}"], tag="f_n2")
            sch.op("vector", (lambda jb=jb: lambda v: v.tensor_scalar_max(
                nrm1[:, jb, 1:2], nrm1[:, jb, 1:2], 1e-12))(),
                reads=[f"fn2{jb}"], writes=[f"fn2{jb}"], tag="f_n3")
            sch.op("vector", (lambda jb=jb: lambda v: v.reciprocal(
                nrm1[:, jb, 2:3], nrm1[:, jb, 1:2]))(),
                   reads=[f"fn2{jb}"], writes=[f"fn3{jb}"], tag="f_n4")
            def z1_fn(sc, jb=jb, zZ=zZ):
                sc.activation(zb[:, jb, 0:512], zZ[0][:, :], AF.Identity,
                              scale=nrm1[:, jb, 2:3])
                return sc.activation(zb[:, jb, 512:1024], zZ[1][:, :],
                                     AF.Identity, scale=nrm1[:, jb, 2:3])
            sch.op("scalar", z1_fn, reads=[*zZk, f"fn3{jb}"],
                   writes=[f"zb{jb}"], tag="f_z1")
            def tr_fn(te, jb=jb):
                for q in range(NC):
                    ins = te.transpose(psT2[:, q * 128:(q + 1) * 128],
                                       zb[:, jb, q * 128:(q + 1) * 128],
                                       st["ident"][:, :])
                return ins
            sch.op("tensor", tr_fn, reads=[f"zb{jb}", "ident"],
                   writes=["psT2"], tag="f_tr")
            sch.op("vector", (lambda jb=jb: lambda v: v.tensor_copy(
                z1f[:, jb, :, :],
                psT2[:, :].rearrange("p (c f) -> p c f", c=NC)))(),
                reads=["psT2"], writes=[f"z1f{jb}"], tag="f_trev")
            for oc in range(2):
                def w_mm(te, oc=oc, jb=jb):
                    for i in range(NC):
                        ins = te.matmul(
                            psW[oc][:, :], z1f[:, jb, i, :],
                            wf_sb[:, i, oc * 512:(oc + 1) * 512],
                            start=(i == 0), stop=(i == NC - 1))
                    return ins
                sch.op("tensor", w_mm, reads=[f"z1f{jb}", "fwf"],
                       writes=[f"psW{oc}"], tag="f_wmm")
            def sq2_fn(sc, jb=jb):
                sc.activation(z2q[:, jb, 0, :], psW[0][:, :], AF.Square)
                return sc.activation(z2q[:, jb, 1, :], psW[1][:, :], AF.Square)
            sch.op("scalar", sq2_fn, reads=["psW0", "psW1"],
                   writes=[f"z2q{jb}"], tag="f_sq2")
            sch.op("vector", (lambda jb=jb: lambda v: v.reduce_sum(
                nrm1[:, jb, 0:1],
                z2q[:, jb, :, :].rearrange("p a f -> p (a f)"), AX.X))(),
                   reads=[f"z2q{jb}"], writes=[f"fn1{jb}"], tag="f_n5")
            sch.op("scalar", (lambda jb=jb: lambda sc: sc.activation(
                nrm1[:, jb, 1:2], nrm1[:, jb, 0:1], AF.Sqrt))(),
                   reads=[f"fn1{jb}"], writes=[f"fn2{jb}"], tag="f_n6")
            sch.op("vector", (lambda jb=jb: lambda v: v.tensor_scalar_max(
                nrm1[:, jb, 1:2], nrm1[:, jb, 1:2], 1e-12))(),
                reads=[f"fn2{jb}"], writes=[f"fn2{jb}"], tag="f_n7")
            sch.op("vector", (lambda jb=jb: lambda v: v.reciprocal(
                nrm1[:, jb, 3:4], nrm1[:, jb, 1:2]))(),
                   reads=[f"fn2{jb}"], writes=[f"fn4{jb}"], tag="f_n8")
            def gel_fn(sc, jb=jb):
                sc.activation(gout[:, jb, 0, :], psW[0][:, :], AF.Gelu,
                              scale=nrm1[:, jb, 3:4])
                return sc.activation(gout[:, jb, 1, :], psW[1][:, :], AF.Gelu,
                                     scale=nrm1[:, jb, 3:4])
            sch.op("scalar", gel_fn, reads=["psW0", "psW1", f"fn4{jb}"],
                   writes=[f"fgout{jb}"], tag="f_gel")
            for oc in range(2):
                sch.op("vector", (lambda oc=oc, jb=jb: lambda v:
                       v.tensor_add(z2q[:, jb, oc, 0:512], gout[:, jb, oc, :],
                                    xt2[:, jb, oc * 512:(oc + 1) * 512]))(),
                    reads=[f"fgout{jb}", f"x2t{jb}"], writes=[f"z2q{jb}"],
                    tag="f_res")
            sch.op("sync", (lambda t=t, jb=jb: lambda s: s.dma_start(
                out=xout[t],
                in_=z2q[:, jb, :, 0:512].rearrange("p a f -> p (a f)")))(),
                reads=[f"z2q{jb}"], writes=[xout_name], dma=True, tag="f_ost")

    sch.barrier()
    stk.close()



def _prep_inputs(inputs):
    f32c = lambda a: np.ascontiguousarray(a, dtype=np.float32)
    shared = {
        "wqkvT": f32c(inputs["L0_attn_w_qkv"].T),
        "wprojT": f32c(inputs["L0_attn_w_proj"].T),
        "U1": f32c(inputs["L0_ffn_U1"]),
        "V1": f32c(inputs["L0_ffn_V1"]),
        "u2r": f32c(inputs["L0_ffn_U2"]).reshape(32, 128, 256),
        "V2": f32c(inputs["L0_ffn_V2"]),
        "b1": f32c(inputs["L0_ffn_b1"]).reshape(1, 4 * D),
        "b2": f32c(inputs["L0_ffn_b2"]).reshape(1, D),
        "ppT": f32c(inputs["L1_attn_pos_phases"][:S].T),
        "Wpv": f32c(inputs["L1_attn_Wpv"]),
        "Wk1a": f32c(inputs["L1_attn_Wk"]),
        "Wkv": f32c(inputs["L1_attn_Wkv"]),
        "Wg1": f32c(inputs["L1_attn_Wg1"]),
        "Wb1": f32c(inputs["L1_attn_Wb1"]),
        "wb2r": f32c(inputs["L1_attn_Wb2"]).reshape(4, 128, 2),
        "wg2c": f32c(inputs["L1_attn_Wg2"].reshape(NC, 128).T),
        "lngc": f32c(inputs["L1_attn_ln_g"].reshape(NC, 128).T),
        "lnbc": f32c(inputs["L1_attn_ln_b"].reshape(NC, 128).T),
        "Wo1a": f32c(inputs["L1_attn_Wo"]),
        "bpv": f32c(inputs["L1_attn_bpv"]).reshape(1, D),
        "bk1a": f32c(inputs["L1_attn_bk"]).reshape(1, D),
        "bkv": f32c(inputs["L1_attn_bkv"]).reshape(1, D),
        "bg1": f32c(inputs["L1_attn_bg1"]).reshape(1, D),
        "bg2": f32c(inputs["L1_attn_bg2"]).reshape(1, 1),
        "bb1": f32c(inputs["L1_attn_bb1"]).reshape(1, 512),
        "bb2": f32c(inputs["L1_attn_bb2"]).reshape(1, 2),
        "bo1a": f32c(inputs["L1_attn_bo"]).reshape(1, D),
        "Wq1f": f32c(inputs["L1_ffn_Wq"]),
        "Wk1f": f32c(inputs["L1_ffn_Wk"]),
        "Wv1f": f32c(inputs["L1_ffn_Wv"]),
        "Wo1f": f32c(inputs["L1_ffn_Wo"]),
        "Wff": f32c(inputs["L1_ffn_Wff"]),
    }
    x = inputs["L0_attn_x"]
    in_maps = []
    for c in range(8):
        m = dict(shared)
        m["x_in"] = f32c(x[c]).reshape(NT, 128, D)
        in_maps.append(m)
    return in_maps


def kernel(**inputs):
    nc = build_program()
    in_maps = _prep_inputs(inputs)
    res = run_bass_kernel_spmd(nc, in_maps, core_ids=list(range(8)))
    out = np.stack([r["xout"].reshape(S, D) for r in res.results], axis=0)
    return out


# version 34
# speedup vs baseline: 1.0219x; 1.0027x over previous
import math
import numpy as np
import concourse.bass as bass
import concourse.mybir as mybir
from concourse.bass_utils import run_bass_kernel_spmd

F32 = mybir.dt.float32
BF16 = mybir.dt.bfloat16
AF = mybir.ActivationFunctionType
ALU = mybir.AluOpType
AX = mybir.AxisListType

S = 4096
D = 1024
NT = S // 128
NC = D // 128
H = 16
HD = 64
NPAIR = 8

PHASES = ("l0a", "l0f", "l1a", "l1f")
DUMP_INTERNALS = False



class _Op:
    __slots__ = ("eng", "fn", "dma", "deps", "idx", "need_inc", "sem", "val",
                 "inc", "waits", "tag", "slot")

    def __init__(self, eng, fn, dma, idx, tag):
        self.eng = eng
        self.fn = fn
        self.dma = dma
        self.idx = idx
        self.deps = set()
        self.need_inc = False
        self.sem = None
        self.val = 0
        self.inc = 16 if dma else 1
        self.waits = []
        self.tag = tag


def _key(x):
    if isinstance(x, str):
        return x
    if isinstance(x, bass.AP):
        return x.tensor.name
    return x.name


class Sched:
    ENGINES = ("tensor", "scalar", "vector", "gpsimd", "sync")

    def __init__(self, nc):
        self.nc = nc
        self.ops = []
        self.last_writer = {}
        self.readers = {}
        self.epoch = 0
        self.epoch_of = []

    def op(self, eng, fn, reads=(), writes=(), dma=False, tag=""):
        o = _Op(eng, fn, dma, len(self.ops), tag)
        raw_waw = set()
        war = set()
        for k in reads:
            k = _key(k)
            w = self.last_writer.get(k)
            if w is not None:
                raw_waw.add(w)
        for k in writes:
            k = _key(k)
            w = self.last_writer.get(k)
            if w is not None:
                raw_waw.add(w)
            for r in self.readers.get(k, ()):
                war.add(r)
        def keep(d):
            if d.dma or d.eng != eng:
                return True
            return eng not in ("tensor", "sync")
        o.deps = {d for d in (raw_waw | war) if keep(d)}
        for k in reads:
            self.readers.setdefault(_key(k), []).append(o)
        for k in writes:
            k = _key(k)
            self.last_writer[k] = o
            self.readers[k] = []
        self.ops.append(o)
        self.epoch_of.append(self.epoch)
        return o

    def barrier(self):
        tails = {}
        for o in self.ops:
            if self.epoch_of[o.idx] != self.epoch:
                continue
            tails[(o.eng, o.dma)] = o
        deps = set(tails.values())
        for eng in self.ENGINES:
            mydeps = {d for d in deps if d.dma or d.eng != eng}
            o = _Op(eng, lambda e: e.nop(), False, len(self.ops), "barrier")
            o.deps = mydeps
            self.ops.append(o)
            self.epoch_of.append(self.epoch)
        self.last_writer = {}
        self.readers = {}
        self.epoch += 1

    DMA_SLOTS = 24

    def finalize(self):
        nc = self.nc
        for o in self.ops:
            if o.dma:
                o.need_inc = True
            for d in o.deps:
                d.need_inc = True
        waiters = {}
        for o in self.ops:
            for d in o.deps:
                waiters.setdefault(d.idx, []).append(o)
        slot_last = {}
        dma_count = {}
        for o in self.ops:
            if not o.dma:
                continue
            nslot = self.DMA_SLOTS
            base = dma_count.get(o.eng, 0)
            chosen = None
            for k in range(nslot):
                s = (base + k) % nslot
                prev = slot_last.get((o.eng, s))
                if prev is None or all(w.idx < o.idx
                                       for w in waiters.get(prev.idx, ())):
                    chosen = s
                    if prev is not None:
                        for w in waiters.get(prev.idx, ()):
                            if w is not o:
                                o.deps.add(w)
                    break
            assert chosen is not None, f"no free dma slot for op {o.idx}"
            o.slot = (o.eng, chosen)
            slot_last[o.slot] = o
            dma_count[o.eng] = base + 1
            for d in o.deps:
                d.need_inc = True
        for o in self.ops:
            if o.tag != "barrier":
                continue
            for key, last in slot_last.items():
                if last.idx < o.idx and (last.dma or last.eng != o.eng):
                    o.deps.add(last)
                    last.need_inc = True
        sems = {}
        counters = {}

        def get_sem(key, name):
            if key not in sems:
                sems[key] = nc.alloc_semaphore(name)
                counters[key] = 0
            return key

        for o in self.ops:
            if o.need_inc:
                if o.dma:
                    k = get_sem(o.slot, f"d_{o.slot[0]}_{o.slot[1]}")
                else:
                    k = get_sem(o.eng, f"c_{o.eng}")
                counters[k] += o.inc
                o.sem = sems[k]
                o.val = counters[k]
        for k, v in counters.items():
            assert v < 2 ** 31, (k, v)
        marks = {e: {} for e in self.ENGINES}
        for o in self.ops:
            m = marks[o.eng]
            for d in sorted(o.deps, key=lambda d: d.idx):
                assert d.idx < o.idx
                assert d.sem is not None
                if m.get(d.sem.num, -1) >= d.val:
                    continue
                o.waits.append((d.sem, d.val))
                m[d.sem.num] = d.val
        by_eng = {e: [] for e in self.ENGINES}
        for o in self.ops:
            by_eng[o.eng].append(o)
        with nc.Block() as block:
            def make(eng_name):
                def run(engine):
                    for o in by_eng[eng_name]:
                        for sem, val in o.waits:
                            engine.wait_ge(sem, val)
                        ins = o.fn(engine)
                        if o.need_inc:
                            ins.then_inc(o.sem, o.inc)
                return run
            block.tensor(make("tensor"))
            block.scalar(make("scalar"))
            block.vector(make("vector"))
            block.gpsimd(make("gpsimd"))
            block.sync(make("sync"))



def _triu_ones(n):
    return np.triu(np.ones((n, n), np.float32))


def build_program(phases=None):
    phases = PHASES if phases is None else phases
    nc = bass.Bass("TRN2", target_bir_lowering=False)
    sch = Sched(nc)

    T = {}

    def din(name, shape, dtype=F32):
        T[name] = nc.dram_tensor(name, list(shape), dtype, kind="ExternalInput")
        return T[name]

    def dint(name, shape, dtype=F32):
        kind = "ExternalOutput" if DUMP_INTERNALS else "Internal"
        T[name] = nc.dram_tensor(name, list(shape), dtype, kind=kind)
        return T[name]

    din("x_in", [NT, 128, D])
    xout = nc.dram_tensor("xout", [NT, 128, D], F32, kind="ExternalOutput")
    T["xout"] = xout

    din("wqkvT", [D, 3 * D])
    din("wprojT", [D, D])
    din("U1", [D, 256])
    din("V1", [256, 4 * D])
    din("u2r", [32, 128, 256])
    din("V2", [256, D])
    din("b1", [1, 4 * D])
    din("b2", [1, D])
    din("ppT", [D, S])
    din("Wpv", [D, D])
    din("Wk1a", [D, D])
    din("Wkv", [D, D])
    din("Wg1", [2 * D, D])
    din("Wb1", [D, 512])
    din("wb2r", [4, 128, 2])
    din("wg2c", [128, NC])
    din("lngc", [128, NC])
    din("lnbc", [128, NC])
    din("Wo1a", [D, D])
    din("Wq1f", [D, D])
    din("Wk1f", [D, D])
    din("Wv1f", [D, D])
    din("Wo1f", [D, D])
    din("Wff", [D, D])
    din("bpv", [1, D])
    din("bk1a", [1, D])
    din("bkv", [1, D])
    din("bg1", [1, D])
    din("bg2", [1, 1])
    din("bb1", [1, 512])
    din("bb2", [1, 2])
    din("bo1a", [1, D])

    dint("pc_sp", [NC, 128, S], BF16)
    dint("ps_sp", [NC, 128, S], BF16)
    dint("q_sp", [NPAIR, 128, S], BF16)
    dint("k_sp", [NPAIR, 128, S], BF16)
    dint("v_sp", [NT, 128, H * 65], BF16)
    dint("o_sp", [NPAIR, 128, S], BF16)
    dint("x1", [NT, 128, D])
    dint("x2", [NT, 128, D])
    dint("x3", [NT, 128, D])

    tri = nc.inline_tensor(_triu_ones(128).astype(np.float32), name="c_triu")
    T["c_triu"] = tri

    alloc = nc.alloc_sbuf_tensor

    epsT = alloc("epsT", [128, 1], F32)
    ident = alloc("ident", [128, 128], BF16)
    mask_triu = alloc("mask_triu", [128, 128], BF16)
    ones64 = alloc("ones64", [1, 64], F32)

    ones1 = alloc("ones1", [1, 128], BF16)
    onesrow = alloc("onesrow", [1, 512], BF16)
    sch.op("vector", lambda v: v.memset(epsT[:, :], 1e-5), writes=["epsT"])
    sch.op("gpsimd", lambda g: g.dma_start(out=mask_triu[:, :], in_=tri[:, :]),
           writes=["mask_triu"], dma=True)
    sch.op("vector", lambda v: v.memset(ones64[:, :], 1.0), writes=["ones64"])
    sch.op("vector", lambda v: v.memset(ones1[:, :], 1.0), writes=["ones1"])
    sch.op("vector", lambda v: v.memset(onesrow[:, :], 1.0), writes=["onesrow"])
    sch.op("gpsimd", lambda g: g.memset(ident[:, :], 1.0), writes=["ident"])
    sch.op("gpsimd", lambda g: g.affine_select(
        out=ident[:, :], in_=ident[:, :], compare_op=ALU.is_ge,
        fill=0.0, base=0, pattern=[[-1, 128]], channel_multiplier=1),
        reads=["ident"], writes=["ident"])
    sch.op("gpsimd", lambda g: g.affine_select(
        out=ident[:, :], in_=ident[:, :], compare_op=ALU.is_ge, fill=0.0,
        base=0, pattern=[[1, 128]], channel_multiplier=-1),
        reads=["ident"], writes=["ident"])

    from contextlib import ExitStack as _ES
    state = dict(nc=nc, sch=sch, T=T, epsT=epsT, ident=ident,
                 mask_triu=mask_triu, ones64=ones64, ones1=ones1,
                 onesrow=onesrow, gstk=_ES())

    xcur = "x_in"
    if "l0a" in phases:
        emit_l0a(state, xcur, "x1")
        xcur = "x1"
    if "l0f" in phases:
        emit_l0f(state, xcur, "x2")
        xcur = "x2"
    if "l1a" in phases:
        emit_l1a(state, xcur, "x3")
        xcur = "x3"
    if "l1f" in phases:
        emit_l1f(state, xcur, "xout")
        xcur = "xout"

    if xcur != "xout":
        emit_copyout(state, xcur)
    sch.barrier()
    sch.finalize()
    return nc


def emit_copyout(st, xcur):
    nc, sch, T = st["nc"], st["sch"], st["T"]
    if xcur == "x_in":
        src = T["x_in"]
    else:
        src = T[xcur]
    xout = T["xout"]
    buf = nc.alloc_sbuf_tensor("cp_buf", [128, 4 * D], F32)
    for t0 in range(0, NT, 4):
        for j in range(4):
            t = t0 + j
            sch.op("sync", (lambda t=t, j=j: lambda s: s.dma_start(
                out=buf[:, j * D:(j + 1) * D], in_=src[t]))(),
                reads=[xcur], writes=[f"cp{j}"], dma=True, tag="cpin")
        for j in range(4):
            t = t0 + j
            sch.op("sync", (lambda t=t, j=j: lambda s: s.dma_start(
                out=xout[t], in_=buf[:, j * D:(j + 1) * D]))(),
                reads=[f"cp{j}"], writes=["xout"], dma=True, tag="cpout")



def emit_ln_chunk(st, xsrc_name, xsrc, c8, xt, stats, lnt, psT, ln_ch,
                  ln_col0=0, ln_keys=None, xt_extra_reads=(), nslot=2,
                  units=None):
    nc, sch = st["nc"], st["sch"]
    epsT, ident = st["epsT"], st["ident"]

    def tile(j):
        t = c8 * 4 + j
        xbuf = xt[:, j % nslot, :]
        kx = f"xt{j % nslot}"
        sch.op("sync", (lambda t=t, xbuf=xbuf: lambda s: s.dma_start(
            out=xbuf, in_=xsrc[t]))(),
            reads=[xsrc_name, *xt_extra_reads], writes=[kx], dma=True, tag="lnx")


        def bn_fn(v, xbuf=xbuf):
            v.bn_stats(stats[:, 0:6], xbuf[:, 0:512])
            return v.bn_stats(stats[:, 6:12], xbuf[:, 512:1024])
        sch.op("vector", bn_fn, reads=[kx], writes=["st_bn"], tag="lnbn")
        sch.op("vector", lambda v: v.bn_aggr(stats[:, 12:14], stats[:, 0:12]),
               reads=["st_bn"], writes=["st_mv"], tag="lnaggr")
        sch.op("scalar", lambda sc: sc.activation(
            stats[:, 14:15], stats[:, 13:14], AF.Sqrt, bias=epsT[:, :]),
            reads=["st_mv", "epsT"], writes=["st_sd"], tag="lnsqrt")
        sch.op("vector", lambda v: v.reciprocal(stats[:, 15:16],
                                                stats[:, 14:15]),
               reads=["st_sd"], writes=["st_rstd"], tag="lnrecip")
        sch.op("vector", lambda v: v.scalar_tensor_tensor(
            stats[:, 16:17], stats[:, 12:13], -1.0, stats[:, 15:16],
            ALU.mult, ALU.mult),
            reads=["st_mv", "st_rstd"], writes=["st_nm"], tag="lnnm")
        sch.op("scalar", (lambda xbuf=xbuf: lambda sc: sc.activation(
            lnt[:, :], xbuf, AF.Identity,
            bias=stats[:, 16:17], scale=stats[:, 15:16]))(),
            reads=["st_rstd", "st_nm", kx], writes=["lnt"], tag="lnapply")

        def tr_fn(te):
            for q in range(NC):
                ins = te.transpose(psT[:, q * 128:(q + 1) * 128],
                                   lnt[:, q * 128:(q + 1) * 128], ident[:, :])
            return ins
        sch.op("tensor", tr_fn, reads=["lnt", "ident"], writes=["psT"],
               tag="lntr")
        col = ln_col0 + 128 * j
        lk = ln_keys if ln_keys is not None else ["ln_ch"]
        sch.op("vector", (lambda col=col: lambda v: v.tensor_copy(
            ln_ch[:, :, col:col + 128],
            psT[:, :].rearrange("p (c f) -> p c f", c=NC)))(),
            reads=["psT"], writes=lk, tag="lnevac")

    for j in range(4):
        if units is None:
            tile(j)
        else:
            units.append(lambda j=j: tile(j))



def make_pt_units(st, alloc):
    nc, sch, T = st["nc"], st["sch"], st["T"]
    ppb = alloc("pt_pp", [128, 2, 1024], F32)
    ptA = alloc("pt_A", [128, 2, 1024], F32)
    pcb = alloc("pt_pc", [128, 2, 1024], BF16)
    psb = alloc("pt_ps", [128, 2, 1024], BF16)
    pc_sp, ps_sp = T["pc_sp"], T["ps_sp"]

    def pt_unit(u):
        Dc, Q = u // 4, u % 4
        W = 1024 * Q
        p = u % 2
        sch.op("gpsimd", (lambda Dc=Dc, W=W, p=p: lambda g: g.dma_start(
            out=ppb[:, p, :],
            in_=T["ppT"][Dc * 128:(Dc + 1) * 128, W:W + 1024]))(),
            reads=[T["ppT"]], writes=[f"ptpp{p}"], dma=True, tag="pt_l")
        sch.op("gpsimd", (lambda p=p: lambda g: g.tensor_scalar(
            ptA[:, p, :], ppb[:, p, :], 1.0 / TWO_PI, MAGIC,
            ALU.mult, ALU.add))(),
            reads=[f"ptpp{p}"], writes=[f"ptA{p}"], tag="pt_rr1")
        sch.op("gpsimd", (lambda p=p: lambda g: g.tensor_scalar(
            ptA[:, p, :], ptA[:, p, :], -MAGIC, -TWO_PI,
            ALU.add, ALU.mult))(),
            reads=[f"ptA{p}"], writes=[f"ptA{p}"], tag="pt_rr2")
        sch.op("gpsimd", (lambda p=p: lambda g: g.tensor_tensor(
            ptA[:, p, :], ptA[:, p, :], ppb[:, p, :], ALU.add))(),
            reads=[f"ptA{p}", f"ptpp{p}"], writes=[f"ptA{p}"], tag="pt_rr3")
        sch.op("scalar", (lambda p=p: lambda sc: sc.activation(
            psb[:, p, :], ptA[:, p, :], AF.Sin))(),
            reads=[f"ptA{p}"], writes=[f"ptps{p}"], tag="pt_sin")
        sch.op("scalar", (lambda p=p: lambda sc: sc.activation(
            ppb[:, p, :], ptA[:, p, :], AF.Sin, scale=0.5))(),
            reads=[f"ptA{p}"], writes=[f"ptpp{p}"], tag="pt_sin2")
        sch.op("scalar", (lambda p=p: lambda sc: sc.activation(
            ppb[:, p, :], ppb[:, p, :], AF.Square))(),
            reads=[f"ptpp{p}"], writes=[f"ptpp{p}"], tag="pt_sq")
        sch.op("gpsimd", (lambda p=p: lambda g: g.tensor_scalar(
            pcb[:, p, :], ppb[:, p, :], -2.0, 1.0, ALU.mult, ALU.add))(),
            reads=[f"ptpp{p}"], writes=[f"ptpc{p}"], tag="pt_cos")
        sch.op("gpsimd", (lambda Dc=Dc, W=W, p=p: lambda g: g.dma_start(
            out=pc_sp[Dc][:, W:W + 1024], in_=pcb[:, p, :]))(),
            reads=[f"ptpc{p}"], writes=["pcsp"], dma=True, tag="pt_st")
        sch.op("gpsimd", (lambda Dc=Dc, W=W, p=p: lambda g: g.dma_start(
            out=ps_sp[Dc][:, W:W + 1024], in_=psb[:, p, :]))(),
            reads=[f"ptps{p}"], writes=["pssp"], dma=True, tag="pt_st")

    return [(lambda u=u: pt_unit(u)) for u in range(32)]


def emit_l0a(st, xin_name, xout_name):
    from contextlib import ExitStack
    nc, sch, T = st["nc"], st["sch"], st["T"]
    x_in, x1 = T[xin_name], T[xout_name]
    wqkvT, wprojT = T["wqkvT"], T["wprojT"]
    q_sp, k_sp, v_sp, o_sp = T["q_sp"], T["k_sp"], T["v_sp"], T["o_sp"]
    mask_triu, ones64 = st["mask_triu"], st["ones64"]

    stk = ExitStack()
    alloc = lambda *a, **k: stk.enter_context(nc.sbuf_tensor(*a, **k))
    palloc = lambda *a, **k: stk.enter_context(nc.psum_tensor(*a, **k))
    wq_sb = alloc("l0a_wq", [128, NC * 3 * D], BF16)
    xt = alloc("l0a_xt", [128, 2, D], F32)
    stats = alloc("l0a_stats", [128, 18], F32)
    lnt = alloc("l0a_lnt", [128, D], BF16)
    ln_ch = alloc("l0a_lnch", [128, NC, 512], BF16)
    qkv_e = alloc("l0a_qkve", [128, 2 * 512], BF16)
    vsb = alloc("l0a_vsb", [128, 2, H * 65], BF16)

    psT = palloc("l0a_psT", [128, 1024], BF16)
    psA = palloc("l0a_psA", [128, 512], F32)
    psB = palloc("l0a_psB", [128, 512], F32)

    def vones(v):
        for h in range(H):
            for b in range(2):
                ins = v.memset(vsb[:, b, h * 65 + 64:h * 65 + 65], 1.0)
        return ins
    sch.op("vector", vones, writes=["vsb0", "vsb1"], tag="vones")

    for i in range(NC):
        sch.op("gpsimd", (lambda i=i: lambda g: g.dma_start(
            out=wq_sb[:, i * 3 * D:(i + 1) * 3 * D], in_=wqkvT[i * 128:(i + 1) * 128, :]))(),
            reads=["wqkvT"], writes=["wq_sb"], dma=True, tag="l0a_wload")

    pt_units = make_pt_units(st, alloc)
    for c8 in range(8):
        emit_ln_chunk(st, xin_name, x_in, c8, xt, stats, lnt, psT, ln_ch)
        for u in pt_units[c8 * 4:(c8 + 1) * 4]:
            u()
        for oc in range(16):
            ps = psA if oc % 2 == 0 else psB
            pk = "psA" if oc % 2 == 0 else "psB"

            def qk_fn(te, oc=oc, ps=ps):
                for i in range(NC):
                    ins = te.matmul(ps[:, :],
                                    wq_sb[:, i * 3 * D + oc * 128: i * 3 * D + (oc + 1) * 128],
                                    ln_ch[:, i, :],
                                    start=(i == 0), stop=(i == NC - 1))
                return ins
            sch.op("tensor", qk_fn, reads=["wq_sb", "ln_ch"], writes=[pk],
                   tag="l0a_qk")
            eb = qkv_e[:, (oc % 2) * 512:(oc % 2 + 1) * 512]
            ek = f"qkve{oc % 2}"
            sch.op("vector", (lambda ps=ps, eb=eb: lambda v: v.tensor_copy(
                eb, ps[:, :]))(), reads=[pk], writes=[ek],
                tag="l0a_qkevac")
            dst = q_sp if oc < 8 else k_sp
            p = oc % 8
            sch.op("sync", (lambda dst=dst, p=p, c8=c8, eb=eb: lambda s:
                   s.dma_start(out=dst[p][:, c8 * 512:(c8 + 1) * 512], in_=eb))(),
                   reads=[ek], writes=["q_sp" if oc < 8 else "k_sp"],
                   dma=True, tag="l0a_qkst")
        for j in range(4):
            t = c8 * 4 + j
            for jc in range(2):
                ps = psA if jc == 0 else psB
                pk = "psA" if jc == 0 else "psB"

                def v_fn(te, j=j, jc=jc, ps=ps):
                    for i in range(NC):
                        ins = te.matmul(
                            ps[:, :],
                            ln_ch[:, i, j * 128:(j + 1) * 128],
                            wq_sb[:, i * 3 * D + 2 * D + jc * 512: i * 3 * D + 2 * D + (jc + 1) * 512],
                            start=(i == 0), stop=(i == NC - 1))
                    return ins
                sch.op("tensor", v_fn, reads=["wq_sb", "ln_ch"], writes=[pk],
                       tag="l0a_v")
                vk = f"vsb{j % 2}"
                sch.op("vector", (lambda ps=ps, j=j, jc=jc: lambda v:
                       v.tensor_copy(
                           vsb[:, j % 2, jc * 8 * 65:(jc * 8 + 8) * 65].rearrange(
                               "p (h e) -> p h e", h=8)[:, :, 0:64],
                           ps[:, :].rearrange("p (h e) -> p h e", h=8)))(),
                       reads=[pk], writes=[vk], tag="l0a_vevac")
            sch.op("sync", (lambda t=t, j=j: lambda s: s.dma_start(
                out=v_sp[t], in_=vsb[:, j % 2, :]))(),
                reads=[f"vsb{j % 2}"], writes=["v_sp"], dma=True, tag="l0a_vst")

    sch.barrier()
    stk.close()

    stk = ExitStack()
    alloc = lambda *a, **k: stk.enter_context(nc.sbuf_tensor(*a, **k))
    palloc = lambda *a, **k: stk.enter_context(nc.psum_tensor(*a, **k))
    qp = alloc("l0a_qp", [128, 2, S], BF16)
    kp = alloc("l0a_kp", [128, 2, S], BF16)
    vp = alloc("l0a_vp", [128, 2, NT * 130], BF16)
    e_sb = alloc("l0a_e", [128, 2, 2, 1024], BF16)
    opair = alloc("l0a_opair", [128, 1024], BF16)
    zrow = alloc("l0a_zrow", [1, 4, 512], F32)
    rzb = alloc("l0a_rzb", [64, 4, 512], F32)

    psS = [palloc(f"l0a_psS{i}", [128, 1024], F32) for i in range(2)]
    psO = [palloc(f"l0a_psO{i}", [128, 512], F32) for i in range(4)]

    def emit_pair_loads(p):
        b = p % 2
        bk = f"pb{b}"
        sch.op("sync", (lambda p=p, b=b: lambda s: s.dma_start(
            out=qp[:, b, :], in_=q_sp[p]))(), reads=["q_sp"],
            writes=[bk + "q"], dma=True, tag="l0a_qld")
        sch.op("sync", (lambda p=p, b=b: lambda s: s.dma_start(
            out=kp[:, b, :], in_=k_sp[p]))(), reads=["k_sp"],
            writes=[bk + "k"], dma=True, tag="l0a_kld")
        vsrc = bass.AP(v_sp, p * 130,
                       [[H * 65, 128], [128 * H * 65, NT], [1, 130]])
        sch.op("sync", (lambda b=b, vsrc=vsrc: lambda s: s.dma_start(
            out=vp[:, b, :].rearrange("p (t e) -> p t e", t=NT),
            in_=vsrc))(),
            reads=["v_sp"], writes=[bk + "v"], dma=True, tag="l0a_vld")

    emit_pair_loads(0)
    for p in range(NPAIR):
        b = p % 2
        bk = f"pb{b}"
        if p + 1 < NPAIR:
            emit_pair_loads(p + 1)

        for sp in range(4):
            W = 1024 * sp
            imax = 8 * sp + 7
            pend_av = None
            for i in range(imax + 1):
                valid_c = [c for c in range(2)
                           if 128 * i < W + 512 * c + 512]
                pa = i % 2
                ek = (f"e{pa}0", f"e{pa}1")
                c0 = valid_c[0]
                cw = 512 * len(valid_c)
                for hh in range(2):
                    def sc_fn(te, i=i, valid_c=valid_c, W=W, b=b, hh=hh):
                        for c in valid_c:
                            ins = te.matmul(
                                psS[hh][:, 512 * c:512 * c + 512],
                                kp[64 * hh:64 + 64 * hh, b, 128 * i:128 * i + 128],
                                qp[64 * hh:64 + 64 * hh, b, W + 512 * c: W + 512 * c + 512],
                                start=True, stop=True,
                                tile_position=(64 * hh, 0))
                        return ins
                    sch.op("tensor", sc_fn, reads=[bk + "q", bk + "k"],
                           writes=[f"psS{hh}"], tag="l0a_sc")
                    sch.op("scalar", (lambda i=i, c0=c0, cw=cw, pa=pa, hh=hh:
                           lambda se: se.activation(
                               e_sb[:, pa, hh, 512 * c0:512 * c0 + cw],
                               psS[hh][:, 512 * c0:512 * c0 + cw],
                               AF.Exp, scale=0.125))(),
                           reads=[f"psS{hh}"], writes=[ek[hh]], tag="l0a_exp")

                off = 128 * i - W
                if 0 <= off < 1024:
                    def mask_fn(v, off=off, pa=pa):
                        for hh in range(2):
                            if off > 0:
                                v.memset(e_sb[:, pa, hh, 0:off], 0.0)
                            ins = v.tensor_mul(e_sb[:, pa, hh, off:off + 128],
                                               e_sb[:, pa, hh, off:off + 128],
                                               mask_triu[:, :])
                        return ins
                    sch.op("vector", mask_fn, reads=[*ek, "mask_triu"],
                           writes=[*ek], tag="l0a_mask")

                if pend_av is not None:
                    pend_av()

                def mk_av(i=i, valid_c=valid_c, imax=imax, b=b, W=W, pa=pa,
                          ek=ek):
                    def av_fn(te):
                        for c in valid_c:
                            ic_max = min(imax, (W + 512 * c + 511) // 128)
                            for hh in range(2):
                                ins = te.matmul(
                                    psO[2 * c + hh][0:65, :],
                                    vp[:, b, 130 * i + 65 * hh:130 * i + 65 * hh + 65],
                                    e_sb[:, pa, hh, 512 * c:512 * c + 512],
                                    start=(i == 0), stop=(i == ic_max))
                        return ins
                    def emit():
                        sch.op("tensor", av_fn, reads=[*ek, bk + "v"],
                               writes=[f"psO{2 * c + hh}"
                                       for c in valid_c for hh in range(2)],
                               tag="l0a_av")
                    return emit
                pend_av = mk_av()
            pend_av()

            for c in range(2):
                for hh in range(2):
                    pidx = 2 * c + hh
                    zsl = zrow[:, pidx, :]
                    zk = f"zrow{pidx}"
                    sch.op("vector", (lambda pidx=pidx, zsl=zsl: lambda v:
                           v.tensor_copy(zsl, psO[pidx][64:65, :]))(),
                           reads=[f"psO{pidx}"], writes=[zk], tag="l0a_zc")
                    sch.op("scalar", (lambda zsl=zsl: lambda se:
                           se.activation(zsl, zsl, AF.Ln))(),
                           reads=[zk], writes=[zk], tag="l0a_zl")
                    sch.op("scalar", (lambda zsl=zsl: lambda se:
                           se.activation(zsl, zsl, AF.Exp, scale=-1.0))(),
                           reads=[zk], writes=[zk], tag="l0a_ze")
                    sch.op("tensor", (lambda pidx=pidx, zsl=zsl: lambda te:
                           te.matmul(psS[pidx % 2][0:64, 512 * (pidx // 2):512 * (pidx // 2) + 512],
                                     ones64[:, :], zsl,
                                     start=True, stop=True))(),
                           reads=[zk, "ones64"], writes=[f"psS{pidx % 2}"],
                           tag="l0a_zb")
                    sch.op("vector", (lambda pidx=pidx: lambda v:
                           v.tensor_copy(rzb[:, pidx, :],
                                         psS[pidx % 2][0:64, 512 * (pidx // 2):512 * (pidx // 2) + 512]))(),
                           reads=[f"psS{pidx % 2}"], writes=[f"rzb{pidx}"],
                           tag="l0a_zbe")
                    sch.op("vector", (lambda pidx=pidx, c=c, hh=hh: lambda v:
                           v.tensor_mul(
                               opair[64 * hh:64 * hh + 64, 512 * c:512 * c + 512],
                               psO[pidx][0:64, :],
                               rzb[:, pidx, :]))(),
                           reads=[f"psO{pidx}", f"rzb{pidx}"], writes=["opair"],
                           tag="l0a_onorm")
            sch.op("sync", (lambda p=p, W=W: lambda s: s.dma_start(
                out=o_sp[p][:, W:W + 1024], in_=opair[:, :]))(),
                reads=["opair"], writes=["o_sp"], dma=True, tag="l0a_ost")

    sch.barrier()
    stk.close()

    DFF = 4 * D
    NF = DFF // 128
    ga = lambda *a, **k: st["gstk"].enter_context(nc.sbuf_tensor(*a, **k))
    l0fw = {
        "u1_sb": ga("l0f_u1", [128, NC, 256], BF16),
        "v1_sb": ga("l0f_v1", [128, 2, DFF], BF16),
        "u2_sb": ga("l0f_u2", [128, NF, 256], BF16),
        "v2_sb": ga("l0f_v2", [128, 2, D], BF16),
        "b1_sb": ga("l0f_b1", [1, DFF], BF16),
        "b2_sb": ga("l0f_b2", [1, D], BF16),
    }
    st["l0f_w"] = l0fw

    stk = ExitStack()
    alloc = lambda *a, **k: stk.enter_context(nc.sbuf_tensor(*a, **k))
    palloc = lambda *a, **k: stk.enter_context(nc.psum_tensor(*a, **k))
    psC = [palloc(f"l0a_c_ps{i}", [128, 512], F32) for i in range(4)]
    wp_sb = alloc("l0a_wp", [128, NC * D], BF16)
    ofm = alloc("l0a_ofm", [128, 2, NC, 512], BF16)
    xt2 = alloc("l0a_xt2", [128, 2 * D], F32)
    xn = alloc("l0a_xn", [128, 2 * D], F32)

    for i in range(NC):
        sch.op("gpsimd", (lambda i=i: lambda g: g.dma_start(
            out=wp_sb[:, i * D:(i + 1) * D], in_=wprojT[i * 128:(i + 1) * 128, :]))(),
            reads=["wprojT"], writes=["wp_sb"], dma=True, tag="l0a_wpld")

    def emit_ofm_loads(c8):
        b = c8 % 2
        for p in range(NPAIR):
            sch.op("gpsimd", (lambda p=p, c8=c8, b=b: lambda g: g.dma_start(
                out=ofm[:, b, p, :], in_=o_sp[p][:, c8 * 512:(c8 + 1) * 512]))(),
                reads=["o_sp", f"ofm{b}"], writes=[f"ofm{b}"], dma=True,
                tag="l0a_old")

    pend_wl = []

    def fw_load(name, dst, src_ap, key):
        pend_wl.append(lambda: sch.op(
            "gpsimd", (lambda dst=dst, src_ap=src_ap: lambda g:
                       g.dma_start(out=dst, in_=src_ap))(),
            reads=[name], writes=[key], dma=True, tag="l0f_wl"))
    U1, V1, u2r, V2 = T["U1"], T["V1"], T["u2r"], T["V2"]
    for i in range(NC):
        fw_load("U1", l0fw["u1_sb"][:, i, :], U1[i * 128:(i + 1) * 128, :], "u1_sb")
    for rc in range(2):
        fw_load("V1", l0fw["v1_sb"][:, rc, :], V1[rc * 128:(rc + 1) * 128, :], "v1_sb")
        fw_load("V2", l0fw["v2_sb"][:, rc, :], V2[rc * 128:(rc + 1) * 128, :], "v2_sb")
    fw_load("u2r", l0fw["u2_sb"][:, :, :],
            bass.AP(u2r, 0, [[256, 128], [128 * 256, NF], [1, 256]]), "u2_sb")
    fw_load("b1", l0fw["b1_sb"][:, :], T["b1"][:, :], "b1_sb")
    fw_load("b2", l0fw["b2_sb"][:, :], T["b2"][:, :], "b2_sb")
    pend_wl.append(lambda: sch.op(
        "vector", lambda v: v.tensor_scalar_mul(
            l0fw["u2_sb"][:, :, :], l0fw["u2_sb"][:, :, :], 0.5),
        reads=["u2_sb"], writes=["u2_sb"], tag="l0f_u2h"))

    emit_ofm_loads(0)
    for c8 in range(8):
        b = c8 % 2
        if c8 + 1 < 8:
            emit_ofm_loads(c8 + 1)
        for u in pend_wl[:2]:
            u()
        del pend_wl[:2]
        for j in range(4):
            t = c8 * 4 + j
            jb = j % 2
            sch.op("sync", (lambda t=t, jb=jb: lambda s: s.dma_start(
                out=xt2[:, jb * D:(jb + 1) * D], in_=x_in[t]))(),
                reads=[xin_name, f"x2t{jb}"], writes=[f"x2t{jb}"], dma=True,
                tag="l0a_xld")
            for oc in range(2):
                ps = psC[2 * oc + (j % 2)]
                pk = f"psC{2 * oc + (j % 2)}"

                def pr_fn(te, j=j, oc=oc, b=b, ps=ps):
                    for p in range(NPAIR):
                        ins = te.matmul(
                            ps[:, :], ofm[:, b, p, j * 128:(j + 1) * 128],
                            wp_sb[:, p * D + oc * 512: p * D + (oc + 1) * 512],
                            start=(p == 0), stop=(p == NPAIR - 1))
                    return ins
                sch.op("tensor", pr_fn, reads=[f"ofm{b}", "wp_sb"],
                       writes=[pk], tag="l0a_proj")
                sch.op("vector", (lambda oc=oc, jb=jb, ps=ps: lambda v:
                       v.tensor_add(xn[:, jb * D + oc * 512: jb * D + oc * 512 + 512],
                                    ps[:, :],
                                    xt2[:, jb * D + oc * 512: jb * D + oc * 512 + 512]))(),
                       reads=[pk, f"x2t{jb}"], writes=[f"xn{jb}"],
                       tag="l0a_res")
            sch.op("sync", (lambda t=t, jb=jb: lambda s: s.dma_start(
                out=x1[t], in_=xn[:, jb * D:(jb + 1) * D]))(),
                reads=[f"xn{jb}"], writes=[xout_name], dma=True, tag="l0a_xst")

    for u in pend_wl:
        u()
    del pend_wl[:]
    sch.barrier()
    stk.close()



def emit_l0f(st, xin_name, xout_name):
    from contextlib import ExitStack
    nc, sch, T = st["nc"], st["sch"], st["T"]
    x_in, x2 = T[xin_name], T[xout_name]
    ones1, onesrow = st["ones1"], st["onesrow"]
    DFF, R = 4 * D, 256
    NF = DFF // 128

    stk = ExitStack()
    alloc = lambda *a, **k: stk.enter_context(nc.sbuf_tensor(*a, **k))
    palloc = lambda *a, **k: stk.enter_context(nc.psum_tensor(*a, **k))

    w = st["l0f_w"]
    u1_sb, v1_sb, u2_sb = w["u1_sb"], w["v1_sb"], w["u2_sb"]
    v2_sb, b1_sb, b2_sb = w["v2_sb"], w["b1_sb"], w["b2_sb"]
    xt = alloc("l0f_xt", [128, 4, D], F32)
    stats = alloc("l0f_stats", [128, 18], F32)
    lnt = alloc("l0f_lnt", [128, D], BF16)
    ln_ch = alloc("l0f_lnch", [128, NC, 512], BF16)
    p_fm = alloc("l0f_pfm", [128, 2, 512], BF16)
    h_fm = alloc("l0f_hfm", [128, NF, 512], BF16)
    q_fm = alloc("l0f_qfm", [128, 2, 512], BF16)
    sq = alloc("l0f_sq", [128, 2, 512], F32)
    th = alloc("l0f_th", [128, 2, 512], F32)
    xn = alloc("l0f_xn", [128, 2, 512], F32)

    psT = palloc("l0f_psT", [128, 1024], BF16)
    psP = palloc("l0f_psP", [128, 512], F32)
    psH = [palloc(f"l0f_psH{i}", [128, 512], F32) for i in range(2)]
    psQ = palloc("l0f_psQ", [128, 512], F32)
    psO = [palloc(f"l0f_psO{i}", [128, 512], F32) for i in range(2)]

    C1 = 0.044715
    C2 = 0.7978845608028654

    for c8 in range(8):
        emit_ln_chunk(st, xin_name, x_in, c8, xt, stats, lnt, psT, ln_ch,
                      nslot=4)
        for rc in range(2):
            def p_fn(te, rc=rc):
                for i in range(NC):
                    ins = te.matmul(psP[:, :], u1_sb[:, i, rc * 128:(rc + 1) * 128],
                                    ln_ch[:, i, :],
                                    start=(i == 0), stop=(i == NC - 1))
                return ins
            sch.op("tensor", p_fn, reads=["u1_sb", "ln_ch"], writes=["psP"],
                   tag="l0f_p")
            sch.op("vector", (lambda rc=rc: lambda v: v.tensor_copy(
                p_fm[:, rc, :], psP[:, :]))(),
                reads=["psP"], writes=["p_fm"], tag="l0f_pe")
        for fc in range(NF):
            g = fc % 2
            ps = psH[g]
            pk = f"psH{g}"

            def h_fn(te, fc=fc, ps=ps):
                for rc in range(2):
                    te.matmul(ps[:, :], v1_sb[:, rc, fc * 128:(fc + 1) * 128],
                              p_fm[:, rc, :], start=(rc == 0), stop=False)
                return te.matmul(ps[:, :], b1_sb[:, fc * 128:(fc + 1) * 128],
                                 onesrow[:, :], start=False, stop=True)
            sch.op("tensor", h_fn, reads=["v1_sb", "p_fm", "b1_sb", "onesrow"],
                   writes=[pk], tag="l0f_h")
            sch.op("scalar", (lambda ps=ps, g=g: lambda sc: sc.activation(
                sq[:, g, :], ps[:, :], AF.Square))(),
                reads=[pk], writes=[f"sq{g}"], tag="l0f_sq")
            sch.op("vector", (lambda ps=ps, g=g: lambda v: v.tensor_mul(
                sq[:, g, :], sq[:, g, :], ps[:, :]))(),
                reads=[pk, f"sq{g}"], writes=[f"sq{g}"], tag="l0f_cube")
            sch.op("vector", (lambda ps=ps, g=g: lambda v: v.scalar_tensor_tensor(
                th[:, g, :], sq[:, g, :], C1, ps[:, :], ALU.mult, ALU.add))(),
                reads=[pk, f"sq{g}"], writes=[f"th{g}"], tag="l0f_arg")
            sch.op("scalar", (lambda g=g: lambda sc: sc.activation(
                th[:, g, :], th[:, g, :], AF.Tanh, scale=C2))(),
                reads=[f"th{g}"], writes=[f"th{g}"], tag="l0f_tanh")
            sch.op("vector", (lambda ps=ps, g=g, fc=fc: lambda v:
                   v.scalar_tensor_tensor(h_fm[:, fc, :], th[:, g, :], 1.0,
                                          ps[:, :], ALU.add, ALU.mult))(),
                reads=[pk, f"th{g}"], writes=["h_fm"], tag="l0f_hout")
        for rc in range(2):
            def q_fn(te, rc=rc):
                for fc in range(NF):
                    ins = te.matmul(psQ[:, :], u2_sb[:, fc, rc * 128:(rc + 1) * 128],
                                    h_fm[:, fc, :],
                                    start=(fc == 0), stop=(fc == NF - 1))
                return ins
            sch.op("tensor", q_fn, reads=["u2_sb", "h_fm"], writes=["psQ"],
                   tag="l0f_q")
            sch.op("vector", (lambda rc=rc: lambda v: v.tensor_copy(
                q_fm[:, rc, :], psQ[:, :]))(),
                reads=["psQ"], writes=["q_fm"], tag="l0f_qe")
        for j in range(4):
            t = c8 * 4 + j
            for oc in range(2):
                ps = psO[oc]
                pk = f"psO{oc}"

                def o_fn(te, j=j, oc=oc, ps=ps):
                    for rc in range(2):
                        te.matmul(ps[:, :], q_fm[:, rc, j * 128:(j + 1) * 128],
                                  v2_sb[:, rc, oc * 512:(oc + 1) * 512],
                                  start=(rc == 0), stop=False)
                    return te.matmul(ps[:, :], ones1[:, :],
                                     b2_sb[:, oc * 512:(oc + 1) * 512],
                                     start=False, stop=True)
                sch.op("tensor", o_fn, reads=["q_fm", "v2_sb", "b2_sb", "ones1"],
                       writes=[pk], tag="l0f_o")
                sch.op("vector", (lambda j=j, oc=oc, ps=ps: lambda v:
                       v.tensor_add(xn[:, oc, :], ps[:, :],
                                    xt[:, j, oc * 512:(oc + 1) * 512]))(),
                       reads=[pk, f"xt{j}"], writes=[f"xn{oc}"], tag="l0f_res")
            sch.op("sync", (lambda t=t: lambda s: s.dma_start(
                out=x2[t], in_=xn[:, :, :]))(),
                reads=["xn0", "xn1"], writes=[xout_name], dma=True,
                tag="l0f_xst")

    sch.barrier()
    stk.close()
    st["gstk"].close()



TWO_PI = 2.0 * math.pi
MAGIC = 12582912.0


def emit_l1a(st, xin_name, xout_name):
    from contextlib import ExitStack
    nc, sch, T = st["nc"], st["sch"], st["T"]
    x_in, x3 = T[xin_name], T[xout_name]
    ones1, onesrow = st["ones1"], st["onesrow"]

    stk = ExitStack()
    alloc = lambda *a, **k: stk.enter_context(nc.sbuf_tensor(*a, **k))
    palloc = lambda *a, **k: stk.enter_context(nc.psum_tensor(*a, **k))

    wpv_sb = alloc("a_wpv", [128, NC, D], BF16)
    wk_sb = alloc("a_wk", [128, NC, D], BF16)
    wkv_sb = alloc("a_wkv", [128, NC, D], BF16)
    wg1_sb = alloc("a_wg1", [128, 16, D], BF16)
    wb1_sb = alloc("a_wb1", [128, NC, 512], BF16)
    wb2_sb = alloc("a_wb2", [128, 4, 2], BF16)
    wb2d = alloc("a_wb2d", [128, 4], BF16)
    wg2_sb = alloc("a_wg2", [128, NC], BF16)
    wo_sb = alloc("a_wo", [128, NC, D], BF16)
    lng_sb = alloc("a_lng", [128, NC], F32)
    lnb_sb = alloc("a_lnb", [128, NC], BF16)
    brow = alloc("a_brow", [1, 4 * D + 512], BF16)
    bg2r = alloc("a_bg2", [1, 1], F32)
    bb2r = alloc("a_bb2", [1, 2], F32)
    bb2d = alloc("a_bb2d", [1, 1], F32)
    bo_row = alloc("a_borow", [1, D], F32)
    bo_bf = alloc("a_bobf", [1, D], BF16)
    onescol = alloc("a_onescol", [128, 1], BF16)
    onescolf = alloc("a_onescolf", [128, 1], F32)
    onesf = alloc("a_onesf", [1, 128], F32)
    epsrow = alloc("a_epsrow", [1, 1], F32)

    ln_w = alloc("a_lnw", [128, 2, NC, 513], BF16)
    xr = alloc("a_xr", [128, 2, D], F32)
    xg = alloc("a_xg", [128, D], F32)
    xo = alloc("a_xo", [128, D], F32)
    stats = alloc("a_stats", [128, 36], F32)
    lnt = alloc("a_lnt", [128, D], BF16)
    ghb = alloc("a_ghb", [128, 512], BF16)
    kc = alloc("a_kc", [128, 513], BF16)
    ks = alloc("a_ks", [128, 513], BF16)
    tl = {nm: alloc("a_" + nm, [128, 512],
                    BF16 if nm in ("pc", "ps", "kvg") else F32)
          for nm in ["A", "B", "C", "pc", "ps", "kvg",
                     "a1", "a2", "a3", "a4"]}
    TK = {"A": "tA", "B": "tB", "C": "tC", "pc": "tpc",
          "ps": "tps", "kvg": "tkvg", "a1": "ta1", "a2": "ta2",
          "a3": "ta3", "a4": "ta4"}
    blend = alloc("a_blend", [128, NC, 512], BF16)
    g_b = alloc("a_gb", [128, 512], BF16)
    w0_b = alloc("a_w0b", [128, 512], BF16)
    w1_b = alloc("a_w1b", [128, 512], BF16)
    rb_b = alloc("a_rbb", [128, 512], BF16)
    nm_b = alloc("a_nmb", [128, 512], BF16)
    zeros = alloc("a_zeros", [128, 512], F32)
    rows = alloc("a_rows", [1, 7, 512], F32)
    ROW = {"g": 0, "gc": 1, "sq": 2, "rsq": 3, "a0": 4, "d": 5, "bw0": 6,
           "rsq32": 2, "t0": 5, "w0": 4, "w1": 5,
           "mean": 0, "msq": 1, "var": 2, "sd": 3, "rstd": 6, "nm": 4}
    def row(nm):
        return rows[:, ROW[nm], :]
    def rk(nm):
        return f"row{ROW[nm]}"
    car_sc = alloc("a_carsc", [128, NC, 4], F32)
    car_kc = alloc("a_carkc", [128, NC, 2], F32)
    car_g = alloc("a_carg", [1, 1], F32)

    psT = palloc("a_psT", [128, 1024], BF16)
    ps1 = palloc("a_ps1", [128, 512], F32)
    ps2 = palloc("a_ps2", [128, 512], F32)
    ps3 = palloc("a_ps3", [128, 512], F32)
    ps4 = palloc("a_ps4", [128, 512], F32)
    psr1 = palloc("a_psr1", [128, 512], F32)
    psacc = palloc("a_psacc", [128, 512], F32)
    pgh = palloc("a_pgh", [128, 512], F32)

    V, SC, TE, SY, GP = "vector", "scalar", "tensor", "sync", "gpsimd"

    Wpv, Wk, Wkv = T["Wpv"], T["Wk1a"], T["Wkv"]
    Wg1, Wb1, Wo = T["Wg1"], T["Wb1"], T["Wo1a"]
    for i in range(NC):
        for wsb, wdr, key in ((wpv_sb, Wpv, "wpv"), (wk_sb, Wk, "wk"),
                              (wkv_sb, Wkv, "wkv"), (wo_sb, Wo, "wo")):
            sch.op(GP, (lambda wsb=wsb, wdr=wdr, i=i: lambda g: g.dma_start(
                out=wsb[:, i, :], in_=wdr[i * 128:(i + 1) * 128, :]))(),
                reads=[wdr], writes=[key], dma=True, tag="a_wl")
        sch.op(GP, (lambda i=i: lambda g: g.dma_start(
            out=wb1_sb[:, i, :], in_=Wb1[i * 128:(i + 1) * 128, :]))(),
            reads=[Wb1], writes=["wb1"], dma=True, tag="a_wl")
    for i2 in range(16):
        sch.op(GP, (lambda i2=i2: lambda g: g.dma_start(
            out=wg1_sb[:, i2, :], in_=Wg1[i2 * 128:(i2 + 1) * 128, :]))(),
            reads=[Wg1], writes=["wg1"], dma=True, tag="a_wl")
    sch.op(GP, lambda g: g.dma_start(
        out=wb2_sb[:, :, :],
        in_=bass.AP(T["wb2r"], 0, [[2, 128], [128 * 2, 4], [1, 2]])),
        reads=[T["wb2r"]], writes=["wb2"], dma=True, tag="a_wl")
    sch.op(GP, lambda g: g.dma_start(out=wg2_sb[:, :], in_=T["wg2c"][:, :]),
           reads=[T["wg2c"]], writes=["wg2"], dma=True, tag="a_wl")
    sch.op(SY, lambda s: s.dma_start(out=lng_sb[:, :], in_=T["lngc"][:, :]),
           reads=[T["lngc"]], writes=["lng"], dma=True, tag="a_wl")
    sch.op(GP, lambda g: g.dma_start(out=lnb_sb[:, :], in_=T["lnbc"][:, :]),
           reads=[T["lnbc"]], writes=["lnb"], dma=True, tag="a_wl")
    for off, nm in ((0, "bpv"), (D, "bk1a"), (2 * D, "bkv"), (3 * D, "bg1")):
        sch.op(GP, (lambda off=off, nm=nm: lambda g: g.dma_start(
            out=brow[:, off:off + D], in_=T[nm][:, :]))(),
            reads=[T[nm]], writes=["brow"], dma=True, tag="a_wl")
    sch.op(GP, lambda g: g.dma_start(out=brow[:, 4 * D:4 * D + 512],
                                     in_=T["bb1"][:, :]),
           reads=[T["bb1"]], writes=["brow"], dma=True, tag="a_wl")
    sch.op(SY, lambda s: s.dma_start(out=bg2r[:, :], in_=T["bg2"][:, :]),
           reads=[T["bg2"]], writes=["bg2r"], dma=True, tag="a_wl")
    sch.op(SY, lambda s: s.dma_start(out=bb2r[:, :], in_=T["bb2"][:, :]),
           reads=[T["bb2"]], writes=["bb2r"], dma=True, tag="a_wl")
    sch.op(SY, lambda s: s.dma_start(out=bo_row[:, :], in_=T["bo1a"][:, :]),
           reads=[T["bo1a"]], writes=["borow"], dma=True, tag="a_wl")

    sch.op(V, lambda v: v.tensor_sub(bb2d[:, :], bb2r[:, 0:1], bb2r[:, 1:2]),
           reads=["bb2r"], writes=["bb2d"], tag="a_pre")
    sch.op(V, lambda v: v.tensor_sub(wb2d[:, :], wb2_sb[:, :, 0],
                                     wb2_sb[:, :, 1]),
           reads=["wb2"], writes=["wb2d"], tag="a_pre")
    sch.op(V, lambda v: v.memset(onescol[:, :], 1.0), writes=["onescol"])
    sch.op(V, lambda v: v.memset(onescolf[:, :], 1.0), writes=["onescolf"])
    sch.op(V, lambda v: v.memset(onesf[:, :], 1.0), writes=["onesf"])
    sch.op(V, lambda v: v.memset(epsrow[:, :], 1e-5), writes=["epsrow"])
    sch.op(V, lambda v: v.memset(zeros[:, :], 0.0), writes=["zeros"])
    sch.op(V, lambda v: v.memset(ln_w[:, 0, :, 0:1], 0.0), writes=["lnw00"])
    sch.op(V, lambda v: v.memset(kc[:, 0:1], 0.0), writes=["kc0"])
    sch.op(V, lambda v: v.memset(ks[:, 0:1], 0.0), writes=["ks0"])
    sch.op(V, lambda v: v.memset(car_sc[:, :, :], 0.0), writes=["carsc"])
    sch.op(V, lambda v: v.memset(car_kc[:, :, :], 0.0), writes=["carkc"])
    sch.op(V, lambda v: v.memset(car_g[:, :], 0.0), writes=["carg"])

    for oc in range(2):
        def bo_fn(te, oc=oc):
            for i in range(NC):
                ins = te.matmul(psr1[0:1, :], lnb_sb[:, i:i + 1],
                                wo_sb[:, i, oc * 512:(oc + 1) * 512],
                                start=(i == 0), stop=(i == NC - 1))
            return ins
        sch.op(TE, bo_fn, reads=["lnb", "wo"], writes=["psr1"], tag="a_bo")
        sch.op(V, (lambda oc=oc: lambda v: v.tensor_add(
            bo_row[:, oc * 512:(oc + 1) * 512], psr1[0:1, :],
            bo_row[:, oc * 512:(oc + 1) * 512]))(),
            reads=["psr1", "borow"], writes=["borow"], tag="a_bo")
    sch.op(V, lambda v: v.tensor_copy(bo_bf[:, :], bo_row[:, :]),
           reads=["borow"], writes=["bobf"], tag="a_bo")
    for i in range(NC):
        sch.op(V, (lambda i=i: lambda v: v.tensor_scalar_mul(
            wo_sb[:, i, :], wo_sb[:, i, :], lng_sb[:, i:i + 1]))(),
            reads=["wo", "lng"], writes=["wo"], tag="a_wog")


    def lnv_for(wp):
        def lnv(i2):
            if i2 < NC:
                return ln_w[:, wp, i2, 1:513]
            return ln_w[:, wp, i2 - NC, 0:512]
        return lnv

    def sec_next(w):
        units = []
        wp = w % 2
        lnk = f"lnw{wp}"
        lnk0 = f"lnw0{wp}"
        lnv = lnv_for(wp)
        if w > 0:
            units.append(lambda wp=wp, lnk0=lnk0: sch.op(
                V, (lambda wp=wp: lambda v: v.tensor_copy(
                    ln_w[:, wp, :, 0:1], ln_w[:, 1 - wp, :, 512:513]))(),
                reads=[f"lnw{1 - wp}"], writes=[lnk0], tag="a_lnc"))
        emit_ln_chunk(st, xin_name, x_in, w, xr, stats, lnt, psT,
                      ln_w[:, wp, :, :], ln_col0=1, ln_keys=[lnk], nslot=2,
                      units=units)

        for Dc in range(NC):
            def gates_unit(Dc=Dc, lnv=lnv, lnk=lnk, lnk0=lnk0):
                def gh_fn(te, Dc=Dc, lnv=lnv):
                    for i2 in range(16):
                        te.matmul(pgh[:, :],
                                  wg1_sb[:, i2, Dc * 128:(Dc + 1) * 128],
                                  lnv(i2), start=(i2 == 0), stop=False)
                    return te.matmul(
                        pgh[:, :],
                        brow[:, 3 * D + Dc * 128:3 * D + (Dc + 1) * 128],
                        onesrow[:, :], start=False, stop=True)
                sch.op(TE, gh_fn, reads=["wg1", lnk, lnk0, "brow", "onesrow"],
                       writes=["pgh"], tag="a_gh")
                sch.op(SC, lambda sc: sc.activation(ghb[:, :], pgh[:, :],
                                                    AF.Gelu),
                       reads=["pgh"], writes=["ghb"], tag="a_ghg")
                sch.op(TE, (lambda Dc=Dc: lambda te: te.matmul(
                    psacc[64:65, :], wg2_sb[:, Dc:Dc + 1], ghb[:, :],
                    start=(Dc == 0), stop=(Dc == NC - 1)))(),
                    reads=["wg2", "ghb"], writes=["psum_g"], tag="a_gacc")
            units.append(gates_unit)

        def gates_rows():
            sch.op(SC, lambda sc: sc.activation(
                row("g"), psacc[64:65, :], AF.Sigmoid, bias=bg2r[0:1, 0:1]),
                reads=["psum_g", "bg2r"], writes=[rk("g")], tag="a_sig")
            sch.op(V, lambda v: v.tensor_tensor_scan(
                row("gc"), row("g"), zeros[0:1, :], car_g[0:1, 0:1],
                ALU.add, ALU.add), reads=[rk("g"), "zeros", "carg"],
                writes=[rk("gc")], tag="a_gc")
            sch.op(V, lambda v: v.tensor_copy(car_g[:, :],
                                              rows[:, ROW["gc"], 511:512]),
                   reads=[rk("gc")], writes=["carg"], tag="a_gc2")
            sch.op(V, lambda v: v.tensor_scalar_max(row("gc"), row("gc"), 1.0),
                   reads=[rk("gc")], writes=[rk("gc")], tag="a_gc3")
            sch.op(SC, lambda sc: sc.activation(row("sq"), row("gc"), AF.Sqrt),
                   reads=[rk("gc")], writes=[rk("sq")], tag="a_gc4")
            sch.op(V, lambda v: v.reciprocal(row("rsq"), row("sq")),
                   reads=[rk("sq")], writes=[rk("rsq")], tag="a_gc5")
        units.append(gates_rows)

        for Dp in range(4):
            def bh_unit(Dp=Dp, lnv=lnv, lnk=lnk):
                def bh_fn(te, Dp=Dp, lnv=lnv):
                    for i in range(NC):
                        te.matmul(pgh[:, :],
                                  wb1_sb[:, i, Dp * 128:(Dp + 1) * 128],
                                  lnv(i), start=(i == 0), stop=False)
                    return te.matmul(
                        pgh[:, :],
                        brow[:, 4 * D + Dp * 128:4 * D + (Dp + 1) * 128],
                        onesrow[:, :], start=False, stop=True)
                sch.op(TE, bh_fn, reads=["wb1", lnk, "brow", "onesrow"],
                       writes=["pgh"], tag="a_bh")
                sch.op(SC, lambda sc: sc.activation(ghb[:, :], pgh[:, :],
                                                    AF.Gelu),
                       reads=["pgh"], writes=["ghb"], tag="a_bhg")
                sch.op(TE, (lambda Dp=Dp: lambda te: te.matmul(
                    psacc[64:65, :], wb2d[:, Dp:Dp + 1], ghb[:, :],
                    start=(Dp == 0), stop=(Dp == 3)))(),
                    reads=["wb2d", "ghb"], writes=["psum_g"], tag="a_bacc")
            units.append(bh_unit)

        def bh_rows():
            sch.op(SC, lambda sc: sc.activation(
                row("bw0"), psacc[64:65, :], AF.Sigmoid,
                bias=bb2d[0:1, 0:1]),
                reads=["psum_g", "bb2d"], writes=[rk("bw0")], tag="a_bw3")
            sch.op(V, lambda v: v.tensor_scalar_mul(row("rsq32"), row("rsq"),
                                                    1.0 / 32.0),
                   reads=[rk("rsq")], writes=[rk("rsq32")], tag="a_bw5")
            sch.op(V, lambda v: v.tensor_mul(row("t0"), row("bw0"),
                                             row("rsq32")),
                   reads=[rk("bw0"), rk("rsq32")], writes=[rk("t0")],
                   tag="a_bw6")
            sch.op(V, lambda v: v.tensor_scalar_mul(row("w0"), row("bw0"),
                                                    1.0 / 32.0),
                   reads=[rk("bw0")], writes=[rk("w0")], tag="a_bw4")
            sch.op(V, lambda v: v.tensor_sub(row("w1"), row("rsq32"),
                                             row("t0")),
                   reads=[rk("rsq32"), rk("t0")], writes=[rk("w1")],
                   tag="a_bw7")
        units.append(bh_rows)

        def bcast():
            for rnm, dst, dk in (("g", g_b, "g_b"), ("w0", w0_b, "w0_b"),
                                 ("w1", w1_b, "w1_b")):
                sch.op(TE, (lambda rnm=rnm: lambda te: te.matmul(
                    pgh[:, :], onesf[:, :], row(rnm),
                    start=True, stop=True))(),
                    reads=[rk(rnm), "onesf"], writes=["pgh"], tag="a_bc")
                sch.op(V, (lambda dst=dst: lambda v: v.tensor_copy(
                    dst[:, :], pgh[:, :]))(),
                    reads=["pgh"], writes=[dk], tag="a_bc2")
        units.append(bcast)
        return units

    def emit_E(w, Dc):
        W = 512 * w
        wp = w % 2
        lnk = f"lnw{wp}"
        lnv = lnv_for(wp)
        sch.op(SY, (lambda Dc=Dc, W=W: lambda s: s.dma_start(
            out=tl["pc"][:, :], in_=T["pc_sp"][Dc][:, W:W + 512]))(),
            reads=[T["pc_sp"]], writes=["tpc"], dma=True, tag="a_pcl")
        sch.op(SY, (lambda Dc=Dc, W=W: lambda s: s.dma_start(
            out=tl["ps"][:, :], in_=T["ps_sp"][Dc][:, W:W + 512]))(),
            reads=[T["ps_sp"]], writes=["tps"], dma=True, tag="a_psl")

        def proj(te, wsb, boff, ps, Dc, lnv=lnv):
            for i in range(NC):
                te.matmul(ps[:, :], wsb[:, i, Dc * 128:(Dc + 1) * 128],
                          lnv(i), start=(i == 0), stop=False)
            return te.matmul(ps[:, :],
                             brow[:, boff + Dc * 128:boff + (Dc + 1) * 128],
                             onesrow[:, :], start=False, stop=True)
        pv = ps1 if Dc % 2 == 0 else ps3
        pvk = "ps1" if Dc % 2 == 0 else "ps3"
        kv = ps2 if Dc % 2 == 0 else ps4
        kvk = "ps2" if Dc % 2 == 0 else "ps4"
        sch.op(TE, (lambda Dc=Dc, pv=pv, proj=proj: lambda te: proj(
            te, wpv_sb, 0, pv, Dc))(),
            reads=["wpv", lnk, "brow", "onesrow"], writes=[pvk],
            tag="a_pv")
        sch.op(TE, (lambda Dc=Dc, kv=kv, proj=proj: lambda te: proj(
            te, wkv_sb, 2 * D, kv, Dc))(),
            reads=["wkv", lnk, "brow", "onesrow"], writes=[kvk],
            tag="a_kv")
        sch.op(TE, (lambda Dc=Dc, proj=proj: lambda te: proj(
            te, wk_sb, D, psr1, Dc))(),
            reads=["wk", lnk, "brow", "onesrow"], writes=["psr1"],
            tag="a_kp")
        sch.op(SC, lambda sc: sc.activation(tl["C"][:, :], psr1[:, :],
                                            AF.Tanh),
               reads=["psr1"], writes=["tC"], tag="a_tanh")
        if w > 0:
            sch.op(GP, (lambda Dc=Dc: lambda v: v.tensor_copy(
                kc[:, 0:1], car_kc[:, Dc, 0:1]))(),
                reads=["carkc"], writes=["kc0"], tag="a_kcc")
            sch.op(GP, (lambda Dc=Dc: lambda v: v.tensor_copy(
                ks[:, 0:1], car_kc[:, Dc, 1:2]))(),
                reads=["carkc"], writes=["ks0"], tag="a_ksc")
        sch.op(SC, lambda sc: sc.activation(ks[:, 1:513], tl["C"][:, :],
                                            AF.Sin, scale=math.pi),
               reads=["tC"], writes=["ksm"], tag="a_ksin")
        sch.op(SC, lambda sc: sc.activation(tl["A"][:, :], tl["C"][:, :],
                                            AF.Sin, scale=math.pi / 2),
               reads=["tC"], writes=["tA"], tag="a_ksin2")
        sch.op(GP, lambda v: v.tensor_mul(tl["A"][:, :], tl["A"][:, :],
                                          tl["A"][:, :]),
               reads=["tA"], writes=["tA"], tag="a_ksq")
        sch.op(GP, lambda v: v.tensor_scalar(kc[:, 1:513], tl["A"][:, :],
                                            -2.0, 1.0, ALU.mult, ALU.add),
               reads=["tA"], writes=["kcm"], tag="a_kcos")
        sch.op(V, (lambda pv=pv, pvk=pvk: lambda v: v.tensor_mul(
            tl["a1"][:, :], tl["pc"][:, :], pv[:, :]))(),
               reads=["tpc", pvk], writes=["ta1"], tag="a_m1")
        sch.op(V, (lambda pv=pv, pvk=pvk: lambda v: v.tensor_mul(
            tl["a2"][:, :], tl["ps"][:, :], pv[:, :]))(),
               reads=["tps", pvk], writes=["ta2"], tag="a_m2")
        sch.op(V, (lambda kv=kv, kvk=kvk: lambda v: v.tensor_mul(
            tl["kvg"][:, :], kv[:, :], g_b[:, :]))(),
               reads=[kvk, "g_b"], writes=["tkvg"], tag="a_m3")
        sch.op(GP, lambda v: v.tensor_mul(tl["a3"][:, :], kc[:, 0:512],
                                         tl["kvg"][:, :]),
               reads=["kcm", "kc0", "tkvg"], writes=["ta3"], tag="a_m4")
        sch.op(GP, lambda v: v.tensor_mul(tl["a4"][:, :], ks[:, 0:512],
                                         tl["kvg"][:, :]),
               reads=["ksm", "ks0", "tkvg"], writes=["ta4"], tag="a_m5")
        sch.op(V, (lambda Dc=Dc: lambda v: v.tensor_copy(
            car_kc[:, Dc, 0:1], kc[:, 512:513]))(),
            reads=["kcm"], writes=["carkc"], tag="a_kcs")
        sch.op(V, (lambda Dc=Dc: lambda v: v.tensor_copy(
            car_kc[:, Dc, 1:2], ks[:, 512:513]))(),
            reads=["ksm"], writes=["carkc"], tag="a_kss")
        for si, anm in enumerate(["a1", "a2", "a3", "a4"]):
            sch.op(V, (lambda si=si, anm=anm, Dc=Dc: lambda v:
                   v.tensor_tensor_scan(
                       tl[anm][:, :], tl[anm][:, :], zeros[:, :],
                       car_sc[:, Dc, si:si + 1], ALU.add, ALU.add))(),
                reads=[f"t{anm}", "zeros", "carsc"], writes=[f"t{anm}"],
                tag="a_scan")
            sch.op(V, (lambda si=si, anm=anm, Dc=Dc: lambda v:
                   v.tensor_copy(car_sc[:, Dc, si:si + 1],
                                 tl[anm][:, 511:512]))(),
                reads=[f"t{anm}"], writes=["carsc"], tag="a_scar")
        sch.op(V, lambda v: v.tensor_mul(tl["A"][:, :], tl["pc"][:, :],
                                         tl["a1"][:, :]),
               reads=["tpc", "ta1"], writes=["tA"], tag="a_c1")
        sch.op(V, lambda v: v.tensor_mul(tl["C"][:, :], tl["ps"][:, :],
                                         tl["a2"][:, :]),
               reads=["tps", "ta2"], writes=["tC"], tag="a_c2")
        sch.op(V, lambda v: v.tensor_add(tl["A"][:, :], tl["A"][:, :],
                                         tl["C"][:, :]),
               reads=["tA", "tC"], writes=["tA"], tag="a_c3")
        sch.op(GP, lambda v: v.tensor_mul(tl["B"][:, :], kc[:, 1:513],
                                         tl["a3"][:, :]),
               reads=["kcm", "ta3"], writes=["tB"], tag="a_c4")
        sch.op(GP, lambda v: v.tensor_mul(tl["C"][:, :], ks[:, 1:513],
                                         tl["a4"][:, :]),
               reads=["ksm", "ta4"], writes=["tC"], tag="a_c5")
        sch.op(GP, lambda v: v.tensor_add(tl["B"][:, :], tl["B"][:, :],
                                         tl["C"][:, :]),
               reads=["tB", "tC"], writes=["tB"], tag="a_c6")
        sch.op(V, lambda v: v.tensor_mul(tl["A"][:, :], tl["A"][:, :],
                                         w0_b[:, :]),
               reads=["tA", "w0_b"], writes=["tA"], tag="a_c7")
        sch.op(GP, lambda v: v.tensor_mul(tl["B"][:, :], tl["B"][:, :],
                                         w1_b[:, :]),
               reads=["tB", "w1_b"], writes=["tB"], tag="a_c8")
        sch.op(V, (lambda Dc=Dc: lambda v: v.tensor_add(
            blend[:, Dc, :], tl["A"][:, :], tl["B"][:, :]))(),
            reads=["tA", "tB"], writes=["blend"], tag="a_c9")
        def stats_unit(Dc=Dc):
            sch.op(TE, (lambda Dc=Dc: lambda te: te.matmul(
                psacc[0:1, :], onescol[:, :], blend[:, Dc, :],
                start=(Dc == 0), stop=(Dc == NC - 1)))(),
                reads=["blend", "onescol"], writes=["psum_s"], tag="a_sum")
            sq = tl["A"] if Dc % 2 == 0 else tl["C"]
            sqk = "tA" if Dc % 2 == 0 else "tC"
            sch.op(GP, (lambda Dc=Dc, sq=sq: lambda g: g.tensor_mul(
                sq[:, :], blend[:, Dc, :], blend[:, Dc, :]))(),
                reads=["blend"], writes=[sqk], tag="a_bsq")
            sch.op(TE, (lambda Dc=Dc, sq=sq, sqk=sqk: lambda te: te.matmul(
                psacc[32:33, :], onescolf[:, :], sq[:, :],
                start=(Dc == 0), stop=(Dc == NC - 1)))(),
                reads=[sqk, "onescolf"], writes=["psum_q"], tag="a_ssq")
        return stats_unit

    def emit_lnrows(w):
        sch.op(V, lambda v: v.tensor_scalar_mul(row("mean"), psacc[0:1, :],
                                                1.0 / D),
               reads=["psum_s"], writes=[rk("mean")], tag="a_ln1")
        sch.op(V, lambda v: v.tensor_scalar_mul(row("msq"), psacc[32:33, :],
                                                1.0 / D),
               reads=["psum_q"], writes=[rk("msq")], tag="a_ln2")
        sch.op(V, lambda v: v.tensor_mul(row("t0"), row("mean"), row("mean")),
               reads=[rk("mean")], writes=[rk("t0")], tag="a_ln3")
        sch.op(V, lambda v: v.tensor_sub(row("var"), row("msq"), row("t0")),
               reads=[rk("msq"), rk("t0")], writes=[rk("var")], tag="a_ln4")
        sch.op(SC, lambda sc: sc.activation(row("sd"), row("var"), AF.Sqrt,
                                            bias=epsrow[0:1, 0:1]),
               reads=[rk("var"), "epsrow"], writes=[rk("sd")], tag="a_ln5")
        sch.op(V, lambda v: v.reciprocal(row("rstd"), row("sd")),
               reads=[rk("sd")], writes=[rk("rstd")], tag="a_ln6")
        sch.op(V, lambda v: v.scalar_tensor_tensor(
            row("nm"), row("mean"), -1.0, row("rstd"), ALU.mult, ALU.mult),
            reads=[rk("mean"), rk("rstd")], writes=[rk("nm")], tag="a_ln7")
        for rnm, dst, dk in (("rstd", rb_b, "rb_b"), ("nm", nm_b, "nm_b")):
            sch.op(TE, (lambda rnm=rnm: lambda te: te.matmul(
                pgh[:, :], onesf[:, :], row(rnm), start=True, stop=True))(),
                reads=[rk(rnm), "onesf"], writes=["pgh"], tag="a_lnb1")
            sch.op(V, (lambda dst=dst: lambda v: v.tensor_copy(
                dst[:, :], pgh[:, :]))(),
                reads=["pgh"], writes=[dk], tag="a_lnb2")

    def emit_apply(w):
        for Dc in range(NC):
            sch.op(GP, (lambda Dc=Dc: lambda g: g.tensor_mul(
                ghb[:, :], blend[:, Dc, :], rb_b[:, :]))(),
                reads=["blend", "rb_b"], writes=["ghb"], tag="a_ap1")
            sch.op(V, (lambda Dc=Dc: lambda v: v.tensor_add(
                blend[:, Dc, :], ghb[:, :], nm_b[:, :]))(),
                reads=["ghb", "nm_b"], writes=["blend"], tag="a_ap2")

    def emit_G(w):
        for j in range(4):
            t = 4 * w + j
            sch.op(SY, (lambda t=t: lambda s: s.dma_start(
                out=xg[:, :], in_=x_in[t]))(),
                reads=[xin_name], writes=["xg"], dma=True, tag="a_xld")
            def bn_fn(v):
                v.bn_stats(stats[:, 18:24], xg[:, 0:512])
                return v.bn_stats(stats[:, 24:30], xg[:, 512:1024])
            sch.op(V, bn_fn, reads=["xg"], writes=["st2_bn"], tag="a_gbn")
            sch.op(V, lambda v: v.bn_aggr(stats[:, 30:32], stats[:, 18:30]),
                   reads=["st2_bn"], writes=["st2_mv"], tag="a_gag")
            sch.op(SC, lambda sc: sc.activation(
                stats[:, 32:33], stats[:, 31:32], AF.Sqrt,
                bias=st["epsT"][:, :]),
                reads=["st2_mv", "epsT"], writes=["st2_sd"], tag="a_gsq")
            sch.op(V, lambda v: v.reciprocal(stats[:, 33:34], stats[:, 32:33]),
                   reads=["st2_sd"], writes=["st2_rstd"], tag="a_grc")
            sch.op(V, lambda v: v.scalar_tensor_tensor(
                stats[:, 34:35], stats[:, 30:31], -1.0, stats[:, 33:34],
                ALU.mult, ALU.mult),
                reads=["st2_mv", "st2_rstd"], writes=["st2_nm"], tag="a_gnm")
            for oc in range(2):
                def wo_fn(te, j=j, oc=oc):
                    for i in range(NC):
                        te.matmul(pgh[:, :],
                                  blend[:, i, j * 128:(j + 1) * 128],
                                  wo_sb[:, i, oc * 512:(oc + 1) * 512],
                                  start=(i == 0), stop=False)
                    return te.matmul(pgh[:, :], ones1[:, :],
                                     bo_bf[:, oc * 512:(oc + 1) * 512],
                                     start=False, stop=True)
                xok = f"xo{oc}"
                sch.op(TE, wo_fn, reads=["blend", "wo", "bobf", "ones1"],
                       writes=["pgh"], tag="a_wo2")
                sch.op(SC, (lambda oc=oc: lambda sc: sc.activation(
                    xo[:, oc * 512:(oc + 1) * 512], pgh[:, :], AF.Identity,
                    bias=stats[:, 34:35]))(),
                    reads=["pgh", "st2_nm"], writes=[xok], tag="a_res2")
                sch.op(V, (lambda oc=oc: lambda v: v.tensor_add(
                    xo[:, oc * 512:(oc + 1) * 512],
                    xo[:, oc * 512:(oc + 1) * 512],
                    xg[:, oc * 512:(oc + 1) * 512]))(),
                    reads=[xok, "xg"], writes=[xok], tag="a_res")
                sch.op(V, (lambda oc=oc: lambda v: v.scalar_tensor_tensor(
                    xo[:, oc * 512:(oc + 1) * 512],
                    xg[:, oc * 512:(oc + 1) * 512],
                    stats[:, 33:34],
                    xo[:, oc * 512:(oc + 1) * 512], ALU.mult, ALU.add))(),
                    reads=["xg", "st2_rstd", xok], writes=[xok],
                    tag="a_res3")
                sch.op(SY, (lambda t=t, oc=oc: lambda s: s.dma_start(
                    out=x3[t][:, oc * 512:(oc + 1) * 512],
                    in_=xo[:, oc * 512:(oc + 1) * 512]))(),
                    reads=[xok], writes=[xout_name], dma=True, tag="a_xst")

    nxt = sec_next(0)
    for u in nxt:
        u()
    for w in range(8):
        nxt = sec_next(w + 1) if w < 7 else []
        pos = 0
        stats_units = []
        for Dc in range(NC):
            take = (len(nxt) * Dc) // NC - pos
            for u in nxt[pos:pos + take]:
                u()
            pos += take
            stats_units.append(emit_E(w, Dc))
        for u in nxt[pos:]:
            u()
        pos = len(nxt)
        for u in stats_units:
            u()
        emit_lnrows(w)
        emit_apply(w)
        emit_G(w)

    sch.barrier()
    stk.close()



def emit_l1f(st, xin_name, xout_name):
    from contextlib import ExitStack
    nc, sch, T = st["nc"], st["sch"], st["T"]
    x_in, xout = T[xin_name], T[xout_name]
    q_sp, k_sp, v_sp, o_sp = T["q_sp"], T["k_sp"], T["v_sp"], T["o_sp"]

    stk = ExitStack()
    alloc = lambda *a, **k: stk.enter_context(nc.sbuf_tensor(*a, **k))
    palloc = lambda *a, **k: stk.enter_context(nc.psum_tensor(*a, **k))
    wq_sb = alloc("f_wq", [128, NC, D], BF16)
    wk_sb = alloc("f_wk", [128, NC, D], BF16)
    wv_sb = alloc("f_wv", [128, NC, D], BF16)
    xt = alloc("f_xt", [128, 2, D], F32)
    stats = alloc("f_stats", [128, 18], F32)
    lnt = alloc("f_lnt", [128, D], BF16)
    ln_ch = alloc("f_lnch", [128, NC, 512], BF16)
    qkv_e = alloc("f_qkve", [128, 2, 512], BF16)
    vsb = alloc("f_vsb", [128, 2, D], BF16)
    psT = palloc("f_psT", [128, 1024], BF16)
    psA = palloc("f_psA", [128, 512], F32)
    psB = palloc("f_psB", [128, 512], F32)

    for i in range(NC):
        for wsb, nm, key in ((wq_sb, "Wq1f", "wq"), (wk_sb, "Wk1f", "wk"),
                             (wv_sb, "Wv1f", "wv")):
            sch.op("gpsimd", (lambda wsb=wsb, nm=nm, i=i: lambda g:
                   g.dma_start(out=wsb[:, i, :],
                               in_=T[nm][i * 128:(i + 1) * 128, :]))(),
                reads=[T[nm]], writes=[key], dma=True, tag="f_wl")

    for c8 in range(8):
        emit_ln_chunk(st, xin_name, x_in, c8, xt, stats, lnt, psT, ln_ch,
                      nslot=2)
        for oc in range(16):
            ps = psA if oc % 2 == 0 else psB
            pk = "psA" if oc % 2 == 0 else "psB"
            wsb = wq_sb if oc < 8 else wk_sb
            wkey = "wq" if oc < 8 else "wk"
            p = oc % 8

            def qk_fn(te, p=p, ps=ps, wsb=wsb):
                for i in range(NC):
                    ins = te.matmul(ps[:, :], wsb[:, i, p * 128:(p + 1) * 128],
                                    ln_ch[:, i, :],
                                    start=(i == 0), stop=(i == NC - 1))
                return ins
            sch.op("tensor", qk_fn, reads=[wkey, "ln_ch"], writes=[pk],
                   tag="f_qk")
            eb = qkv_e[:, oc % 2, :]
            ek = f"qkve{oc % 2}"
            sch.op("vector", (lambda ps=ps, eb=eb: lambda v: v.tensor_copy(
                eb, ps[:, :]))(), reads=[pk], writes=[ek],
                tag="f_qkev")
            dst = q_sp if oc < 8 else k_sp
            sch.op("sync", (lambda dst=dst, p=p, c8=c8, eb=eb: lambda s:
                   s.dma_start(out=dst[p][:, c8 * 512:(c8 + 1) * 512], in_=eb))(),
                   reads=[ek], writes=["q_sp" if oc < 8 else "k_sp"],
                   dma=True, tag="f_qkst")
        for j in range(4):
            t = c8 * 4 + j
            for jc in range(2):
                ps = psA if jc == 0 else psB
                pk = "psA" if jc == 0 else "psB"

                def v_fn(te, j=j, jc=jc, ps=ps):
                    for i in range(NC):
                        ins = te.matmul(
                            ps[:, :], ln_ch[:, i, j * 128:(j + 1) * 128],
                            wv_sb[:, i, jc * 512:(jc + 1) * 512],
                            start=(i == 0), stop=(i == NC - 1))
                    return ins
                sch.op("tensor", v_fn, reads=["wv", "ln_ch"], writes=[pk],
                       tag="f_v")
                vk = f"vsb{j % 2}"
                sch.op("vector", (lambda ps=ps, j=j, jc=jc: lambda v:
                       v.tensor_copy(vsb[:, j % 2, jc * 512:(jc + 1) * 512],
                                     ps[:, :]))(),
                       reads=[pk], writes=[vk], tag="f_vev")
            sch.op("sync", (lambda t=t, j=j: lambda s: s.dma_start(
                out=v_sp[t][:, 0:D], in_=vsb[:, j % 2, :]))(),
                reads=[f"vsb{j % 2}"], writes=["v_sp"], dma=True, tag="f_vst")

    sch.barrier()
    stk.close()

    stk = ExitStack()
    alloc = lambda *a, **k: stk.enter_context(nc.sbuf_tensor(*a, **k))
    palloc = lambda *a, **k: stk.enter_context(nc.psum_tensor(*a, **k))
    qp = alloc("f_qp", [128, 2, S], BF16)
    kp = alloc("f_kp", [128, 2, S], BF16)
    vp = alloc("f_vp", [128, 2, NT * 128], BF16)
    e_sb = alloc("f_e", [128, 2, 2, S], BF16)
    zparts = alloc("f_zp", [128, 2, 2, 4], F32)
    zrec = alloc("f_zr", [128, 2, 2], F32)
    vsc = alloc("f_vsc", [128, 2, 128], BF16)
    acc = alloc("f_acc", [128, 6, 512], F32)
    opair = alloc("f_opair", [128, 2, 512], BF16)

    psS = [palloc(f"f_psS{i}", [128, 1024], F32) for i in range(2)]
    psSS = [palloc(f"f_psSS{i}", [128, 512], F32) for i in range(2)]
    psO = [palloc(f"f_psO{i}", [128, 512], F32) for i in range(2)]

    def emit_pair_loads(p):
        b = p % 2
        bk = f"pb{b}"
        sch.op("sync", (lambda p=p, b=b: lambda s: s.dma_start(
            out=qp[:, b, :], in_=q_sp[p]))(), reads=["q_sp"],
            writes=[bk + "q"], dma=True, tag="f_qld")
        sch.op("sync", (lambda p=p, b=b: lambda s: s.dma_start(
            out=kp[:, b, :], in_=k_sp[p]))(), reads=["k_sp"],
            writes=[bk + "k"], dma=True, tag="f_kld")
        vsrc = bass.AP(v_sp, p * 128,
                       [[H * 65, 128], [128 * H * 65, NT], [1, 128]])
        sch.op("sync", (lambda b=b, vsrc=vsrc: lambda s: s.dma_start(
            out=vp[:, b, :].rearrange("p (t e) -> p t e", t=NT),
            in_=vsrc))(),
            reads=["v_sp"], writes=[bk + "v"], dma=True, tag="f_vld")

    emit_pair_loads(0)
    for p in range(NPAIR):
        b = p % 2
        bk = f"pb{b}"
        if p + 1 < NPAIR:
            emit_pair_loads(p + 1)

        sch.op("vector", lambda v: v.memset(acc[:, :, :], 0.0),
               writes=["facc"], tag="f_acc0")

        def make_av_parts(i):
            pa = i % 2
            ek = (f"e{pa}0", f"e{pa}1")

            def psum_part(i=i, pa=pa, ek=ek):
                def av_fn(te):
                    for c in range(2):
                        bank = psO[c]
                        for hh in range(2):
                            ins = te.matmul(
                                bank[64 * hh:64 * hh + 64, :],
                                vsc[:, pa, 64 * hh:64 * hh + 64],
                                e_sb[:, pa, hh, 512 * c:512 * c + 512],
                                start=(i == 0), stop=(i == NT - 1),
                                tile_position=(0, 64 * hh))
                    return ins
                sch.op("tensor", av_fn, reads=[f"vsc{pa}", *ek],
                       writes=[f"psO{c}" for c in range(2)], tag="f_av")

            def scratch_part(cs, i=i, pa=pa, ek=ek):
                for c in cs:
                    sb = c % 2
                    def av2_fn(te, c=c, sb=sb):
                        for hh in range(2):
                            ins = te.matmul(
                                psSS[sb][64 * hh:64 * hh + 64, :],
                                vsc[:, pa, 64 * hh:64 * hh + 64],
                                e_sb[:, pa, hh, 512 * c:512 * c + 512],
                                start=True, stop=True,
                                tile_position=(0, 64 * hh))
                        return ins
                    sch.op("tensor", av2_fn, reads=[f"vsc{pa}", *ek],
                           writes=[f"psSS{sb}"], tag="f_av2")
                    sch.op("vector", (lambda c=c, sb=sb: lambda v:
                           v.tensor_add(acc[:, c - 2, :], acc[:, c - 2, :],
                                        psSS[sb][:, :]))(),
                        reads=[f"psSS{sb}", "facc"], writes=["facc"],
                        tag="f_av3")

            return [lambda: psum_part(),
                    lambda: scratch_part((2, 3)),
                    lambda: scratch_part((4, 5)),
                    lambda: scratch_part((6, 7))]

        pending = []
        for i in range(NT):
            pa = i % 2
            ek = (f"e{pa}0", f"e{pa}1")
            for j in range(4):
                for hh in range(2):
                    def sc2_fn(te, i=i, j=j, b=b, hh=hh):
                        for q2 in range(2):
                            ins = te.matmul(
                                psS[hh][:, q2 * 512:(q2 + 1) * 512],
                                kp[64 * hh:64 + 64 * hh, b,
                                   128 * i:128 * i + 128],
                                qp[64 * hh:64 + 64 * hh, b,
                                   1024 * j + 512 * q2:1024 * j + 512 * q2 + 512],
                                start=True, stop=True,
                                tile_position=(64 * hh, 0))
                        return ins
                    sch.op("tensor", sc2_fn,
                           reads=[bk + "q", bk + "k"], writes=[f"psS{hh}"],
                           tag="f_sc")
                    sch.op("scalar", (lambda j=j, hh=hh, pa=pa:
                           lambda se: se.activation(
                               e_sb[:, pa, hh, 1024 * j:1024 * j + 1024],
                               psS[hh][:, :], AF.Exp, scale=1.0 / 32.0,
                               accum_out=zparts[:, pa, hh, j:j + 1]))(),
                           reads=[f"psS{hh}"], writes=[ek[hh], f"zp{pa}"],
                           tag="f_exp")
                if pending:
                    pending.pop(0)()
            def z_fn(v, pa=pa):
                v.reduce_sum(zrec[:, pa, 0:1], zparts[:, pa, 0, 0:4], AX.X)
                return v.reduce_sum(zrec[:, pa, 1:2], zparts[:, pa, 1, 0:4],
                                    AX.X)
            sch.op("vector", z_fn, reads=[f"zp{pa}"], writes=[f"zrec{pa}"],
                   tag="f_z")
            sch.op("vector", (lambda pa=pa: lambda v: v.reciprocal(
                zrec[:, pa, :], zrec[:, pa, :]))(),
                reads=[f"zrec{pa}"], writes=[f"zrec{pa}"], tag="f_zr")
            def vs_fn(v, i=i, b=b, pa=pa):
                v.tensor_scalar_mul(vsc[:, pa, 0:64],
                                    vp[:, b, 128 * i:128 * i + 64],
                                    zrec[:, pa, 0:1])
                return v.tensor_scalar_mul(
                    vsc[:, pa, 64:128],
                    vp[:, b, 128 * i + 64:128 * i + 128],
                    zrec[:, pa, 1:2])
            sch.op("vector", vs_fn, reads=[f"zrec{pa}", bk + "v"],
                   writes=[f"vsc{pa}"], tag="f_vs")
            assert not pending
            pending = make_av_parts(i)
        for part in pending:
            part()
        for c in range(2):
            def ev_fn(v, c=c):
                return v.tensor_copy(
                    opair[:, c % 2, 0:512], psO[c][:, :])
            sch.op("vector", ev_fn, reads=[f"psO{c}"],
                   writes=[f"op{c % 2}"], tag="f_ev")
            sch.op("sync", (lambda p=p, c=c: lambda s: s.dma_start(
                out=o_sp[p][:, 512 * c:512 * c + 512],
                in_=opair[:, c % 2, 0:512]))(),
                reads=[f"op{c % 2}"], writes=["o_sp"], dma=True, tag="f_est")
        for c in range(2, 8):
            def ev2_fn(v, c=c):
                return v.tensor_copy(opair[:, c % 2, 0:512],
                                     acc[:, c - 2, :])
            sch.op("vector", ev2_fn, reads=["facc"], writes=[f"op{c % 2}"],
                   tag="f_ev2")
            sch.op("sync", (lambda p=p, c=c: lambda s: s.dma_start(
                out=o_sp[p][:, 512 * c:512 * c + 512],
                in_=opair[:, c % 2, 0:512]))(),
                reads=[f"op{c % 2}"], writes=["o_sp"], dma=True, tag="f_est2")

    sch.barrier()
    stk.close()

    stk = ExitStack()
    alloc = lambda *a, **k: stk.enter_context(nc.sbuf_tensor(*a, **k))
    palloc = lambda *a, **k: stk.enter_context(nc.psum_tensor(*a, **k))
    wo_sb = alloc("f_wo", [128, NC, D], BF16)
    wf_sb = alloc("f_wf", [128, NC, D], BF16)
    ofm = alloc("f_ofm", [128, 2, NC, 512], BF16)
    xt2 = alloc("f_xt2", [128, 2, D], F32)
    zb = alloc("f_zb", [128, 2, D], BF16)
    z2q = alloc("f_z2q", [128, 2, 2, 512], F32)
    nrm1 = alloc("f_nrm1", [128, 2, 4], F32)
    z1f = alloc("f_z1f", [128, 2, NC, 128], BF16)
    gout = alloc("f_gout", [128, 2, 2, 512], F32)
    psT2 = palloc("f_psT2", [128, 1024], BF16)
    psZ = [palloc(f"f_psZ{i}", [128, 512], F32) for i in range(4)]
    psW = [palloc(f"f_psW{i}", [128, 512], F32) for i in range(2)]

    for i in range(NC):
        sch.op("gpsimd", (lambda i=i: lambda g: g.dma_start(
            out=wo_sb[:, i, :], in_=T["Wo1f"][i * 128:(i + 1) * 128, :]))(),
            reads=[T["Wo1f"]], writes=["fwo"], dma=True, tag="f_wl2")
        sch.op("gpsimd", (lambda i=i: lambda g: g.dma_start(
            out=wf_sb[:, i, :], in_=T["Wff"][i * 128:(i + 1) * 128, :]))(),
            reads=[T["Wff"]], writes=["fwf"], dma=True, tag="f_wl2")

    def emit_fofm_loads(c8):
        b = c8 % 2
        for p in range(NPAIR):
            sch.op("gpsimd", (lambda p=p, c8=c8, b=b: lambda g: g.dma_start(
                out=ofm[:, b, p, :], in_=o_sp[p][:, c8 * 512:(c8 + 1) * 512]))(),
                reads=["o_sp", f"ofm{b}"], writes=[f"ofm{b}"], dma=True,
                tag="f_old")

    emit_fofm_loads(0)
    for c8 in range(8):
        b = c8 % 2
        if c8 + 1 < 8:
            emit_fofm_loads(c8 + 1)
        for j in range(4):
            t = c8 * 4 + j
            jb = j % 2
            zZ = [psZ[2 * jb], psZ[2 * jb + 1]]
            zZk = [f"psZ{2 * jb}", f"psZ{2 * jb + 1}"]
            sch.op("sync", (lambda t=t, jb=jb: lambda s: s.dma_start(
                out=xt2[:, jb, :], in_=x_in[t]))(),
                reads=[xin_name], writes=[f"x2t{jb}"], dma=True, tag="f_xld")
            for oc in range(2):
                def z_mm(te, j=j, oc=oc, b=b, zZ=zZ):
                    for pi in range(NPAIR):
                        ins = te.matmul(
                            zZ[oc][:, :], ofm[:, b, pi, j * 128:(j + 1) * 128],
                            wo_sb[:, pi, oc * 512:(oc + 1) * 512],
                            start=(pi == 0), stop=(pi == NPAIR - 1))
                    return ins
                sch.op("tensor", z_mm, reads=[f"ofm{b}", "fwo"],
                       writes=[zZk[oc]], tag="f_zmm")
            def sq_fn(sc, jb=jb, zZ=zZ):
                sc.activation(z2q[:, jb, 0, :], zZ[0][:, :], AF.Square)
                return sc.activation(z2q[:, jb, 1, :], zZ[1][:, :], AF.Square)
            sch.op("scalar", sq_fn, reads=zZk, writes=[f"z2q{jb}"],
                   tag="f_sq")
            sch.op("vector", (lambda jb=jb: lambda v: v.reduce_sum(
                nrm1[:, jb, 0:1],
                z2q[:, jb, :, :].rearrange("p a f -> p (a f)"), AX.X))(),
                   reads=[f"z2q{jb}"], writes=[f"fn1{jb}"], tag="f_n1")
            sch.op("scalar", (lambda jb=jb: lambda sc: sc.activation(
                nrm1[:, jb, 1:2], nrm1[:, jb, 0:1], AF.Sqrt))(),
                   reads=[f"fn1{jb}"], writes=[f"fn2{jb}"], tag="f_n2")
            sch.op("vector", (lambda jb=jb: lambda v: v.tensor_scalar_max(
                nrm1[:, jb, 1:2], nrm1[:, jb, 1:2], 1e-12))(),
                reads=[f"fn2{jb}"], writes=[f"fn2{jb}"], tag="f_n3")
            sch.op("vector", (lambda jb=jb: lambda v: v.reciprocal(
                nrm1[:, jb, 2:3], nrm1[:, jb, 1:2]))(),
                   reads=[f"fn2{jb}"], writes=[f"fn3{jb}"], tag="f_n4")
            def z1_fn(sc, jb=jb, zZ=zZ):
                sc.activation(zb[:, jb, 0:512], zZ[0][:, :], AF.Identity,
                              scale=nrm1[:, jb, 2:3])
                return sc.activation(zb[:, jb, 512:1024], zZ[1][:, :],
                                     AF.Identity, scale=nrm1[:, jb, 2:3])
            sch.op("scalar", z1_fn, reads=[*zZk, f"fn3{jb}"],
                   writes=[f"zb{jb}"], tag="f_z1")
            def tr_fn(te, jb=jb):
                for q in range(NC):
                    ins = te.transpose(psT2[:, q * 128:(q + 1) * 128],
                                       zb[:, jb, q * 128:(q + 1) * 128],
                                       st["ident"][:, :])
                return ins
            sch.op("tensor", tr_fn, reads=[f"zb{jb}", "ident"],
                   writes=["psT2"], tag="f_tr")
            sch.op("vector", (lambda jb=jb: lambda v: v.tensor_copy(
                z1f[:, jb, :, :],
                psT2[:, :].rearrange("p (c f) -> p c f", c=NC)))(),
                reads=["psT2"], writes=[f"z1f{jb}"], tag="f_trev")
            for oc in range(2):
                def w_mm(te, oc=oc, jb=jb):
                    for i in range(NC):
                        ins = te.matmul(
                            psW[oc][:, :], z1f[:, jb, i, :],
                            wf_sb[:, i, oc * 512:(oc + 1) * 512],
                            start=(i == 0), stop=(i == NC - 1))
                    return ins
                sch.op("tensor", w_mm, reads=[f"z1f{jb}", "fwf"],
                       writes=[f"psW{oc}"], tag="f_wmm")
            def sq2_fn(sc, jb=jb):
                sc.activation(z2q[:, jb, 0, :], psW[0][:, :], AF.Square)
                return sc.activation(z2q[:, jb, 1, :], psW[1][:, :], AF.Square)
            sch.op("scalar", sq2_fn, reads=["psW0", "psW1"],
                   writes=[f"z2q{jb}"], tag="f_sq2")
            sch.op("vector", (lambda jb=jb: lambda v: v.reduce_sum(
                nrm1[:, jb, 0:1],
                z2q[:, jb, :, :].rearrange("p a f -> p (a f)"), AX.X))(),
                   reads=[f"z2q{jb}"], writes=[f"fn1{jb}"], tag="f_n5")
            sch.op("scalar", (lambda jb=jb: lambda sc: sc.activation(
                nrm1[:, jb, 1:2], nrm1[:, jb, 0:1], AF.Sqrt))(),
                   reads=[f"fn1{jb}"], writes=[f"fn2{jb}"], tag="f_n6")
            sch.op("vector", (lambda jb=jb: lambda v: v.tensor_scalar_max(
                nrm1[:, jb, 1:2], nrm1[:, jb, 1:2], 1e-12))(),
                reads=[f"fn2{jb}"], writes=[f"fn2{jb}"], tag="f_n7")
            sch.op("vector", (lambda jb=jb: lambda v: v.reciprocal(
                nrm1[:, jb, 3:4], nrm1[:, jb, 1:2]))(),
                   reads=[f"fn2{jb}"], writes=[f"fn4{jb}"], tag="f_n8")
            def gel_fn(sc, jb=jb):
                sc.activation(gout[:, jb, 0, :], psW[0][:, :], AF.Gelu,
                              scale=nrm1[:, jb, 3:4])
                return sc.activation(gout[:, jb, 1, :], psW[1][:, :], AF.Gelu,
                                     scale=nrm1[:, jb, 3:4])
            sch.op("scalar", gel_fn, reads=["psW0", "psW1", f"fn4{jb}"],
                   writes=[f"fgout{jb}"], tag="f_gel")
            for oc in range(2):
                sch.op("vector", (lambda oc=oc, jb=jb: lambda v:
                       v.tensor_add(z2q[:, jb, oc, 0:512], gout[:, jb, oc, :],
                                    xt2[:, jb, oc * 512:(oc + 1) * 512]))(),
                    reads=[f"fgout{jb}", f"x2t{jb}"], writes=[f"z2q{jb}"],
                    tag="f_res")
            sch.op("sync", (lambda t=t, jb=jb: lambda s: s.dma_start(
                out=xout[t],
                in_=z2q[:, jb, :, 0:512].rearrange("p a f -> p (a f)")))(),
                reads=[f"z2q{jb}"], writes=[xout_name], dma=True, tag="f_ost")

    sch.barrier()
    stk.close()



def _prep_inputs(inputs):
    f32c = lambda a: np.ascontiguousarray(a, dtype=np.float32)
    shared = {
        "wqkvT": f32c(inputs["L0_attn_w_qkv"].T),
        "wprojT": f32c(inputs["L0_attn_w_proj"].T),
        "U1": f32c(inputs["L0_ffn_U1"]),
        "V1": f32c(inputs["L0_ffn_V1"]),
        "u2r": f32c(inputs["L0_ffn_U2"]).reshape(32, 128, 256),
        "V2": f32c(inputs["L0_ffn_V2"]),
        "b1": f32c(inputs["L0_ffn_b1"]).reshape(1, 4 * D),
        "b2": f32c(inputs["L0_ffn_b2"]).reshape(1, D),
        "ppT": f32c(inputs["L1_attn_pos_phases"][:S].T),
        "Wpv": f32c(inputs["L1_attn_Wpv"]),
        "Wk1a": f32c(inputs["L1_attn_Wk"]),
        "Wkv": f32c(inputs["L1_attn_Wkv"]),
        "Wg1": f32c(inputs["L1_attn_Wg1"]),
        "Wb1": f32c(inputs["L1_attn_Wb1"]),
        "wb2r": f32c(inputs["L1_attn_Wb2"]).reshape(4, 128, 2),
        "wg2c": f32c(inputs["L1_attn_Wg2"].reshape(NC, 128).T),
        "lngc": f32c(inputs["L1_attn_ln_g"].reshape(NC, 128).T),
        "lnbc": f32c(inputs["L1_attn_ln_b"].reshape(NC, 128).T),
        "Wo1a": f32c(inputs["L1_attn_Wo"]),
        "bpv": f32c(inputs["L1_attn_bpv"]).reshape(1, D),
        "bk1a": f32c(inputs["L1_attn_bk"]).reshape(1, D),
        "bkv": f32c(inputs["L1_attn_bkv"]).reshape(1, D),
        "bg1": f32c(inputs["L1_attn_bg1"]).reshape(1, D),
        "bg2": f32c(inputs["L1_attn_bg2"]).reshape(1, 1),
        "bb1": f32c(inputs["L1_attn_bb1"]).reshape(1, 512),
        "bb2": f32c(inputs["L1_attn_bb2"]).reshape(1, 2),
        "bo1a": f32c(inputs["L1_attn_bo"]).reshape(1, D),
        "Wq1f": f32c(inputs["L1_ffn_Wq"]),
        "Wk1f": f32c(inputs["L1_ffn_Wk"]),
        "Wv1f": f32c(inputs["L1_ffn_Wv"]),
        "Wo1f": f32c(inputs["L1_ffn_Wo"]),
        "Wff": f32c(inputs["L1_ffn_Wff"]),
    }
    x = inputs["L0_attn_x"]
    in_maps = []
    for c in range(8):
        m = dict(shared)
        m["x_in"] = f32c(x[c]).reshape(NT, 128, D)
        in_maps.append(m)
    return in_maps


def kernel(**inputs):
    nc = build_program()
    in_maps = _prep_inputs(inputs)
    res = run_bass_kernel_spmd(nc, in_maps, core_ids=list(range(8)))
    out = np.stack([r["xout"].reshape(S, D) for r in res.results], axis=0)
    return out
